# version 03
# baseline (speedup 1.0000x reference)
import numpy as np
import concourse.bass as bass
import concourse.mybir as mybir
from concourse.bass_utils import run_bass_kernel_spmd

F32 = mybir.dt.float32
BF16 = mybir.dt.bfloat16
AF = mybir.ActivationFunctionType
ALU = mybir.AluOpType
AX = mybir.AxisListType

B = 2
S = 4096
D = 4096
TPC = 1024
NC = 8
KC = 32
MASK_NEG = -1.0e9
ATTA_SCALE = 1.0 / float(np.sqrt(512.0))
ATTB_SCALE = 1.0 / float(np.sqrt(128.0))
CQ_FACTOR = 0.55
LN_EPS = 1e-5

N_SUB = 4



def _tile_oc_kc(w):
    Dk, O = w.shape
    return np.ascontiguousarray(
        w.reshape(Dk // 128, 128, O // 128, 128).transpose(2, 0, 1, 3))


def _tile_kc_ob(w, ob):
    Dk, O = w.shape
    return np.ascontiguousarray(
        w.reshape(Dk // 128, 128, O // ob, ob).transpose(2, 0, 1, 3))


def _prep_inputs(inp):
    x = np.asarray(inp["L0_attn_a_x"], np.float32)

    wqk = np.asarray(inp["L0_attn_a_Wqk_w"], np.float32)
    bqk = np.asarray(inp["L0_attn_a_Wqk_b"], np.float32)
    perm = np.empty(1024, np.int64)
    for h in range(8):
        for e in range(64):
            perm[h * 64 + e] = h * 128 + 2 * e
            perm[512 + h * 64 + e] = h * 128 + 2 * e + 1
    wqk_tiled = _tile_oc_kc(np.ascontiguousarray(wqk.T[:, perm]))
    bqk_dev = np.ascontiguousarray(bqk[perm].reshape(8, 128).T)

    wv = np.asarray(inp["L0_attn_a_Wv_w"], np.float32)
    bv = np.asarray(inp["L0_attn_a_Wv_b"], np.float32)
    wv_tiled = _tile_kc_ob(np.ascontiguousarray(wv.T), 512)
    bv_dev = bv.reshape(1, 512)

    wo = np.asarray(inp["L0_attn_a_Wo_w"], np.float32)
    bo = np.asarray(inp["L0_attn_a_Wo_b"], np.float32)
    wo_tiled = _tile_oc_kc(np.ascontiguousarray(wo.T))
    bo_dev = np.ascontiguousarray(bo.reshape(32, 128).T)

    qkv = np.asarray(inp["L0_attn_b_qkv_w"], np.float32)
    qkv_b = np.asarray(inp["L0_attn_b_qkv_b"], np.float32)
    qw = qkv[:, 0:4096].reshape(4096, 32, 128)
    kw = qkv[:, 4096:8192].reshape(4096, 32, 128)
    vw = qkv[:, 8192:12288]
    qb = qkv_b[0:4096].reshape(32, 128)
    kb = qkv_b[4096:8192].reshape(32, 128)
    vb = qkv_b[8192:12288]
    qk_cols = np.empty((4096, 64, 128), np.float32)
    qk_bias = np.empty((64, 128), np.float32)
    for h in range(32):
        qk_cols[:, 2 * h, 0:64] = qw[:, h, 0:64]
        qk_cols[:, 2 * h, 64:128] = kw[:, h, 0:64]
        qk_cols[:, 2 * h + 1, 0:64] = qw[:, h, 64:128]
        qk_cols[:, 2 * h + 1, 64:128] = kw[:, h, 64:128]
        qk_bias[2 * h, 0:64] = qb[h, 0:64]
        qk_bias[2 * h, 64:128] = kb[h, 0:64]
        qk_bias[2 * h + 1, 0:64] = qb[h, 64:128]
        qk_bias[2 * h + 1, 64:128] = kb[h, 64:128]
    wqkv_qk = _tile_oc_kc(qk_cols.reshape(4096, 64 * 128))
    bqkv_qk = np.ascontiguousarray(qk_bias.T)
    wqkv_v = _tile_kc_ob(np.ascontiguousarray(vw), 512)
    bqkv_v = np.ascontiguousarray(vb.reshape(8, 512))

    ow = np.asarray(inp["L0_attn_b_out_w"], np.float32)
    ob_ = np.asarray(inp["L0_attn_b_out_b"], np.float32)
    ow_tiled = _tile_oc_kc(ow)
    ob_dev = np.ascontiguousarray(ob_.reshape(32, 128).T)

    gw = np.asarray(inp["L0_ffn_gate_w"], np.float32)
    gw_tiled = _tile_kc_ob(np.ascontiguousarray(gw.T), 8)
    la = np.asarray(inp["L0_ffn_lora_A"], np.float32)
    lb = np.asarray(inp["L0_ffn_lora_B"], np.float32)
    a_cat = np.ascontiguousarray(np.concatenate([la[e].T for e in range(8)], axis=1))
    b_cat = np.ascontiguousarray(np.concatenate([lb[e].T for e in range(8)], axis=0))
    a_tiled = _tile_oc_kc(a_cat)
    b_tiled = _tile_oc_kc(b_cat)

    ident = np.eye(128, dtype=np.float32)
    indic = np.zeros((4, 8, 128), np.float32)
    for rc in range(4):
        for m in range(128):
            indic[rc, (rc * 128 + m) // 64, m] = 1.0

    inv_freq = 10000.0 ** (-(np.arange(64, dtype=np.float64) / 64.0))

    shared = {
        "wqk": wqk_tiled, "bqk": bqk_dev, "wv": wv_tiled, "bv": bv_dev,
        "wo": wo_tiled, "bo": bo_dev,
        "wqkv_qk": wqkv_qk, "bqkv_qk": bqkv_qk, "wqkv_v": wqkv_v, "bqkv_v": bqkv_v,
        "ow": ow_tiled, "obias": ob_dev,
        "gw": gw_tiled, "wa": a_tiled, "wbw": b_tiled,
        "ident": ident, "indic": indic,
    }

    in_maps = []
    for core in range(NC):
        b = core // 4
        c = core % 4
        blocks = [c + 4 * j for j in range(8)]
        pos = np.concatenate([np.arange(g * 128, (g + 1) * 128) for g in blocks])
        x_fm = np.ascontiguousarray(x[b, pos, :].T)

        ang = pos[:, None].astype(np.float64) * inv_freq[None, :]
        cos_t = np.cos(ang).astype(np.float32).T
        sin_t = np.sin(ang).astype(np.float32).T
        ropeC = np.ascontiguousarray(np.concatenate([cos_t, cos_t], axis=0))
        ropeS = np.ascontiguousarray(np.concatenate([sin_t, sin_t], axis=0))

        mask = np.zeros((4, 2, 128, 1024), np.float32)
        for p in range(4):
            for j in range(2):
                g = 2 * p + j
                for ki in range(4):
                    kappa = 4 * g + ki
                    gk = kappa * 128 + np.arange(128)
                    for sub in range(2):
                        slot = 2 * p + sub
                        gq = (c + 4 * slot) * 128 + np.arange(128)
                        m = np.where(gk[:, None] <= gq[None, :], 0.0, MASK_NEG)
                        mask[p, j, :, ki * 256 + sub * 128: ki * 256 + sub * 128 + 128] = m

        maskT = np.zeros((4, 2, 128, 1024), np.float32)
        for p in range(4):
            for sub in range(2):
                slot = 2 * p + sub
                gq = (c + 4 * slot) * 128 + np.arange(128)
                gk = 8 * p * 128 + np.arange(1024)
                maskT[p, sub] = np.where(gk[None, :] <= gq[:, None], 0.0, MASK_NEG)
        d = {"x_fm": x_fm, "ropeC": ropeC, "ropeS": ropeS,
             "maskb": mask.reshape(8, 128, 1024),
             "maskbT": maskT.reshape(8, 128, 1024)}
        d.update(shared)
        in_maps.append(d)
    return in_maps


def _reassemble(results):
    out = np.empty((B, S, D), np.float32)
    for core in range(NC):
        b = core // 4
        c = core % 4
        yt = results[core]["y"].T.reshape(8, 128, 4096)
        for j in range(8):
            g = c + 4 * j
            out[b, g * 128:(g + 1) * 128, :] = yt[j]
    return out



class Sem:
    def __init__(self, nc, name):
        self.h = nc.alloc_semaphore(name)
        self.n = 0

    def inc(self, d):
        self.n += d
        return self.n


class Bld:
    ENGS = ("te", "ve", "act", "sy", "gp")

    def __init__(self, nc):
        self.nc = nc
        self.ops = {k: [] for k in self.ENGS}
        self.sem = {k: Sem(nc, "s_" + k) for k in
                    ("te", "ve", "act", "sy", "gp", "st", "gpc")}

    def emit(self, eng, fn):
        self.ops[eng].append(fn)

    def wait(self, eng, sem_name, val):
        if val <= 0:
            return
        s = self.skey(sem_name)
        self.emit(eng, lambda e, s=s, v=val: e.wait_ge(s.h, v))

    def skey(self, key):
        if key not in self.sem:
            self.sem[key] = Sem(self.nc, "s_" + key)
        return self.sem[key]

    def dma(self, eng, out, in_, key):
        if eng == "act":
            self._selfedge("act")
        s = self.skey(key)
        self.emit(eng, lambda e, o=out, i=in_, s=s:
                  e.dma_start(out=o, in_=i).then_inc(s.h, 16))
        return s.inc(16)

    def _selfedge(self, eng):
        s = self.sem[eng]
        if s.n > 0:
            v = s.n
            self.emit(eng, lambda e, s=s, v=v: e.wait_ge(s.h, v))

    def comp(self, eng, fn, sem_name=None):
        if eng in ("ve", "act"):
            self._selfedge(eng)
        s = self.sem[sem_name or ("gpc" if eng == "gp" else eng)]
        self.emit(eng, lambda e, f=fn, s=s: f(e).then_inc(s.h, 1))
        return s.inc(1)

    def compq(self, eng, fn):
        if eng in ("ve", "act"):
            self.comp(eng, fn)
            return
        self.emit(eng, lambda e, f=fn: f(e))

    def join(self):
        targets = {k: v.n for k, v in self.sem.items()}
        for eng in self.ENGS:
            for sn, v in targets.items():
                if sn == eng:
                    continue
                self.wait(eng, sn, v)


def build_program():
    nc = bass.Bass("TRN2", target_bir_lowering=False, num_devices=NC)
    bld = Bld(nc)

    DT = nc.dram_tensor
    x_fm = DT("x_fm", [D, TPC], F32, kind="ExternalInput").ap()
    y_out = DT("y", [D, TPC], F32, kind="ExternalOutput").ap()
    wqk_h = DT("wqk", [8, KC, 128, 128], F32, kind="ExternalInput").ap()
    bqk_h = DT("bqk", [128, 8], F32, kind="ExternalInput").ap()
    wv_h = DT("wv", [1, KC, 128, 512], F32, kind="ExternalInput").ap()
    bv_h = DT("bv", [1, 512], F32, kind="ExternalInput").ap()
    wo_h = DT("wo", [KC, 4, 128, 128], F32, kind="ExternalInput").ap()
    bo_h = DT("bo", [128, KC], F32, kind="ExternalInput").ap()
    wqkqk_h = DT("wqkv_qk", [64, KC, 128, 128], F32, kind="ExternalInput").ap()
    bqkqk_h = DT("bqkv_qk", [128, 64], F32, kind="ExternalInput").ap()
    wqkvv_h = DT("wqkv_v", [8, KC, 128, 512], F32, kind="ExternalInput").ap()
    bqkvv_h = DT("bqkv_v", [8, 512], F32, kind="ExternalInput").ap()
    ow_h = DT("ow", [KC, KC, 128, 128], F32, kind="ExternalInput").ap()
    obias_h = DT("obias", [128, KC], F32, kind="ExternalInput").ap()
    gw_h = DT("gw", [1, KC, 128, 8], F32, kind="ExternalInput").ap()
    wa_h = DT("wa", [4, KC, 128, 128], F32, kind="ExternalInput").ap()
    wbw_h = DT("wbw", [KC, 4, 128, 128], F32, kind="ExternalInput").ap()
    ident_h = DT("ident", [128, 128], F32, kind="ExternalInput").ap()
    indic_h = DT("indic", [4, 8, 128], F32, kind="ExternalInput").ap()
    ropeC_h = DT("ropeC", [128, TPC], F32, kind="ExternalInput").ap()
    ropeS_h = DT("ropeS", [128, TPC], F32, kind="ExternalInput").ap()
    maskb_h = DT("maskb", [8, 128, 1024], F32, kind="ExternalInput").ap()
    maskbT_h = DT("maskbT", [8, 128, 1024], F32, kind="ExternalInput").ap()

    Xd = DT("Xd", [D, TPC], F32, kind="Internal").ap()
    qbuf = DT("qbuf", [D, TPC], BF16, kind="Internal").ap()
    kvb_in = DT("kvb_in", [2 * D * TPC], BF16, kind="Internal").ap()
    kvb_all = DT("kvb_all", [4, 2 * D * TPC], BF16, kind="Internal").ap()
    kva_in = DT("kva_in", [2 * 512 * TPC], BF16, kind="Internal").ap()
    kva_all = DT("kva_all", [4, 2 * 512 * TPC], BF16, kind="Internal").ap()

    kvbK = kvb_in[0:D * TPC].rearrange("(d t) -> d t", d=D)
    kvbV = kvb_in[D * TPC:2 * D * TPC].rearrange("(t d) -> t d", t=TPC)
    kvaK = kva_in[0:512 * TPC].rearrange("(d t) -> d t", d=512)
    kvaV = kva_in[512 * TPC:2 * 512 * TPC].rearrange("(t d) -> t d", t=TPC)

    A = lambda name, shape, dt: nc.alloc_sbuf_tensor(name, shape, dt).ap()
    xt = A("xt", [128, KC * TPC], BF16)
    off1, _ = nc.bump_sbuf(16384)
    qa = nc.alloc_sbuf_tensor_at("qa", [65, 8 * TPC], BF16, offset=off1).ap()
    maskS = nc.alloc_sbuf_tensor_at("maskS", [128, 8 * 1024], BF16, offset=off1).ap()
    off2, _ = nc.bump_sbuf(8192)
    ctxa = nc.alloc_sbuf_tensor_at("ctxa", [128, 4 * TPC], BF16, offset=off2).ap()
    ropeC = nc.alloc_sbuf_tensor_at("ropeCs", [128, TPC], F32, offset=off2).ap()
    ropeS = nc.alloc_sbuf_tensor_at("ropeSs", [128, TPC], F32, offset=off2 + 4096).ap()
    wsl = [A(f"wsl{i}", [128, 16 * 512], BF16) for i in range(2)]

    def qk_slab(i):
        return wsl[i // 2][:, (i % 2) * (KC * 128):(i % 2 + 1) * (KC * 128)]

    wb_oc = [A(f"wboc{i}", [128, KC * 128], BF16) for i in range(2)]
    hhat = A("hhat", [128, 4 * TPC], BF16)
    katile = A("katile", [65, KC * 128], BF16)
    vatile = A("vatile", [128, KC * 65], BF16)
    pbuf = [A(f"pbuf{i}", [128, 1024], BF16) for i in range(2)]
    kbufs = [A(f"kbuf{i}", [128, 512], BF16) for i in range(2)]
    vbufs = [A(f"vbuf{i}", [128, 512], BF16) for i in range(2)]
    qtile = A("qtile", [128, 256], BF16)
    stg = [A(f"stg{i}", [128, 512], BF16) for i in range(4)]
    tmpf = [A(f"tmpf{i}", [128, 512], F32) for i in range(3)]
    xb2 = [A(f"xb{i}", [128, 512], BF16) for i in range(2)]
    sq2 = [A(f"sq{i}", [128, 512], BF16) for i in range(2)]
    rowbuf = A("rowbuf", [1, 2048], F32)
    rowbf = A("rowbf", [1, 1024], BF16)
    g8 = A("g8", [128, 48], F32)
    wtm = A("wtm", [128, 8], BF16)
    wfm = A("wfm", [8, TPC], BF16)
    ident = A("idents", [128, 128], BF16)
    indic = A("indics", [8, 4 * 128], BF16)
    ones_col = A("ones_col", [128, 1], BF16)
    ones_row = A("ones_row", [1, 128], BF16)
    bqk_s = A("bqk_s", [128, 8], F32)
    bo_s = A("bo_s", [128, KC], F32)
    bqkqk_s = A("bqkqk_s", [128, 64], F32)
    obias_s = A("obias_s", [128, KC], F32)
    bv_s = A("bv_s", [1, 512], BF16)
    bvv_s = A("bvv_s", [1, 512], BF16)
    gw_s = A("gw_s", [128, KC * 8], BF16)

    P = lambda name, shape, dt=F32: nc.alloc_psum_tensor(name, shape, dt).ap()
    ps2a = P("ps2a", [128, 1024])
    ps2b = P("ps2b", [128, 1024])
    pb4 = P("pb4", [128, 512])
    pb5 = P("pb5", [128, 512])
    pb6 = P("pb6", [128, 512])
    pbt = P("pbt", [128, 1024], BF16)

    bld.dma("sy", bqk_s[:, :], bqk_h[:, :], "cst")
    bld.dma("sy", bo_s[:, :], bo_h[:, :], "cst")
    bld.dma("sy", bqkqk_s[:, :], bqkqk_h[:, :], "cst")
    bld.dma("sy", obias_s[:, :], obias_h[:, :], "cst")
    bld.dma("gp", ident[:, :], ident_h[:, :], "cstg")
    bld.dma("gp", indic.rearrange("e (r c) -> e r c", r=4),
            indic_h.rearrange("r e c -> e r c"), "cstg")
    bld.dma("gp", bv_s[:, :], bv_h[:, :], "cstg")
    bld.dma("gp", gw_s.rearrange("d (k e) -> d k e", k=KC),
            gw_h[0].rearrange("k d e -> d k e"), "cstg")
    bld.comp("gp", lambda e: e.memset(ones_col[:, :], 1.0))
    bld.comp("gp", lambda e: e.memset(ones_row[:, :], 1.0))
    bld.join()

    def slab_dma(eng, buf, n_kc, width, w_oc, key):
        return bld.dma(eng,
                       buf[:, 0:n_kc * width].rearrange("d (k o) -> d k o", k=n_kc),
                       w_oc.rearrange("k d o -> d k o"), key)

    def phase_ln(src):
        tmp_use = [0, 0]
        sqg = [0, 0]
        mmg = [0, 0]
        for tt in range(2):
            t0 = tt * 512
            sum_done = 0
            for kc in range(KC):
                sl = kc % 2
                bld.wait("sy", "ve", tmp_use[sl])
                lt = bld.dma("sy", tmpf[sl][:, :],
                             src[kc * 128:(kc + 1) * 128, t0:t0 + 512],
                             f"ltmp{sl}")
                bld.wait("ve", f"ltmp{sl}", lt)
                bld.wait("ve", "act", sqg[sl])
                bld.wait("ve", "te", mmg[sl])
                ct = bld.comp("ve", lambda e, sl=sl: e.tensor_copy(
                    xb2[sl][:, :], tmpf[sl][:, :]))
                tmp_use[sl] = ct
                bld.wait("act", "ve", ct)
                bld.wait("act", "te", mmg[sl])
                sq_t = bld.comp("act", lambda e, sl=sl: e.activation(
                    out=sq2[sl][:, :], in_=xb2[sl][:, :], func=AF.Square))
                sqg[sl] = sq_t
                bld.wait("te", "ve", ct)
                bld.compq("te", lambda e, sl=sl, kc=kc: e.matmul(
                    pb4[0:1, :], ones_col[:, :], xb2[sl][:, :],
                    start=(kc == 0), stop=(kc == KC - 1)))
                bld.wait("te", "act", sq_t)
                sum_done = bld.comp("te", lambda e, sl=sl, kc=kc: e.matmul(
                    pb5[0:1, :], ones_col[:, :], sq2[sl][:, :],
                    start=(kc == 0), stop=(kc == KC - 1)))
                mmg[sl] = sum_done
            m_row = rowbuf[0:1, 0:512]
            e2_row = rowbuf[0:1, 512:1024]
            var_row = rowbuf[0:1, 1024:1536]
            r_row = rowbuf[0:1, 1536:2048]
            bld.wait("ve", "te", sum_done)
            bld.compq("ve", lambda e: e.tensor_scalar(
                m_row, pb4[0:1, :], 1.0 / D, None, ALU.mult))
            bld.compq("ve", lambda e: e.tensor_scalar(
                e2_row, pb5[0:1, :], 1.0 / D, None, ALU.mult))
            bld.compq("ve", lambda e: e.tensor_tensor(var_row, m_row, m_row, ALU.mult))
            bld.compq("ve", lambda e: e.tensor_tensor(
                var_row, e2_row, var_row, ALU.subtract))
            bld.compq("ve", lambda e: e.tensor_scalar(
                var_row, var_row, LN_EPS, None, ALU.add))
            vt = bld.comp("ve", lambda e: e.reciprocal(r_row, var_row))
            bld.wait("act", "ve", vt)
            at = bld.comp("act", lambda e: e.activation(
                out=r_row, in_=r_row, func=AF.Sqrt))
            bld.wait("ve", "act", at)
            m_bf = rowbf[0:1, 0:512]
            r_bf = rowbf[0:1, 512:1024]
            bld.compq("ve", lambda e: e.tensor_copy(m_bf, m_row))
            rowt = bld.comp("ve", lambda e: e.tensor_copy(r_bf, r_row))
            bld.wait("te", "ve", rowt)
            bld.compq("te", lambda e: e.matmul(
                pb6[:, :], ones_row[:, :], m_bf, start=True, stop=True))
            bc_t = bld.comp("te", lambda e: e.matmul(
                ps2a[:, 0:512], ones_row[:, :], r_bf, start=True, stop=True))
            bld.wait("ve", "te", bc_t)
            for kc in range(KC):
                sl = kc % 2
                bld.wait("sy", "ve", tmp_use[sl])
                lt = bld.dma("sy", tmpf[sl][:, :],
                             src[kc * 128:(kc + 1) * 128, t0:t0 + 512],
                             f"ltmp{sl}")
                bld.wait("ve", f"ltmp{sl}", lt)
                bld.compq("ve", lambda e, sl=sl: e.tensor_tensor(
                    tmpf[sl][:, :], tmpf[sl][:, :], pb6[:, :], ALU.subtract))
                tmp_use[sl] = bld.comp("ve", lambda e, sl=sl, kc=kc, t0=t0:
                                       e.tensor_tensor(
                    xt[:, kc * TPC + t0: kc * TPC + t0 + 512],
                    tmpf[sl][:, :], ps2a[:, 0:512], ALU.mult))
        bld.join()

    def emit_proj_fm(n_oc, n_kc, w_dram, rhs_fn, epi_fn):
        bank_free = [("ve", 0), ("ve", 0)]
        buf_guard = [0, 0]
        for oc in range(n_oc):
            slot = oc % 2
            bld.wait("gp", "te", buf_guard[slot])
            wl = slab_dma("gp", wb_oc[slot], n_kc, 128, w_dram[oc], f"wb{slot}")
            bld.wait("te", f"wb{slot}", wl)
            for tt in range(2):
                bank = (pb4, pb5)[tt]
                fs, fv = bank_free[tt]
                bld.wait("te", fs, fv)
                mm_t = 0
                for kc in range(n_kc):
                    fn = lambda e, oc=oc, tt=tt, kc=kc, bank=bank, slot=slot: e.matmul(
                        bank[:, :], wb_oc[slot][:, kc * 128:(kc + 1) * 128],
                        rhs_fn(kc, tt), start=(kc == 0), stop=(kc == n_kc - 1))
                    if kc == n_kc - 1:
                        mm_t = bld.comp("te", fn)
                    else:
                        bld.compq("te", fn)
                bank_free[tt] = epi_fn(oc, tt, bank, mm_t)
            buf_guard[slot] = bld.sem["te"].n
        bld.join()

    def xt_rhs(kc, tt):
        return xt[:, kc * TPC + tt * 512: kc * TPC + tt * 512 + 512]

    def residual_epi(src, dst, bias_s):
        slot_guard = [0, 0]

        def epi(oc, tt, bank, mm_t, src=src, dst=dst, bias_s=bias_s):
            i = (oc * 2 + tt) % 2
            bld.wait("sy", f"rst{i}", slot_guard[i])
            lx = bld.dma("sy", tmpf[i][:, :],
                         src[oc * 128:(oc + 1) * 128, tt * 512: tt * 512 + 512],
                         f"ltmp{i}")
            bld.wait("ve", f"ltmp{i}", lx)
            bld.wait("ve", "te", mm_t)
            if bias_s is not None:
                vt = bld.comp("ve", lambda e, oc=oc, bank=bank, i=i:
                              e.scalar_tensor_tensor(
                    tmpf[i][:, :], bank[:, :], bias_s[:, oc:oc + 1],
                    tmpf[i][:, :], ALU.add, ALU.add))
            else:
                vt = bld.comp("ve", lambda e, bank=bank, i=i: e.tensor_tensor(
                    tmpf[i][:, :], bank[:, :], tmpf[i][:, :], ALU.add))
            bld.wait("act", "ve", vt)
            stt = bld.dma("act", dst[oc * 128:(oc + 1) * 128, tt * 512: tt * 512 + 512],
                          tmpf[i][:, :], f"rst{i}")
            slot_guard[i] = stt
            return ("ve", vt)
        return epi

    def sublayer_attn_a(src, dst):
        phase_ln(src)

        sg_guard = {}

        def epi(oc, tt, bank, mm_t):
            sl = (oc * 2 + tt) % 4
            bld.wait("act", "te", mm_t)
            bld.wait("act", f"qst{sl}", sg_guard.get(sl, 0))
            at = bld.comp("act", lambda e, oc=oc, bank=bank, sl=sl: e.add(
                stg[sl][:, :], bank[:, :], bqk_s[:, oc:oc + 1]))
            bld.wait("sy", "act", at)
            if oc < 4:
                for half in range(2):
                    h = oc * 2 + half
                    bld.dma("sy", qa[0:64, h * TPC + tt * 512: h * TPC + tt * 512 + 512],
                            stg[sl][half * 64:(half + 1) * 64, :], f"qst{sl}")
            else:
                ko = oc - 4
                for half in range(2):
                    h = ko * 2 + half
                    bld.dma("sy", kvaK[h * 64:(h + 1) * 64, tt * 512: tt * 512 + 512],
                            stg[sl][half * 64:(half + 1) * 64, :], f"qst{sl}")
            done = bld.skey(f"qst{sl}").n
            sg_guard[sl] = done
            bld.wait("ve", "act", at)
            vt = bld.comp("ve", lambda e: e.nop())
            return ("ve", vt)
        emit_proj_fm(8, KC, wqk_h, xt_rhs, epi)

        for half in range(2):
            slab_dma("gp", wsl[half], 16, 512,
                     wv_h[0, half * 16:(half + 1) * 16], f"wsl{half}")
        bld.wait("te", "wsl0", bld.skey("wsl0").n)
        bld.wait("te", "wsl1", bld.skey("wsl1").n)
        vfree = [("ve", 0), ("ve", 0)]
        st_g = [0, 0]
        for tc in range(8):
            bank = (pb4, pb5)[tc % 2]
            fs, fv = vfree[tc % 2]
            bld.wait("te", fs, fv)
            for kc in range(KC):
                bld.compq("te", lambda e, tc=tc, kc=kc, bank=bank: e.matmul(
                    bank[:, :],
                    xt[:, kc * TPC + tc * 128: kc * TPC + tc * 128 + 128],
                    wsl[kc // 16][:, (kc % 16) * 512:(kc % 16) * 512 + 512],
                    start=(kc == 0), stop=False))
            mm_t = bld.comp("te", lambda e, bank=bank: e.matmul(
                bank[:, :], ones_row[:, :], bv_s[:, :], start=False, stop=True))
            bld.wait("ve", "te", mm_t)
            bld.wait("ve", f"vst{tc % 2}", st_g[tc % 2])
            vt = bld.comp("ve", lambda e, tc=tc, bank=bank: e.tensor_copy(
                stg[tc % 2][:, :], bank[:, :]))
            vfree[tc % 2] = ("ve", vt)
            bld.wait("act", "ve", vt)
            st_g[tc % 2] = bld.dma(
                "act", kvaV[tc * 128:(tc + 1) * 128, :], stg[tc % 2][:, :],
                f"vst{tc % 2}")
        bld.join()

        bld.comp("gp", lambda e: e.memset(katile[64:65, :], 1.0))
        bld.comp("gp", lambda e: e.memset(
            vatile.rearrange("p (k c) -> p k c", k=KC)[:, :, 64:65], 1.0))

        bld.comp("gp", lambda e: e.collective_compute(
            "AllGather", ALU.bypass, ins=[kva_in[:]], outs=[kva_all[:, :]],
            replica_groups=[[0, 1, 2, 3], [4, 5, 6, 7]]))
        bld.join()

        kvaKg = kva_all[:, 0:512 * TPC].rearrange("r (d t) -> r d t", d=512)
        kvaVg = kva_all[:, 512 * TPC:2 * 512 * TPC].rearrange("r (t d) -> r t d", t=TPC)

        kv_guard = 0
        ol_free = 0
        pbt_g = [0]
        for h in range(8):
            bld.wait("sy", "te", kv_guard)
            for r in range(4):
                bld.dma("sy",
                        katile[0:64, :].rearrange("p (kk t) -> p kk t", kk=KC)[:, r::4, :],
                        kvaKg[r, h * 64:(h + 1) * 64, :].rearrange(
                            "d (s t) -> d s t", s=8), "kva")
                bld.dma("sy",
                        vatile.rearrange("p (kk c) -> p kk c", kk=KC)[:, r::4, 0:64],
                        kvaVg[r, :, h * 64:(h + 1) * 64].rearrange(
                            "(s p) d -> p s d", s=8), "kva")
            kvload = bld.skey("kva").n
            bld.wait("te", "kva", kvload)
            bld.wait("te", "ve", ol_free)
            bld.wait("te", "act", pbt_g[0])
            p1v = [0, 0]
            trp = 0
            for qb in range(8):
                for kt in range(8):
                    bank = (pb4, pb5)[kt % 2]
                    bld.wait("te", "ve", p1v[kt % 2])
                    s1 = bld.comp("te", lambda e, h=h, qb=qb, kt=kt, bank=bank: e.matmul(
                        bank[:, :], qa[0:64, h * TPC + qb * 128: h * TPC + qb * 128 + 128],
                        katile[0:64, kt * 512:(kt + 1) * 512], start=True, stop=True))
                    bld.wait("ve", "te", s1)
                    p1v[kt % 2] = bld.comp("ve", lambda e, kt=kt, bank=bank: e.reduce_max(
                        g8[:, kt:kt + 1], bank[:, :], axis=AX.X))
                bld.compq("ve", lambda e: e.reduce_max(
                    g8[:, 40:41], g8[:, 0:8], axis=AX.X))
                bld.wait("ve", "te", trp)
                nv = bld.comp("ve", lambda e: e.tensor_scalar(
                    wtm[:, 0:1], g8[:, 40:41], -1.0, None, ALU.mult))
                bld.wait("te", "ve", nv)
                trp = bld.comp("te", lambda e, qb=qb: e.transpose(
                    pbt[0:1, qb * 128:(qb + 1) * 128], wtm[:, 0:1], ident[:, :]))
            bld.wait("act", "te", trp)
            at = bld.comp("act", lambda e: e.copy(rowbf[0:1, 0:1024], pbt[0:1, 0:1024]))
            pbt_g[0] = at
            bld.wait("sy", "act", at)
            cq_l = bld.dma("sy", qa[64:65, h * TPC:(h + 1) * TPC],
                           rowbf[0:1, 0:1024], "cq")

            for qt in range(2):
                bld.wait("te", "cq", cq_l)
                bld.wait("te", "kva", kvload)
                bld.wait("te", "ve", ol_free)
                first = True
                p_t = [0, 0]
                av_t = 0
                for kk in range(KC):
                    sb = (ps2a, ps2b)[kk % 2][:, 0:512]
                    bld.wait("te", "act", p_t[kk % 2])
                    s_t = bld.comp("te", lambda e, kk=kk, h=h, qt=qt, sb=sb: e.matmul(
                        sb, katile[:, kk * 128:(kk + 1) * 128],
                        qa[:, h * TPC + qt * 512: h * TPC + qt * 512 + 512],
                        start=True, stop=True))
                    bld.wait("act", "te", s_t)
                    p_t[kk % 2] = bld.comp("act", lambda e, kk=kk, sb=sb: e.activation(
                        out=pbuf[kk % 2][:, 0:512], in_=sb, func=AF.Exp))
                    bld.wait("te", "act", p_t[kk % 2])
                    fnv = lambda e, kk=kk, first=first: e.matmul(
                        pb6[0:65, :], vatile[:, kk * 65: kk * 65 + 65],
                        pbuf[kk % 2][:, 0:512], start=first, stop=(kk == KC - 1))
                    if kk == KC - 1:
                        av_t = bld.comp("te", fnv)
                    else:
                        bld.compq("te", fnv)
                    first = False
                bld.wait("ve", "te", av_t)
                bld.wait("ve", "cq", cq_l)
                rl = rowbuf[0:1, 1536:2048]
                bld.compq("ve", lambda e, rl=rl: e.reciprocal(rl, pb6[64:65, :]))
                vt = bld.comp("ve", lambda e, rl=rl: e.tensor_scalar(
                    rowbf[0:1, 0:512], rl, ATTA_SCALE, None, ALU.mult))
                bld.wait("te", "ve", vt)
                bc_t = bld.comp("te", lambda e: e.matmul(
                    pb4[0:64, :], ones_row[0:1, 0:64], rowbf[0:1, 0:512],
                    start=True, stop=True))
                bld.wait("ve", "te", bc_t)
                bld.compq("ve", lambda e: e.tensor_copy(tmpf[2][0:64, :], pb4[0:64, :]))
                bld.wait("ve", "ctxst", bld.skey("ctxst").n)
                ct = bld.comp("ve", lambda e: e.tensor_tensor(
                    stg[3][0:64, :], pb6[0:64, :], tmpf[2][0:64, :], ALU.mult))
                ol_free = ct
                bld.wait("sy", "ve", ct)
                bld.dma("sy", ctxa[(h % 2) * 64:(h % 2) * 64 + 64,
                                   (h // 2) * TPC + qt * 512:
                                   (h // 2) * TPC + qt * 512 + 512],
                        stg[3][0:64, :], "ctxst")
            kv_guard = bld.sem["te"].n
        bld.join()

        def ctx_rhs(kc, tt):
            return ctxa[:, kc * TPC + tt * 512: kc * TPC + tt * 512 + 512]
        emit_proj_fm(KC, 4, wo_h, ctx_rhs, residual_epi(src, dst, bo_s))

    def sublayer_ffn(src, dst):
        phase_ln(src)
        tr_t = 0
        gate_free = 0
        for tc in range(8):
            bld.wait("te", "ve", gate_free)
            mm_t = 0
            for kc in range(KC):
                fn = lambda e, tc=tc, kc=kc: e.matmul(
                    pb4[:, 0:8], xt[:, kc * TPC + tc * 128: kc * TPC + tc * 128 + 128],
                    gw_s[:, kc * 8:(kc + 1) * 8], start=(kc == 0), stop=(kc == KC - 1))
                if kc == KC - 1:
                    mm_t = bld.comp("te", fn)
                else:
                    bld.compq("te", fn)
            bld.wait("ve", "te", mm_t)
            g = g8[:, 0:8]
            m1 = g8[:, 8:9]
            mk1 = g8[:, 9:17]
            gg2 = g8[:, 17:25]
            m2 = g8[:, 25:26]
            mk2 = g8[:, 26:34]
            dd = g8[:, 34:35]
            s1 = g8[:, 35:36]
            sn = g8[:, 36:37]
            wa_ = g8[:, 37:45]
            gate_free = bld.comp("ve", lambda e, g=g: e.tensor_copy(g, pb4[:, 0:8]))
            bld.compq("ve", lambda e, g=g, m1=m1: e.reduce_max(m1, g, axis=AX.X))
            bld.compq("ve", lambda e, g=g, m1=m1, mk1=mk1: e.tensor_scalar(
                mk1, g, m1, None, ALU.is_ge))
            bld.compq("ve", lambda e, g=g, mk1=mk1, gg2=gg2: e.scalar_tensor_tensor(
                gg2, mk1, MASK_NEG, g, ALU.mult, ALU.add))
            bld.compq("ve", lambda e, gg2=gg2, m2=m2: e.reduce_max(m2, gg2, axis=AX.X))
            bld.compq("ve", lambda e, gg2=gg2, m2=m2, mk2=mk2: e.tensor_scalar(
                mk2, gg2, m2, None, ALU.is_ge))
            vt = bld.comp("ve", lambda e, m1=m1, m2=m2, dd=dd: e.tensor_tensor(
                dd, m1, m2, ALU.subtract))
            bld.wait("act", "ve", vt)
            at = bld.comp("act", lambda e, dd=dd, s1=s1: e.activation(
                out=s1, in_=dd, func=AF.Sigmoid))
            bld.wait("ve", "act", at)
            bld.compq("ve", lambda e, s1=s1, sn=sn: e.tensor_scalar(
                sn, s1, -1.0, 1.0, ALU.mult, ALU.add))
            bld.compq("ve", lambda e, mk1=mk1, s1=s1, wa_=wa_: e.tensor_scalar(
                wa_, mk1, s1, None, ALU.mult))
            bld.wait("ve", "te", tr_t)
            vt = bld.comp("ve", lambda e, mk2=mk2, sn=sn, wa_=wa_: e.scalar_tensor_tensor(
                wtm[:, :], mk2, sn, wa_, ALU.mult, ALU.add))
            bld.wait("te", "ve", vt)
            tr_t = bld.comp("te", lambda e, tc=tc: e.transpose(
                pbt[0:8, tc * 128:(tc + 1) * 128], wtm[:, :], ident[:, :]))
        bld.wait("ve", "te", tr_t)
        wf_t = bld.comp("ve", lambda e: e.tensor_copy(wfm[:, :], pbt[0:8, 0:1024]))

        hh_free = [("ve", 0), ("ve", 0)]
        buf_guard = [0, 0]
        wtg = 0
        for rc in range(4):
            slot = rc % 2
            bld.wait("gp", "te", buf_guard[slot])
            wl = slab_dma("gp", wb_oc[slot], KC, 128, wa_h[rc], f"wb{slot}")
            bld.wait("te", f"wb{slot}", wl)
            for tt in range(2):
                bank = (pb4, pb5)[tt]
                fs, fv = hh_free[tt]
                bld.wait("te", fs, fv)
                mm_t = 0
                for kc in range(KC):
                    fn = lambda e, kc=kc, tt=tt, bank=bank, slot=slot: e.matmul(
                        bank[:, :], wb_oc[slot][:, kc * 128:(kc + 1) * 128],
                        xt_rhs(kc, tt), start=(kc == 0), stop=(kc == KC - 1))
                    if kc == KC - 1:
                        mm_t = bld.comp("te", fn)
                    else:
                        bld.compq("te", fn)
                bld.wait("te", "ve", wf_t)
                bld.wait("te", "ve", wtg)
                wb_t = bld.comp("te", lambda e, rc=rc, tt=tt: e.matmul(
                    pb6[:, :], indic[:, rc * 128:(rc + 1) * 128],
                    wfm[:, tt * 512: tt * 512 + 512], start=True, stop=True))
                bld.wait("ve", "te", wb_t)
                wtg = bld.comp("ve", lambda e: e.tensor_copy(stg[2][:, :], pb6[:, :]))
                hh = bld.comp("ve", lambda e, rc=rc, tt=tt, bank=bank: e.tensor_tensor(
                    hhat[:, rc * TPC + tt * 512: rc * TPC + tt * 512 + 512],
                    bank[:, :], stg[2][:, :], ALU.mult))
                hh_free[tt] = ("ve", hh)
            buf_guard[slot] = bld.sem["te"].n
        bld.join()

        def h_rhs(kc, tt):
            return hhat[:, kc * TPC + tt * 512: kc * TPC + tt * 512 + 512]
        emit_proj_fm(KC, 4, wbw_h, h_rhs, residual_epi(src, dst, None))

    def sublayer_attn_b(src, dst):
        phase_ln(src)
        bld.dma("sy", ropeC[:, :], ropeC_h[:, :], "cst")
        rl_t = bld.dma("sy", ropeS[:, :], ropeS_h[:, :], "cst")
        bld.wait("ve", "cst", rl_t)

        slab_guard = [0, 0, 0, 0]
        st_guard = {0: 0, 2: 0}
        rope_free = {0: ("ve", 0), 1: ("ve", 0)}
        for h in range(32):
            sA = (h % 2) * 2
            sB = sA + 1
            bld.wait("gp", "te", slab_guard[sA])
            wlA = slab_dma("gp", qk_slab(sA), KC, 128, wqkqk_h[2 * h], f"slab{sA}")
            bld.wait("gp", "te", slab_guard[sB])
            wlB = slab_dma("gp", qk_slab(sB), KC, 128, wqkqk_h[2 * h + 1], f"slab{sB}")
            bld.wait("te", f"slab{sA}", wlA)
            bld.wait("te", f"slab{sB}", wlB)
            for tt in range(2):
                pA = (pb4, pb6)[tt]
                pB = (pb5[:, 0:512], ps2a[:, 0:512])[tt]
                fs, fv = rope_free[tt]
                bld.wait("te", fs, fv)
                for kc in range(KC):
                    bld.compq("te", lambda e, kc=kc, sA=sA, tt=tt, pA=pA: e.matmul(
                        pA[:, 0:512], qk_slab(sA)[:, kc * 128:(kc + 1) * 128],
                        xt_rhs(kc, tt), start=(kc == 0), stop=(kc == KC - 1)))
                mm_t = 0
                for kc in range(KC):
                    fn = lambda e, kc=kc, sB=sB, tt=tt, pB=pB: e.matmul(
                        pB, qk_slab(sB)[:, kc * 128:(kc + 1) * 128],
                        xt_rhs(kc, tt), start=(kc == 0), stop=(kc == KC - 1))
                    if kc == KC - 1:
                        mm_t = bld.comp("te", fn)
                    else:
                        bld.compq("te", fn)
                t0 = tt * 512
                bA = bqkqk_s[:, 2 * h: 2 * h + 1]
                bB = bqkqk_s[:, 2 * h + 1: 2 * h + 2]
                sg0 = tt * 2
                bld.wait("ve", "te", mm_t)
                bld.wait("ve", f"ropest{sg0}", st_guard[sg0])
                bld.compq("ve", lambda e, pA=pA, bA=bA, t0=t0: e.scalar_tensor_tensor(
                    tmpf[0][:, :], pA[:, 0:512], bA, ropeC[:, t0:t0 + 512],
                    ALU.add, ALU.mult))
                bld.compq("ve", lambda e, pB=pB, bB=bB, t0=t0: e.scalar_tensor_tensor(
                    tmpf[1][:, :], pB, bB, ropeS[:, t0:t0 + 512],
                    ALU.add, ALU.mult))
                bld.compq("ve", lambda e, sg0=sg0: e.tensor_tensor(
                    stg[sg0][:, :], tmpf[0][:, :], tmpf[1][:, :], ALU.subtract))
                bld.compq("ve", lambda e, pB=pB, bB=bB, t0=t0: e.scalar_tensor_tensor(
                    tmpf[0][:, :], pB, bB, ropeC[:, t0:t0 + 512],
                    ALU.add, ALU.mult))
                bld.compq("ve", lambda e, pA=pA, bA=bA, t0=t0: e.scalar_tensor_tensor(
                    tmpf[1][:, :], pA[:, 0:512], bA, ropeS[:, t0:t0 + 512],
                    ALU.add, ALU.mult))
                vt = bld.comp("ve", lambda e, sg0=sg0: e.tensor_tensor(
                    stg[sg0 + 1][:, :], tmpf[0][:, :], tmpf[1][:, :], ALU.add))
                rope_free[tt] = ("ve", vt)
                bld.wait("act", "ve", vt)
                bld.dma("act", qbuf[h * 128: h * 128 + 64, t0:t0 + 512],
                        stg[sg0][0:64, :], f"ropest{sg0}")
                bld.dma("act", qbuf[h * 128 + 64: h * 128 + 128, t0:t0 + 512],
                        stg[sg0 + 1][0:64, :], f"ropest{sg0}")
                bld.dma("act", kvbK[h * 128: h * 128 + 64, t0:t0 + 512],
                        stg[sg0][64:128, :], f"ropest{sg0}")
                stq = bld.dma("act", kvbK[h * 128 + 64: h * 128 + 128, t0:t0 + 512],
                              stg[sg0 + 1][64:128, :], f"ropest{sg0}")
                st_guard[sg0] = stq
            slab_guard[sA] = bld.sem["te"].n
            slab_guard[sB] = slab_guard[sA]
        bld.join()

        vv_guard = 0
        vfree = [("ve", 0), ("ve", 0)]
        st_g = [0, 0]
        for ob in range(8):
            bld.wait("gp", "te", vv_guard)
            bld.dma("gp", bvv_s[:, :], bqkvv_h[ob:ob + 1, :], "bvv")
            for half in range(2):
                slab_dma("gp", wsl[half], 16, 512,
                         wqkvv_h[ob, half * 16:(half + 1) * 16], f"wsl{half}")
            bld.wait("te", "bvv", bld.skey("bvv").n)
            bld.wait("te", "wsl0", bld.skey("wsl0").n)
            bld.wait("te", "wsl1", bld.skey("wsl1").n)
            for tc in range(8):
                bank = (pb4, pb5)[tc % 2]
                fs, fv = vfree[tc % 2]
                bld.wait("te", fs, fv)
                for kc in range(KC):
                    bld.compq("te", lambda e, tc=tc, kc=kc, bank=bank: e.matmul(
                        bank[:, :],
                        xt[:, kc * TPC + tc * 128: kc * TPC + tc * 128 + 128],
                        wsl[kc // 16][:, (kc % 16) * 512:(kc % 16) * 512 + 512],
                        start=(kc == 0), stop=False))
                mm_t = bld.comp("te", lambda e, bank=bank: e.matmul(
                    bank[:, :], ones_row[:, :], bvv_s[:, :], start=False, stop=True))
                bld.wait("ve", "te", mm_t)
                bld.wait("ve", f"vst{tc % 2}", st_g[tc % 2])
                vt = bld.comp("ve", lambda e, tc=tc, bank=bank: e.tensor_copy(
                    stg[tc % 2][:, :], bank[:, :]))
                vfree[tc % 2] = ("ve", vt)
                bld.wait("act", "ve", vt)
                st_g[tc % 2] = bld.dma(
                    "act", kvbV[tc * 128:(tc + 1) * 128, ob * 512:(ob + 1) * 512],
                    stg[tc % 2][:, :], f"vst{tc % 2}")
            vv_guard = bld.sem["te"].n
        bld.join()

        bld.comp("gp", lambda e: e.collective_compute(
            "AllGather", ALU.bypass, ins=[kvb_in[:]], outs=[kvb_all[:, :]],
            replica_groups=[[0, 1, 2, 3], [4, 5, 6, 7]]))
        bld.dma("gp", maskS.rearrange("k (g c) -> k g c", g=8),
                maskb_h.rearrange("g k c -> k g c"), "cstg")
        bld.join()
        bld.dma("gp", wsl[1].rearrange("k (g c) -> k g c", g=8),
                maskbT_h.rearrange("g k c -> k g c"), "mskT")
        bld.join()

        kvbKg = kvb_all[:, 0:D * TPC].rearrange("r (d t) -> r d t", d=D)
        kvbVg = kvb_all[:, D * TPC:2 * D * TPC].rearrange("r (t d) -> r t d", t=TPC)

        kpair = wsl[0][:, 0:KC * 128]
        vpair = wsl[0][:, KC * 128: 2 * KC * 128]

        def maskT_ap(p, qs):
            return wsl[1][:, (p * 2 + qs) * 1024:(p * 2 + qs + 1) * 1024]

        kvp_guard = 0
        ol_free = [("ve", 0), ("ve", 0)]
        l_guard = 0
        q_guard = 0
        p1v = [0, 0]
        pbt_g = 0
        negb = rowbf[0:1, 512:768]
        for h in range(32):
            for p in range(4):
                ng = 2 * p + 2
                bld.wait("sy", "te", q_guard)
                ql = bld.dma("sy", qtile[:, :],
                             qbuf[h * 128:(h + 1) * 128, p * 256:(p + 1) * 256], "qt")
                bld.wait("sy", "te", kvp_guard)
                for g in range(ng):
                    bld.dma("sy",
                            kpair[:, g * 512:(g + 1) * 512].rearrange(
                                "p (r t) -> p r t", r=4),
                            kvbKg[:, h * 128:(h + 1) * 128,
                                  g * 128:(g + 1) * 128].rearrange("r d t -> d r t"),
                            "kvp")
                    bld.dma("sy",
                            vpair[:, g * 512:(g + 1) * 512].rearrange(
                                "p (r c) -> p r c", r=4),
                            kvbVg[:, g * 128:(g + 1) * 128,
                                  h * 128:(h + 1) * 128].rearrange("r t c -> t r c"),
                            "kvp")
                kvl = bld.skey("kvp").n
                bld.wait("te", "qt", ql)
                bld.wait("te", "kvp", kvl)
                bld.wait("te", "act", pbt_g)
                nv = 0
                for qs in range(2):
                    for kt in range(ng):
                        bank = (ps2a, ps2b)[kt % 2]
                        bld.wait("te", "ve", p1v[kt % 2])
                        s1 = bld.comp("te", lambda e, qs=qs, kt=kt, bank=bank: e.matmul(
                            bank[:, 0:512], qtile[:, qs * 128:(qs + 1) * 128],
                            kpair[:, kt * 512:(kt + 1) * 512], start=True, stop=True))
                        bld.wait("ve", "te", s1)
                        if kt >= ng - 2:
                            j2 = kt - (ng - 2)
                            bld.compq("ve", lambda e, p=p, qs=qs, j2=j2, bank=bank:
                                      e.tensor_tensor(
                                tmpf[2][:, :], bank[:, 0:512],
                                maskT_ap(p, qs)[:, j2 * 512:(j2 + 1) * 512], ALU.add))
                            p1v[kt % 2] = bld.comp("ve", lambda e, kt=kt: e.reduce_max(
                                g8[:, kt:kt + 1], tmpf[2][:, :], axis=AX.X))
                        else:
                            p1v[kt % 2] = bld.comp("ve", lambda e, kt=kt, bank=bank:
                                                   e.reduce_max(
                                g8[:, kt:kt + 1], bank[:, 0:512], axis=AX.X))
                    bld.compq("ve", lambda e, ng=ng: e.reduce_max(
                        g8[:, 40:41], g8[:, 0:ng], axis=AX.X))
                    nv = bld.comp("ve", lambda e, qs=qs: e.tensor_scalar(
                        wtm[:, qs:qs + 1], g8[:, 40:41], -1.0, None, ALU.mult))
                bld.wait("te", "ve", nv)
                tr_t = 0
                for qs in range(2):
                    tr_t = bld.comp("te", lambda e, qs=qs: e.transpose(
                        pbt[0:1, qs * 128:(qs + 1) * 128], wtm[:, qs:qs + 1],
                        ident[:, :]))
                bld.wait("act", "te", tr_t)
                ng_t = bld.comp("act", lambda e: e.copy(negb, pbt[0:1, 0:256]))
                pbt_g = ng_t
                olb = (pb4, pb6)[p % 2]
                fs, fv = ol_free[p % 2]
                bld.wait("te", fs, fv)
                bld.wait("te", "ve", l_guard)
                bld.wait("te", "act", ng_t)
                first = True
                p_t = [0, 0]
                av_t = 0
                for g in range(ng):
                    slot = g % 2
                    sbank = (ps2a, ps2b)[slot]
                    bld.wait("te", "act", p_t[slot])
                    s_t = 0
                    for ki in range(4):
                        bld.compq("te", lambda e, g=g, ki=ki, sbank=sbank: e.matmul(
                            sbank[:, ki * 256:(ki + 1) * 256],
                            kpair[:, (g * 4 + ki) * 128:(g * 4 + ki + 1) * 128],
                            qtile[:, :], start=True, stop=False))
                        fn = lambda e, ki=ki, sbank=sbank: e.matmul(
                            sbank[:, ki * 256:(ki + 1) * 256],
                            ones_row[0:1, 0:128], negb, start=False, stop=True)
                        if ki == 3:
                            s_t = bld.comp("te", fn)
                        else:
                            bld.compq("te", fn)
                    if g >= 2 * p:
                        j = g - 2 * p
                        bld.wait("ve", "te", s_t)
                        mk_t = bld.comp("ve", lambda e, p=p, j=j, sbank=sbank:
                                        e.tensor_tensor(
                            sbank[:, :], sbank[:, :],
                            maskS[:, (p * 2 + j) * 1024:(p * 2 + j + 1) * 1024],
                            ALU.add))
                        bld.wait("act", "ve", mk_t)
                    else:
                        bld.wait("act", "te", s_t)
                    p_t[slot] = bld.comp("act", lambda e, slot=slot, sbank=sbank:
                                         e.activation(
                        out=pbuf[slot][:, :], in_=sbank[:, :], func=AF.Exp,
                        scale=ATTB_SCALE))
                    bld.wait("te", "act", p_t[slot])
                    for ki in range(4):
                        bld.compq("te", lambda e, g=g, ki=ki, slot=slot, olb=olb,
                                  first=(first and ki == 0),
                                  last=(g == ng - 1 and ki == 3): e.matmul(
                            olb[:, 0:256],
                            vpair[:, (g * 4 + ki) * 128:(g * 4 + ki + 1) * 128],
                            pbuf[slot][:, ki * 256:(ki + 1) * 256],
                            start=first, stop=last))
                        if ki == 0:
                            first = False
                        last = (g == ng - 1 and ki == 3)
                        lfirst = (g == 0 and ki == 0)
                        fn2 = lambda e, ki=ki, slot=slot, last=last, lfirst=lfirst: \
                            e.matmul(
                            pb5[0:1, 256:512], ones_col[:, :],
                            pbuf[slot][:, ki * 256:(ki + 1) * 256],
                            start=lfirst, stop=last)
                        if ki == 3:
                            av_t = bld.comp("te", fn2)
                        else:
                            bld.compq("te", fn2)
                bld.wait("ve", "te", av_t)
                rl = rowbuf[0:1, 1536:1792]
                bld.compq("ve", lambda e, rl=rl: e.reciprocal(
                    rl, pb5[0:1, 256:512]))
                vt = bld.comp("ve", lambda e, rl=rl: e.tensor_copy(
                    rowbf[0:1, 0:256], rl))
                bld.wait("te", "ve", vt)
                bc_t = bld.comp("te", lambda e: e.matmul(
                    pb5[:, 0:256], ones_row[:, :], rowbf[0:1, 0:256],
                    start=True, stop=True))
                bld.wait("ve", "te", bc_t)
                bld.compq("ve", lambda e: e.tensor_copy(
                    tmpf[2][:, 0:256], pb5[:, 0:256]))
                ct = bld.comp("ve", lambda e, h=h, p=p, olb=olb: e.tensor_tensor(
                    xt[:, h * TPC + p * 256: h * TPC + p * 256 + 256],
                    olb[:, 0:256], tmpf[2][:, 0:256], ALU.mult))
                ol_free[p % 2] = ("ve", ct)
                l_guard = ct
                q_guard = bld.sem["te"].n
                kvp_guard = q_guard
        bld.join()

        emit_proj_fm(KC, KC, ow_h, xt_rhs, residual_epi(src, dst, obias_s))

    subs = [sublayer_attn_a, sublayer_ffn, sublayer_attn_b, sublayer_ffn]
    src = x_fm
    for i in range(N_SUB):
        dst = y_out if i == N_SUB - 1 else Xd
        subs[i](src, dst)
        src = Xd

    bld.join()
    for k in ("rst0", "rst1"):
        bld.wait("act", k, bld.skey(k).n)

    with nc.Block() as block:
        block.tensor(lambda e: [f(e) for f in bld.ops["te"]])
        block.vector(lambda e: [f(e) for f in bld.ops["ve"]])
        block.scalar(lambda e: [f(e) for f in bld.ops["act"]])
        block.sync(lambda e: [f(e) for f in bld.ops["sy"]])
        block.gpsimd(lambda e: [f(e) for f in bld.ops["gp"]])
    return nc


def kernel(**inputs):
    in_maps = _prep_inputs(inputs)
    nc = build_program()
    res = run_bass_kernel_spmd(nc, in_maps, core_ids=list(range(NC)))
    return _reassemble(res.results)


# version 05
# speedup vs baseline: 1.0891x; 1.0891x over previous
import numpy as np
import concourse.bass as bass
import concourse.mybir as mybir
from concourse.bass_utils import run_bass_kernel_spmd

F32 = mybir.dt.float32
BF16 = mybir.dt.bfloat16
AF = mybir.ActivationFunctionType
ALU = mybir.AluOpType
AX = mybir.AxisListType

B = 2
S = 4096
D = 4096
TPC = 1024
NC = 8
KC = 32
MASK_NEG = -1.0e9
ATTA_SCALE = 1.0 / float(np.sqrt(512.0))
ATTB_SCALE = 1.0 / float(np.sqrt(128.0))
CQ_FACTOR = 0.55
LN_EPS = 1e-5

N_SUB = 4



def _tile_oc_kc(w):
    Dk, O = w.shape
    return np.ascontiguousarray(
        w.reshape(Dk // 128, 128, O // 128, 128).transpose(2, 0, 1, 3))


def _tile_kc_ob(w, ob):
    Dk, O = w.shape
    return np.ascontiguousarray(
        w.reshape(Dk // 128, 128, O // ob, ob).transpose(2, 0, 1, 3))


def _prep_inputs(inp):
    x = np.asarray(inp["L0_attn_a_x"], np.float32)

    wqk = np.asarray(inp["L0_attn_a_Wqk_w"], np.float32)
    bqk = np.asarray(inp["L0_attn_a_Wqk_b"], np.float32)
    perm = np.empty(1024, np.int64)
    for h in range(8):
        for e in range(64):
            perm[h * 64 + e] = h * 128 + 2 * e
            perm[512 + h * 64 + e] = h * 128 + 2 * e + 1
    wqk_tiled = _tile_oc_kc(np.ascontiguousarray(wqk.T[:, perm]))
    bqk_dev = np.ascontiguousarray(bqk[perm].reshape(8, 128).T)

    wv = np.asarray(inp["L0_attn_a_Wv_w"], np.float32)
    bv = np.asarray(inp["L0_attn_a_Wv_b"], np.float32)
    wv_tiled = _tile_kc_ob(np.ascontiguousarray(wv.T), 512)
    bv_dev = bv.reshape(1, 512)

    wo = np.asarray(inp["L0_attn_a_Wo_w"], np.float32)
    bo = np.asarray(inp["L0_attn_a_Wo_b"], np.float32)
    wo_tiled = _tile_oc_kc(np.ascontiguousarray(wo.T))
    bo_dev = np.ascontiguousarray(bo.reshape(32, 128).T)

    qkv = np.asarray(inp["L0_attn_b_qkv_w"], np.float32)
    qkv_b = np.asarray(inp["L0_attn_b_qkv_b"], np.float32)
    qw = qkv[:, 0:4096].reshape(4096, 32, 128)
    kw = qkv[:, 4096:8192].reshape(4096, 32, 128)
    vw = qkv[:, 8192:12288]
    qb = qkv_b[0:4096].reshape(32, 128)
    kb = qkv_b[4096:8192].reshape(32, 128)
    vb = qkv_b[8192:12288]
    qk_cols = np.empty((4096, 64, 128), np.float32)
    qk_bias = np.empty((64, 128), np.float32)
    for h in range(32):
        qk_cols[:, 2 * h, 0:64] = qw[:, h, 0:64]
        qk_cols[:, 2 * h, 64:128] = kw[:, h, 0:64]
        qk_cols[:, 2 * h + 1, 0:64] = qw[:, h, 64:128]
        qk_cols[:, 2 * h + 1, 64:128] = kw[:, h, 64:128]
        qk_bias[2 * h, 0:64] = qb[h, 0:64]
        qk_bias[2 * h, 64:128] = kb[h, 0:64]
        qk_bias[2 * h + 1, 0:64] = qb[h, 64:128]
        qk_bias[2 * h + 1, 64:128] = kb[h, 64:128]
    wqkv_qk = _tile_oc_kc(qk_cols.reshape(4096, 64 * 128))
    bqkv_qk = np.ascontiguousarray(qk_bias.T)
    wqkv_v = _tile_kc_ob(np.ascontiguousarray(vw), 512)
    bqkv_v = np.ascontiguousarray(vb.reshape(8, 512))

    ow = np.asarray(inp["L0_attn_b_out_w"], np.float32)
    ob_ = np.asarray(inp["L0_attn_b_out_b"], np.float32)
    ow_tiled = _tile_oc_kc(ow)
    ob_dev = np.ascontiguousarray(ob_.reshape(32, 128).T)

    gw = np.asarray(inp["L0_ffn_gate_w"], np.float32)
    gw_tiled = _tile_kc_ob(np.ascontiguousarray(gw.T), 8)
    la = np.asarray(inp["L0_ffn_lora_A"], np.float32)
    lb = np.asarray(inp["L0_ffn_lora_B"], np.float32)
    a_cat = np.ascontiguousarray(np.concatenate([la[e].T for e in range(8)], axis=1))
    b_cat = np.ascontiguousarray(np.concatenate([lb[e].T for e in range(8)], axis=0))
    a_tiled = _tile_oc_kc(a_cat)
    b_tiled = _tile_oc_kc(b_cat)

    ident = np.eye(128, dtype=np.float32)
    indic = np.zeros((4, 8, 128), np.float32)
    for rc in range(4):
        for m in range(128):
            indic[rc, (rc * 128 + m) // 64, m] = 1.0

    inv_freq = 10000.0 ** (-(np.arange(64, dtype=np.float64) / 64.0))

    shared = {
        "wqk": wqk_tiled, "bqk": bqk_dev, "wv": wv_tiled, "bv": bv_dev,
        "wo": wo_tiled, "bo": bo_dev,
        "wqkv_qk": wqkv_qk, "bqkv_qk": bqkv_qk, "wqkv_v": wqkv_v, "bqkv_v": bqkv_v,
        "ow": ow_tiled, "obias": ob_dev,
        "gw": gw_tiled, "wa": a_tiled, "wbw": b_tiled,
        "ident": ident, "indic": indic,
    }

    in_maps = []
    for core in range(NC):
        b = core // 4
        c = core % 4
        blocks = [c + 4 * j for j in range(8)]
        pos = np.concatenate([np.arange(g * 128, (g + 1) * 128) for g in blocks])
        x_fm = np.ascontiguousarray(x[b, pos, :].T)

        ang = pos[:, None].astype(np.float64) * inv_freq[None, :]
        cos_t = np.cos(ang).astype(np.float32).T
        sin_t = np.sin(ang).astype(np.float32).T
        ropeC = np.ascontiguousarray(np.concatenate([cos_t, cos_t], axis=0))
        ropeS = np.ascontiguousarray(np.concatenate([sin_t, sin_t], axis=0))

        mask = np.zeros((4, 2, 128, 1024), np.float32)
        for p in range(4):
            for j in range(2):
                g = 2 * p + j
                for ki in range(4):
                    kappa = 4 * g + ki
                    gk = kappa * 128 + np.arange(128)
                    for sub in range(2):
                        slot = 2 * p + sub
                        gq = (c + 4 * slot) * 128 + np.arange(128)
                        m = np.where(gk[:, None] <= gq[None, :], 0.0, MASK_NEG)
                        mask[p, j, :, ki * 256 + sub * 128: ki * 256 + sub * 128 + 128] = m

        maskT = np.zeros((4, 2, 128, 1024), np.float32)
        for p in range(4):
            for sub in range(2):
                slot = 2 * p + sub
                gq = (c + 4 * slot) * 128 + np.arange(128)
                gk = 8 * p * 128 + np.arange(1024)
                maskT[p, sub] = np.where(gk[None, :] <= gq[:, None], 0.0, MASK_NEG)
        d = {"x_fm": x_fm, "ropeC": ropeC, "ropeS": ropeS,
             "maskb": mask.reshape(8, 128, 1024),
             "maskbT": maskT.reshape(8, 128, 1024)}
        d.update(shared)
        in_maps.append(d)
    return in_maps


def _reassemble(results):
    out = np.empty((B, S, D), np.float32)
    for core in range(NC):
        b = core // 4
        c = core % 4
        yt = results[core]["y"].T.reshape(8, 128, 4096)
        for j in range(8):
            g = c + 4 * j
            out[b, g * 128:(g + 1) * 128, :] = yt[j]
    return out



class Sem:
    def __init__(self, nc, name):
        self.h = nc.alloc_semaphore(name)
        self.n = 0

    def inc(self, d):
        self.n += d
        return self.n


class Bld:
    ENGS = ("te", "ve", "act", "sy", "gp")

    def __init__(self, nc):
        self.nc = nc
        self.ops = {k: [] for k in self.ENGS}
        self.sem = {k: Sem(nc, "s_" + k) for k in
                    ("te", "ve", "act", "sy", "gp", "st", "gpc")}

    def emit(self, eng, fn):
        self.ops[eng].append(fn)

    def wait(self, eng, sem_name, val):
        if val <= 0:
            return
        s = self.skey(sem_name)
        self.emit(eng, lambda e, s=s, v=val: e.wait_ge(s.h, v))

    def skey(self, key):
        if key not in self.sem:
            self.sem[key] = Sem(self.nc, "s_" + key)
        return self.sem[key]

    def dma(self, eng, out, in_, key):
        if eng == "act":
            self._selfedge("act")
        s = self.skey(key)
        self.emit(eng, lambda e, o=out, i=in_, s=s:
                  e.dma_start(out=o, in_=i).then_inc(s.h, 16))
        return s.inc(16)

    def _selfedge(self, eng):
        s = self.sem[eng]
        if s.n > 0:
            v = s.n
            self.emit(eng, lambda e, s=s, v=v: e.wait_ge(s.h, v))

    def comp(self, eng, fn, sem_name=None):
        if eng in ("ve", "act"):
            self._selfedge(eng)
        s = self.sem[sem_name or ("gpc" if eng == "gp" else eng)]
        self.emit(eng, lambda e, f=fn, s=s: f(e).then_inc(s.h, 1))
        return s.inc(1)

    def compq(self, eng, fn):
        if eng in ("ve", "act"):
            self.comp(eng, fn)
            return
        self.emit(eng, lambda e, f=fn: f(e))

    def join(self):
        targets = {k: v.n for k, v in self.sem.items()}
        for eng in self.ENGS:
            for sn, v in targets.items():
                if sn == eng:
                    continue
                self.wait(eng, sn, v)


def build_program():
    nc = bass.Bass("TRN2", target_bir_lowering=False, num_devices=NC)
    bld = Bld(nc)

    DT = nc.dram_tensor
    x_fm = DT("x_fm", [D, TPC], F32, kind="ExternalInput").ap()
    y_out = DT("y", [D, TPC], F32, kind="ExternalOutput").ap()
    wqk_h = DT("wqk", [8, KC, 128, 128], F32, kind="ExternalInput").ap()
    bqk_h = DT("bqk", [128, 8], F32, kind="ExternalInput").ap()
    wv_h = DT("wv", [1, KC, 128, 512], F32, kind="ExternalInput").ap()
    bv_h = DT("bv", [1, 512], F32, kind="ExternalInput").ap()
    wo_h = DT("wo", [KC, 4, 128, 128], F32, kind="ExternalInput").ap()
    bo_h = DT("bo", [128, KC], F32, kind="ExternalInput").ap()
    wqkqk_h = DT("wqkv_qk", [64, KC, 128, 128], F32, kind="ExternalInput").ap()
    bqkqk_h = DT("bqkv_qk", [128, 64], F32, kind="ExternalInput").ap()
    wqkvv_h = DT("wqkv_v", [8, KC, 128, 512], F32, kind="ExternalInput").ap()
    bqkvv_h = DT("bqkv_v", [8, 512], F32, kind="ExternalInput").ap()
    ow_h = DT("ow", [KC, KC, 128, 128], F32, kind="ExternalInput").ap()
    obias_h = DT("obias", [128, KC], F32, kind="ExternalInput").ap()
    gw_h = DT("gw", [1, KC, 128, 8], F32, kind="ExternalInput").ap()
    wa_h = DT("wa", [4, KC, 128, 128], F32, kind="ExternalInput").ap()
    wbw_h = DT("wbw", [KC, 4, 128, 128], F32, kind="ExternalInput").ap()
    ident_h = DT("ident", [128, 128], F32, kind="ExternalInput").ap()
    indic_h = DT("indic", [4, 8, 128], F32, kind="ExternalInput").ap()
    ropeC_h = DT("ropeC", [128, TPC], F32, kind="ExternalInput").ap()
    ropeS_h = DT("ropeS", [128, TPC], F32, kind="ExternalInput").ap()
    maskb_h = DT("maskb", [8, 128, 1024], F32, kind="ExternalInput").ap()
    maskbT_h = DT("maskbT", [8, 128, 1024], F32, kind="ExternalInput").ap()

    Xd = DT("Xd", [D, TPC], F32, kind="Internal").ap()
    qbuf = DT("qbuf", [D, TPC], BF16, kind="Internal").ap()
    kvb_in = DT("kvb_in", [2 * D * TPC], BF16, kind="Internal").ap()
    kvb_all = DT("kvb_all", [4, 2 * D * TPC], BF16, kind="Internal").ap()
    kva_in = DT("kva_in", [2 * 512 * TPC], BF16, kind="Internal").ap()
    kva_all = DT("kva_all", [4, 2 * 512 * TPC], BF16, kind="Internal").ap()

    kvbK = kvb_in[0:D * TPC].rearrange("(d t) -> d t", d=D)
    kvbV = kvb_in[D * TPC:2 * D * TPC].rearrange("(t d) -> t d", t=TPC)
    kvaK = kva_in[0:512 * TPC].rearrange("(d t) -> d t", d=512)
    kvaV = kva_in[512 * TPC:2 * 512 * TPC].rearrange("(t d) -> t d", t=TPC)

    A = lambda name, shape, dt: nc.alloc_sbuf_tensor(name, shape, dt).ap()
    xt = A("xt", [128, KC * TPC], BF16)
    off1, _ = nc.bump_sbuf(16384)
    qa = nc.alloc_sbuf_tensor_at("qa", [65, 8 * TPC], BF16, offset=off1).ap()
    maskS = nc.alloc_sbuf_tensor_at("maskS", [128, 8 * 1024], BF16, offset=off1).ap()
    off2, _ = nc.bump_sbuf(8192)
    ctxa = nc.alloc_sbuf_tensor_at("ctxa", [128, 4 * TPC], BF16, offset=off2).ap()
    ropeC = nc.alloc_sbuf_tensor_at("ropeCs", [128, TPC], F32, offset=off2).ap()
    ropeS = nc.alloc_sbuf_tensor_at("ropeSs", [128, TPC], F32, offset=off2 + 4096).ap()
    wsl = [A(f"wsl{i}", [128, 16 * 512], BF16) for i in range(2)]

    def qk_slab(i):
        return wsl[i // 2][:, (i % 2) * (KC * 128):(i % 2 + 1) * (KC * 128)]

    wb_oc = [A(f"wboc{i}", [128, KC * 128], BF16) for i in range(2)]
    hhat = A("hhat", [128, 4 * TPC], BF16)
    katile = A("katile", [65, KC * 128], BF16)
    vatile = A("vatile", [128, KC * 65], BF16)
    pbuf = [A(f"pbuf{i}", [128, 1024], BF16) for i in range(2)]
    kbufs = [A(f"kbuf{i}", [128, 512], BF16) for i in range(2)]
    vbufs = [A(f"vbuf{i}", [128, 512], BF16) for i in range(2)]
    qtile = A("qtile", [128, 256], BF16)
    stg = [A(f"stg{i}", [128, 512], BF16) for i in range(4)]
    tmpf = [A(f"tmpf{i}", [128, 512], F32) for i in range(3)]
    xb2 = [A(f"xb{i}", [128, 512], BF16) for i in range(2)]
    sq2 = [A(f"sq{i}", [128, 512], BF16) for i in range(2)]
    rowbuf = A("rowbuf", [1, 2048], F32)
    rowbf = A("rowbf", [1, 1024], BF16)
    g8 = A("g8", [128, 48], F32)
    wtm = A("wtm", [128, 8], BF16)
    wfm = A("wfm", [8, TPC], BF16)
    ident = A("idents", [128, 128], BF16)
    indic = A("indics", [8, 4 * 128], BF16)
    ones_col = A("ones_col", [128, 1], BF16)
    ones_row = A("ones_row", [1, 128], BF16)
    bqk_s = A("bqk_s", [128, 8], F32)
    bo_s = A("bo_s", [128, KC], F32)
    bqkqk_s = A("bqkqk_s", [128, 64], F32)
    obias_s = A("obias_s", [128, KC], F32)
    bv_s = A("bv_s", [1, 512], BF16)
    bvv_s = A("bvv_s", [1, 512], BF16)
    gw_s = A("gw_s", [128, KC * 8], BF16)

    P = lambda name, shape, dt=F32: nc.alloc_psum_tensor(name, shape, dt).ap()
    ps2a = P("ps2a", [128, 1024])
    ps2b = P("ps2b", [128, 1024])
    pb4 = P("pb4", [128, 512])
    pb5 = P("pb5", [128, 512])
    pb6 = P("pb6", [128, 512])
    pbt = P("pbt", [128, 1024], BF16)

    bld.dma("sy", bqk_s[:, :], bqk_h[:, :], "cst")
    bld.dma("sy", bo_s[:, :], bo_h[:, :], "cst")
    bld.dma("sy", bqkqk_s[:, :], bqkqk_h[:, :], "cst")
    bld.dma("sy", obias_s[:, :], obias_h[:, :], "cst")
    bld.dma("gp", ident[:, :], ident_h[:, :], "cstg")
    bld.dma("gp", indic.rearrange("e (r c) -> e r c", r=4),
            indic_h.rearrange("r e c -> e r c"), "cstg")
    bld.dma("gp", bv_s[:, :], bv_h[:, :], "cstg")
    bld.dma("gp", gw_s.rearrange("d (k e) -> d k e", k=KC),
            gw_h[0].rearrange("k d e -> d k e"), "cstg")
    bld.comp("gp", lambda e: e.memset(ones_col[:, :], 1.0))
    bld.comp("gp", lambda e: e.memset(ones_row[:, :], 1.0))
    bld.join()

    def slab_dma(eng, buf, n_kc, width, w_oc, key):
        return bld.dma(eng,
                       buf[:, 0:n_kc * width].rearrange("d (k o) -> d k o", k=n_kc),
                       w_oc.rearrange("k d o -> d k o"), key)

    def phase_ln(src):
        tmp_use = [0, 0]
        sqg = [0, 0]
        mmg = [0, 0]
        for tt in range(2):
            t0 = tt * 512
            sum_done = 0
            for kc in range(KC):
                sl = kc % 2
                bld.wait("sy", "ve", tmp_use[sl])
                lt = bld.dma("sy", tmpf[sl][:, :],
                             src[kc * 128:(kc + 1) * 128, t0:t0 + 512],
                             f"ltmp{sl}")
                bld.wait("ve", f"ltmp{sl}", lt)
                bld.wait("ve", "act", sqg[sl])
                bld.wait("ve", "te", mmg[sl])
                ct = bld.comp("ve", lambda e, sl=sl: e.tensor_copy(
                    xb2[sl][:, :], tmpf[sl][:, :]))
                tmp_use[sl] = ct
                bld.wait("act", "ve", ct)
                bld.wait("act", "te", mmg[sl])
                sq_t = bld.comp("act", lambda e, sl=sl: e.activation(
                    out=sq2[sl][:, :], in_=xb2[sl][:, :], func=AF.Square))
                sqg[sl] = sq_t
                bld.wait("te", "ve", ct)
                bld.compq("te", lambda e, sl=sl, kc=kc: e.matmul(
                    pb4[0:1, :], ones_col[:, :], xb2[sl][:, :],
                    start=(kc == 0), stop=(kc == KC - 1)))
                bld.wait("te", "act", sq_t)
                sum_done = bld.comp("te", lambda e, sl=sl, kc=kc: e.matmul(
                    pb5[0:1, :], ones_col[:, :], sq2[sl][:, :],
                    start=(kc == 0), stop=(kc == KC - 1)))
                mmg[sl] = sum_done
            m_row = rowbuf[0:1, 0:512]
            e2_row = rowbuf[0:1, 512:1024]
            var_row = rowbuf[0:1, 1024:1536]
            r_row = rowbuf[0:1, 1536:2048]
            bld.wait("ve", "te", sum_done)
            bld.compq("ve", lambda e: e.tensor_scalar(
                m_row, pb4[0:1, :], 1.0 / D, None, ALU.mult))
            bld.compq("ve", lambda e: e.tensor_scalar(
                e2_row, pb5[0:1, :], 1.0 / D, None, ALU.mult))
            bld.compq("ve", lambda e: e.tensor_tensor(var_row, m_row, m_row, ALU.mult))
            bld.compq("ve", lambda e: e.tensor_tensor(
                var_row, e2_row, var_row, ALU.subtract))
            bld.compq("ve", lambda e: e.tensor_scalar(
                var_row, var_row, LN_EPS, None, ALU.add))
            vt = bld.comp("ve", lambda e: e.reciprocal(r_row, var_row))
            bld.wait("act", "ve", vt)
            at = bld.comp("act", lambda e: e.activation(
                out=r_row, in_=r_row, func=AF.Sqrt))
            bld.wait("ve", "act", at)
            m_bf = rowbf[0:1, 0:512]
            r_bf = rowbf[0:1, 512:1024]
            bld.compq("ve", lambda e: e.tensor_copy(m_bf, m_row))
            rowt = bld.comp("ve", lambda e: e.tensor_copy(r_bf, r_row))
            bld.wait("te", "ve", rowt)
            bld.compq("te", lambda e: e.matmul(
                pb6[:, :], ones_row[:, :], m_bf, start=True, stop=True))
            bc_t = bld.comp("te", lambda e: e.matmul(
                ps2a[:, 0:512], ones_row[:, :], r_bf, start=True, stop=True))
            bld.wait("ve", "te", bc_t)
            for kc in range(KC):
                sl = kc % 2
                bld.wait("sy", "ve", tmp_use[sl])
                lt = bld.dma("sy", tmpf[sl][:, :],
                             src[kc * 128:(kc + 1) * 128, t0:t0 + 512],
                             f"ltmp{sl}")
                bld.wait("ve", f"ltmp{sl}", lt)
                bld.compq("ve", lambda e, sl=sl: e.tensor_tensor(
                    tmpf[sl][:, :], tmpf[sl][:, :], pb6[:, :], ALU.subtract))
                tmp_use[sl] = bld.comp("ve", lambda e, sl=sl, kc=kc, t0=t0:
                                       e.tensor_tensor(
                    xt[:, kc * TPC + t0: kc * TPC + t0 + 512],
                    tmpf[sl][:, :], ps2a[:, 0:512], ALU.mult))
        bld.join()

    def emit_proj_fm(n_oc, n_kc, w_dram, rhs_fn, epi_fn):
        bank_free = [("ve", 0), ("ve", 0)]
        buf_guard = [0, 0]
        for oc in range(n_oc):
            slot = oc % 2
            bld.wait("gp", "te", buf_guard[slot])
            wl = slab_dma("gp", wb_oc[slot], n_kc, 128, w_dram[oc], f"wb{slot}")
            bld.wait("te", f"wb{slot}", wl)
            for tt in range(2):
                bank = (pb4, pb5)[tt]
                fs, fv = bank_free[tt]
                bld.wait("te", fs, fv)
                mm_t = 0
                for kc in range(n_kc):
                    fn = lambda e, oc=oc, tt=tt, kc=kc, bank=bank, slot=slot: e.matmul(
                        bank[:, :], wb_oc[slot][:, kc * 128:(kc + 1) * 128],
                        rhs_fn(kc, tt), start=(kc == 0), stop=(kc == n_kc - 1))
                    if kc == n_kc - 1:
                        mm_t = bld.comp("te", fn)
                    else:
                        bld.compq("te", fn)
                bank_free[tt] = epi_fn(oc, tt, bank, mm_t)
            buf_guard[slot] = bld.sem["te"].n
        bld.join()

    def xt_rhs(kc, tt):
        return xt[:, kc * TPC + tt * 512: kc * TPC + tt * 512 + 512]

    def residual_epi(src, dst, bias_s):
        slot_guard = [0, 0]

        def epi(oc, tt, bank, mm_t, src=src, dst=dst, bias_s=bias_s):
            i = (oc * 2 + tt) % 2
            bld.wait("sy", f"rst{i}", slot_guard[i])
            lx = bld.dma("sy", tmpf[i][:, :],
                         src[oc * 128:(oc + 1) * 128, tt * 512: tt * 512 + 512],
                         f"ltmp{i}")
            bld.wait("ve", f"ltmp{i}", lx)
            bld.wait("ve", "te", mm_t)
            if bias_s is not None:
                vt = bld.comp("ve", lambda e, oc=oc, bank=bank, i=i:
                              e.scalar_tensor_tensor(
                    tmpf[i][:, :], bank[:, :], bias_s[:, oc:oc + 1],
                    tmpf[i][:, :], ALU.add, ALU.add))
            else:
                vt = bld.comp("ve", lambda e, bank=bank, i=i: e.tensor_tensor(
                    tmpf[i][:, :], bank[:, :], tmpf[i][:, :], ALU.add))
            bld.wait("act", "ve", vt)
            stt = bld.dma("act", dst[oc * 128:(oc + 1) * 128, tt * 512: tt * 512 + 512],
                          tmpf[i][:, :], f"rst{i}")
            slot_guard[i] = stt
            return ("ve", vt)
        return epi

    def sublayer_attn_a(src, dst):
        phase_ln(src)

        sg_guard = {}

        def epi(oc, tt, bank, mm_t):
            sl = (oc * 2 + tt) % 4
            bld.wait("act", "te", mm_t)
            bld.wait("act", f"qst{sl}", sg_guard.get(sl, 0))
            at = bld.comp("act", lambda e, oc=oc, bank=bank, sl=sl: e.add(
                stg[sl][:, :], bank[:, :], bqk_s[:, oc:oc + 1]))
            bld.wait("sy", "act", at)
            if oc < 4:
                for half in range(2):
                    h = oc * 2 + half
                    bld.dma("sy", qa[0:64, h * TPC + tt * 512: h * TPC + tt * 512 + 512],
                            stg[sl][half * 64:(half + 1) * 64, :], f"qst{sl}")
            else:
                ko = oc - 4
                for half in range(2):
                    h = ko * 2 + half
                    bld.dma("sy", kvaK[h * 64:(h + 1) * 64, tt * 512: tt * 512 + 512],
                            stg[sl][half * 64:(half + 1) * 64, :], f"qst{sl}")
            done = bld.skey(f"qst{sl}").n
            sg_guard[sl] = done
            bld.wait("ve", "act", at)
            vt = bld.comp("ve", lambda e: e.nop())
            return ("ve", vt)
        emit_proj_fm(8, KC, wqk_h, xt_rhs, epi)

        for half in range(2):
            slab_dma("gp", wsl[half], 16, 512,
                     wv_h[0, half * 16:(half + 1) * 16], f"wsl{half}")
        bld.wait("te", "wsl0", bld.skey("wsl0").n)
        bld.wait("te", "wsl1", bld.skey("wsl1").n)
        vfree = [("ve", 0), ("ve", 0)]
        st_g = [0, 0]
        for tc in range(8):
            bank = (pb4, pb5)[tc % 2]
            fs, fv = vfree[tc % 2]
            bld.wait("te", fs, fv)
            for kc in range(KC):
                bld.compq("te", lambda e, tc=tc, kc=kc, bank=bank: e.matmul(
                    bank[:, :],
                    xt[:, kc * TPC + tc * 128: kc * TPC + tc * 128 + 128],
                    wsl[kc // 16][:, (kc % 16) * 512:(kc % 16) * 512 + 512],
                    start=(kc == 0), stop=False))
            mm_t = bld.comp("te", lambda e, bank=bank: e.matmul(
                bank[:, :], ones_row[:, :], bv_s[:, :], start=False, stop=True))
            bld.wait("ve", "te", mm_t)
            bld.wait("ve", f"vst{tc % 2}", st_g[tc % 2])
            vt = bld.comp("ve", lambda e, tc=tc, bank=bank: e.tensor_copy(
                stg[tc % 2][:, :], bank[:, :]))
            vfree[tc % 2] = ("ve", vt)
            bld.wait("act", "ve", vt)
            st_g[tc % 2] = bld.dma(
                "act", kvaV[tc * 128:(tc + 1) * 128, :], stg[tc % 2][:, :],
                f"vst{tc % 2}")
        bld.join()

        bld.comp("gp", lambda e: e.memset(katile[64:65, :], 1.0))
        bld.comp("gp", lambda e: e.memset(
            vatile.rearrange("p (k c) -> p k c", k=KC)[:, :, 64:65], 1.0))

        bld.comp("gp", lambda e: e.collective_compute(
            "AllGather", ALU.bypass, ins=[kva_in[:]], outs=[kva_all[:, :]],
            replica_groups=[[0, 1, 2, 3], [4, 5, 6, 7]]))
        bld.join()

        kvaKg = kva_all[:, 0:512 * TPC].rearrange("r (d t) -> r d t", d=512)
        kvaVg = kva_all[:, 512 * TPC:2 * 512 * TPC].rearrange("r (t d) -> r t d", t=TPC)

        kv_guard = 0
        ol_free = 0
        pbt_g = [0]
        for h in range(8):
            bld.wait("sy", "te", kv_guard)
            for r in range(4):
                bld.dma("sy",
                        katile[0:64, :].rearrange("p (kk t) -> p kk t", kk=KC)[:, r::4, :],
                        kvaKg[r, h * 64:(h + 1) * 64, :].rearrange(
                            "d (s t) -> d s t", s=8), "kva")
                bld.dma("sy",
                        vatile.rearrange("p (kk c) -> p kk c", kk=KC)[:, r::4, 0:64],
                        kvaVg[r, :, h * 64:(h + 1) * 64].rearrange(
                            "(s p) d -> p s d", s=8), "kva")
            kvload = bld.skey("kva").n
            bld.wait("te", "kva", kvload)
            bld.wait("te", "ve", ol_free)
            bld.wait("te", "act", pbt_g[0])
            p1v = [0, 0]
            trp = 0
            for qb in range(8):
                for kt in range(8):
                    bank = (pb4, pb5)[kt % 2]
                    bld.wait("te", "ve", p1v[kt % 2])
                    s1 = bld.comp("te", lambda e, h=h, qb=qb, kt=kt, bank=bank: e.matmul(
                        bank[:, :], qa[0:64, h * TPC + qb * 128: h * TPC + qb * 128 + 128],
                        katile[0:64, kt * 512:(kt + 1) * 512], start=True, stop=True))
                    bld.wait("ve", "te", s1)
                    p1v[kt % 2] = bld.comp("ve", lambda e, kt=kt, bank=bank: e.reduce_max(
                        g8[:, kt:kt + 1], bank[:, :], axis=AX.X))
                bld.compq("ve", lambda e: e.reduce_max(
                    g8[:, 40:41], g8[:, 0:8], axis=AX.X))
                bld.wait("ve", "te", trp)
                nv = bld.comp("ve", lambda e: e.tensor_scalar(
                    wtm[:, 0:1], g8[:, 40:41], -1.0, None, ALU.mult))
                bld.wait("te", "ve", nv)
                trp = bld.comp("te", lambda e, qb=qb: e.transpose(
                    pbt[0:1, qb * 128:(qb + 1) * 128], wtm[:, 0:1], ident[:, :]))
            bld.wait("act", "te", trp)
            at = bld.comp("act", lambda e: e.copy(rowbf[0:1, 0:1024], pbt[0:1, 0:1024]))
            pbt_g[0] = at
            bld.wait("sy", "act", at)
            cq_l = bld.dma("sy", qa[64:65, h * TPC:(h + 1) * TPC],
                           rowbf[0:1, 0:1024], "cq")

            for qt in range(2):
                bld.wait("te", "cq", cq_l)
                bld.wait("te", "kva", kvload)
                bld.wait("te", "ve", ol_free)
                first = True
                p_t = [0, 0]
                av_t = 0
                for kk in range(KC):
                    sb = (ps2a, ps2b)[kk % 2][:, 0:512]
                    bld.wait("te", "act", p_t[kk % 2])
                    s_t = bld.comp("te", lambda e, kk=kk, h=h, qt=qt, sb=sb: e.matmul(
                        sb, katile[:, kk * 128:(kk + 1) * 128],
                        qa[:, h * TPC + qt * 512: h * TPC + qt * 512 + 512],
                        start=True, stop=True))
                    bld.wait("act", "te", s_t)
                    p_t[kk % 2] = bld.comp("act", lambda e, kk=kk, sb=sb: e.activation(
                        out=pbuf[kk % 2][:, 0:512], in_=sb, func=AF.Exp))
                    bld.wait("te", "act", p_t[kk % 2])
                    fnv = lambda e, kk=kk, first=first: e.matmul(
                        pb6[0:65, :], vatile[:, kk * 65: kk * 65 + 65],
                        pbuf[kk % 2][:, 0:512], start=first, stop=(kk == KC - 1))
                    if kk == KC - 1:
                        av_t = bld.comp("te", fnv)
                    else:
                        bld.compq("te", fnv)
                    first = False
                bld.wait("ve", "te", av_t)
                bld.wait("ve", "cq", cq_l)
                rl = rowbuf[0:1, 1536:2048]
                bld.compq("ve", lambda e, rl=rl: e.reciprocal(rl, pb6[64:65, :]))
                vt = bld.comp("ve", lambda e, rl=rl: e.tensor_scalar(
                    rowbf[0:1, 0:512], rl, ATTA_SCALE, None, ALU.mult))
                bld.wait("te", "ve", vt)
                bc_t = bld.comp("te", lambda e: e.matmul(
                    pb4[0:64, :], ones_row[0:1, 0:64], rowbf[0:1, 0:512],
                    start=True, stop=True))
                bld.wait("ve", "te", bc_t)
                bld.compq("ve", lambda e: e.tensor_copy(tmpf[2][0:64, :], pb4[0:64, :]))
                bld.wait("ve", "ctxst", bld.skey("ctxst").n)
                ct = bld.comp("ve", lambda e: e.tensor_tensor(
                    stg[3][0:64, :], pb6[0:64, :], tmpf[2][0:64, :], ALU.mult))
                ol_free = ct
                bld.wait("sy", "ve", ct)
                bld.dma("sy", ctxa[(h % 2) * 64:(h % 2) * 64 + 64,
                                   (h // 2) * TPC + qt * 512:
                                   (h // 2) * TPC + qt * 512 + 512],
                        stg[3][0:64, :], "ctxst")
            kv_guard = bld.sem["te"].n
        bld.join()

        def ctx_rhs(kc, tt):
            return ctxa[:, kc * TPC + tt * 512: kc * TPC + tt * 512 + 512]
        emit_proj_fm(KC, 4, wo_h, ctx_rhs, residual_epi(src, dst, bo_s))

    def sublayer_ffn(src, dst):
        phase_ln(src)
        tr_t = 0
        gate_free = 0
        for tc in range(8):
            bld.wait("te", "ve", gate_free)
            mm_t = 0
            for kc in range(KC):
                fn = lambda e, tc=tc, kc=kc: e.matmul(
                    pb4[:, 0:8], xt[:, kc * TPC + tc * 128: kc * TPC + tc * 128 + 128],
                    gw_s[:, kc * 8:(kc + 1) * 8], start=(kc == 0), stop=(kc == KC - 1))
                if kc == KC - 1:
                    mm_t = bld.comp("te", fn)
                else:
                    bld.compq("te", fn)
            bld.wait("ve", "te", mm_t)
            g = g8[:, 0:8]
            m1 = g8[:, 8:9]
            mk1 = g8[:, 9:17]
            gg2 = g8[:, 17:25]
            m2 = g8[:, 25:26]
            mk2 = g8[:, 26:34]
            dd = g8[:, 34:35]
            s1 = g8[:, 35:36]
            sn = g8[:, 36:37]
            wa_ = g8[:, 37:45]
            gate_free = bld.comp("ve", lambda e, g=g: e.tensor_copy(g, pb4[:, 0:8]))
            bld.compq("ve", lambda e, g=g, m1=m1: e.reduce_max(m1, g, axis=AX.X))
            bld.compq("ve", lambda e, g=g, m1=m1, mk1=mk1: e.tensor_scalar(
                mk1, g, m1, None, ALU.is_ge))
            bld.compq("ve", lambda e, g=g, mk1=mk1, gg2=gg2: e.scalar_tensor_tensor(
                gg2, mk1, MASK_NEG, g, ALU.mult, ALU.add))
            bld.compq("ve", lambda e, gg2=gg2, m2=m2: e.reduce_max(m2, gg2, axis=AX.X))
            bld.compq("ve", lambda e, gg2=gg2, m2=m2, mk2=mk2: e.tensor_scalar(
                mk2, gg2, m2, None, ALU.is_ge))
            vt = bld.comp("ve", lambda e, m1=m1, m2=m2, dd=dd: e.tensor_tensor(
                dd, m1, m2, ALU.subtract))
            bld.wait("act", "ve", vt)
            at = bld.comp("act", lambda e, dd=dd, s1=s1: e.activation(
                out=s1, in_=dd, func=AF.Sigmoid))
            bld.wait("ve", "act", at)
            bld.compq("ve", lambda e, s1=s1, sn=sn: e.tensor_scalar(
                sn, s1, -1.0, 1.0, ALU.mult, ALU.add))
            bld.compq("ve", lambda e, mk1=mk1, s1=s1, wa_=wa_: e.tensor_scalar(
                wa_, mk1, s1, None, ALU.mult))
            bld.wait("ve", "te", tr_t)
            vt = bld.comp("ve", lambda e, mk2=mk2, sn=sn, wa_=wa_: e.scalar_tensor_tensor(
                wtm[:, :], mk2, sn, wa_, ALU.mult, ALU.add))
            bld.wait("te", "ve", vt)
            tr_t = bld.comp("te", lambda e, tc=tc: e.transpose(
                pbt[0:8, tc * 128:(tc + 1) * 128], wtm[:, :], ident[:, :]))
        bld.wait("ve", "te", tr_t)
        wf_t = bld.comp("ve", lambda e: e.tensor_copy(wfm[:, :], pbt[0:8, 0:1024]))

        hh_free = [("ve", 0), ("ve", 0)]
        buf_guard = [0, 0]
        wtg = 0
        for rc in range(4):
            slot = rc % 2
            bld.wait("gp", "te", buf_guard[slot])
            wl = slab_dma("gp", wb_oc[slot], KC, 128, wa_h[rc], f"wb{slot}")
            bld.wait("te", f"wb{slot}", wl)
            for tt in range(2):
                bank = (pb4, pb5)[tt]
                fs, fv = hh_free[tt]
                bld.wait("te", fs, fv)
                mm_t = 0
                for kc in range(KC):
                    fn = lambda e, kc=kc, tt=tt, bank=bank, slot=slot: e.matmul(
                        bank[:, :], wb_oc[slot][:, kc * 128:(kc + 1) * 128],
                        xt_rhs(kc, tt), start=(kc == 0), stop=(kc == KC - 1))
                    if kc == KC - 1:
                        mm_t = bld.comp("te", fn)
                    else:
                        bld.compq("te", fn)
                bld.wait("te", "ve", wf_t)
                bld.wait("te", "ve", wtg)
                wb_t = bld.comp("te", lambda e, rc=rc, tt=tt: e.matmul(
                    pb6[:, :], indic[:, rc * 128:(rc + 1) * 128],
                    wfm[:, tt * 512: tt * 512 + 512], start=True, stop=True))
                bld.wait("ve", "te", wb_t)
                wtg = bld.comp("ve", lambda e: e.tensor_copy(stg[2][:, :], pb6[:, :]))
                hh = bld.comp("ve", lambda e, rc=rc, tt=tt, bank=bank: e.tensor_tensor(
                    hhat[:, rc * TPC + tt * 512: rc * TPC + tt * 512 + 512],
                    bank[:, :], stg[2][:, :], ALU.mult))
                hh_free[tt] = ("ve", hh)
            buf_guard[slot] = bld.sem["te"].n
        bld.join()

        def h_rhs(kc, tt):
            return hhat[:, kc * TPC + tt * 512: kc * TPC + tt * 512 + 512]
        emit_proj_fm(KC, 4, wbw_h, h_rhs, residual_epi(src, dst, None))

    def sublayer_attn_b(src, dst):
        phase_ln(src)
        bld.dma("sy", ropeC[:, :], ropeC_h[:, :], "cst")
        rl_t = bld.dma("sy", ropeS[:, :], ropeS_h[:, :], "cst")
        bld.wait("ve", "cst", rl_t)

        slab_guard = [0, 0, 0, 0]
        st_guard = {0: 0, 2: 0}
        rope_free = {0: ("ve", 0), 1: ("ve", 0)}
        for h in range(32):
            sA = (h % 2) * 2
            sB = sA + 1
            bld.wait("gp", "te", slab_guard[sA])
            wlA = slab_dma("gp", qk_slab(sA), KC, 128, wqkqk_h[2 * h], f"slab{sA}")
            bld.wait("gp", "te", slab_guard[sB])
            wlB = slab_dma("gp", qk_slab(sB), KC, 128, wqkqk_h[2 * h + 1], f"slab{sB}")
            bld.wait("te", f"slab{sA}", wlA)
            bld.wait("te", f"slab{sB}", wlB)
            for tt in range(2):
                pA = (pb4, pb6)[tt]
                pB = (pb5[:, 0:512], ps2a[:, 0:512])[tt]
                fs, fv = rope_free[tt]
                bld.wait("te", fs, fv)
                for kc in range(KC):
                    bld.compq("te", lambda e, kc=kc, sA=sA, tt=tt, pA=pA: e.matmul(
                        pA[:, 0:512], qk_slab(sA)[:, kc * 128:(kc + 1) * 128],
                        xt_rhs(kc, tt), start=(kc == 0), stop=(kc == KC - 1)))
                mm_t = 0
                for kc in range(KC):
                    fn = lambda e, kc=kc, sB=sB, tt=tt, pB=pB: e.matmul(
                        pB, qk_slab(sB)[:, kc * 128:(kc + 1) * 128],
                        xt_rhs(kc, tt), start=(kc == 0), stop=(kc == KC - 1))
                    if kc == KC - 1:
                        mm_t = bld.comp("te", fn)
                    else:
                        bld.compq("te", fn)
                t0 = tt * 512
                bA = bqkqk_s[:, 2 * h: 2 * h + 1]
                bB = bqkqk_s[:, 2 * h + 1: 2 * h + 2]
                sg0 = tt * 2
                bld.wait("ve", "te", mm_t)
                bld.wait("ve", f"ropest{sg0}", st_guard[sg0])
                bld.compq("ve", lambda e, pA=pA, bA=bA, t0=t0: e.scalar_tensor_tensor(
                    tmpf[0][:, :], pA[:, 0:512], bA, ropeC[:, t0:t0 + 512],
                    ALU.add, ALU.mult))
                bld.compq("ve", lambda e, pB=pB, bB=bB, t0=t0: e.scalar_tensor_tensor(
                    tmpf[1][:, :], pB, bB, ropeS[:, t0:t0 + 512],
                    ALU.add, ALU.mult))
                bld.compq("ve", lambda e, sg0=sg0: e.tensor_tensor(
                    stg[sg0][:, :], tmpf[0][:, :], tmpf[1][:, :], ALU.subtract))
                bld.compq("ve", lambda e, pB=pB, bB=bB, t0=t0: e.scalar_tensor_tensor(
                    tmpf[0][:, :], pB, bB, ropeC[:, t0:t0 + 512],
                    ALU.add, ALU.mult))
                bld.compq("ve", lambda e, pA=pA, bA=bA, t0=t0: e.scalar_tensor_tensor(
                    tmpf[1][:, :], pA[:, 0:512], bA, ropeS[:, t0:t0 + 512],
                    ALU.add, ALU.mult))
                vt = bld.comp("ve", lambda e, sg0=sg0: e.tensor_tensor(
                    stg[sg0 + 1][:, :], tmpf[0][:, :], tmpf[1][:, :], ALU.add))
                rope_free[tt] = ("ve", vt)
                bld.wait("act", "ve", vt)
                bld.dma("act", qbuf[h * 128: h * 128 + 64, t0:t0 + 512],
                        stg[sg0][0:64, :], f"ropest{sg0}")
                bld.dma("act", qbuf[h * 128 + 64: h * 128 + 128, t0:t0 + 512],
                        stg[sg0 + 1][0:64, :], f"ropest{sg0}")
                bld.dma("act", kvbK[h * 128: h * 128 + 64, t0:t0 + 512],
                        stg[sg0][64:128, :], f"ropest{sg0}")
                stq = bld.dma("act", kvbK[h * 128 + 64: h * 128 + 128, t0:t0 + 512],
                              stg[sg0 + 1][64:128, :], f"ropest{sg0}")
                st_guard[sg0] = stq
            slab_guard[sA] = bld.sem["te"].n
            slab_guard[sB] = slab_guard[sA]
        bld.join()

        vv_guard = 0
        vfree = [("ve", 0), ("ve", 0)]
        st_g = [0, 0]
        for ob in range(8):
            bld.wait("gp", "te", vv_guard)
            bld.dma("gp", bvv_s[:, :], bqkvv_h[ob:ob + 1, :], "bvv")
            for half in range(2):
                slab_dma("gp", wsl[half], 16, 512,
                         wqkvv_h[ob, half * 16:(half + 1) * 16], f"wsl{half}")
            bld.wait("te", "bvv", bld.skey("bvv").n)
            bld.wait("te", "wsl0", bld.skey("wsl0").n)
            bld.wait("te", "wsl1", bld.skey("wsl1").n)
            for tc in range(8):
                bank = (pb4, pb5)[tc % 2]
                fs, fv = vfree[tc % 2]
                bld.wait("te", fs, fv)
                for kc in range(KC):
                    bld.compq("te", lambda e, tc=tc, kc=kc, bank=bank: e.matmul(
                        bank[:, :],
                        xt[:, kc * TPC + tc * 128: kc * TPC + tc * 128 + 128],
                        wsl[kc // 16][:, (kc % 16) * 512:(kc % 16) * 512 + 512],
                        start=(kc == 0), stop=False))
                mm_t = bld.comp("te", lambda e, bank=bank: e.matmul(
                    bank[:, :], ones_row[:, :], bvv_s[:, :], start=False, stop=True))
                bld.wait("ve", "te", mm_t)
                bld.wait("ve", f"vst{tc % 2}", st_g[tc % 2])
                vt = bld.comp("ve", lambda e, tc=tc, bank=bank: e.tensor_copy(
                    stg[tc % 2][:, :], bank[:, :]))
                vfree[tc % 2] = ("ve", vt)
                bld.wait("act", "ve", vt)
                st_g[tc % 2] = bld.dma(
                    "act", kvbV[tc * 128:(tc + 1) * 128, ob * 512:(ob + 1) * 512],
                    stg[tc % 2][:, :], f"vst{tc % 2}")
            vv_guard = bld.sem["te"].n
        bld.join()

        bld.comp("gp", lambda e: e.collective_compute(
            "AllGather", ALU.bypass, ins=[kvb_in[:]], outs=[kvb_all[:, :]],
            replica_groups=[[0, 1, 2, 3], [4, 5, 6, 7]]))
        bld.dma("gp", maskS.rearrange("k (g c) -> k g c", g=8),
                maskb_h.rearrange("g k c -> k g c"), "cstg")
        bld.join()
        bld.dma("gp", wsl[1].rearrange("k (g c) -> k g c", g=8),
                maskbT_h.rearrange("g k c -> k g c"), "mskT")
        bld.join()

        kvbKg = kvb_all[:, 0:D * TPC].rearrange("r (d t) -> r d t", d=D)
        kvbVg = kvb_all[:, D * TPC:2 * D * TPC].rearrange("r (t d) -> r t d", t=TPC)

        kpairs = [wsl[0][:, 0:KC * 128], ctxa[:, 0:KC * 128]]
        vpairs = [wsl[0][:, KC * 128: 2 * KC * 128], hhat[:, 0:KC * 128]]

        def maskT_ap(p, qs):
            return wsl[1][:, (p * 2 + qs) * 1024:(p * 2 + qs + 1) * 1024]

        kvp_guard = [0, 0]
        kvl_t = [0, 0]
        ol_free = [("ve", 0), ("ve", 0)]
        l_guard = 0
        q_guard = 0
        p1v = [0, 0]
        pbt_g = 0
        negb = rowbf[0:1, 512:1024]

        def load_pair(h, p, ps):
            ng = 2 * p + 2
            bld.wait("sy", "te", kvp_guard[ps])
            for g in range(ng):
                bld.dma("sy",
                        kpairs[ps][:, g * 512:(g + 1) * 512].rearrange(
                            "p (r t) -> p r t", r=4),
                        kvbKg[:, h * 128:(h + 1) * 128,
                              g * 128:(g + 1) * 128].rearrange("r d t -> d r t"),
                        f"kvp{ps}")
                bld.dma("sy",
                        vpairs[ps][:, g * 512:(g + 1) * 512].rearrange(
                            "p (r c) -> p r c", r=4),
                        kvbVg[:, g * 128:(g + 1) * 128,
                              h * 128:(h + 1) * 128].rearrange("r t c -> t r c"),
                        f"kvp{ps}")
            kvl_t[ps] = bld.skey(f"kvp{ps}").n

        load_pair(0, 0, 0)
        for h in range(32):
            for p in range(4):
                ng = 2 * p + 2
                ps = (h * 4 + p) % 2
                kpair = kpairs[ps]
                vpair = vpairs[ps]
                bld.wait("sy", "te", q_guard)
                ql = bld.dma("sy", qtile[:, :],
                             qbuf[h * 128:(h + 1) * 128, p * 256:(p + 1) * 256], "qt")
                np_, nh = (p + 1, h) if p < 3 else (0, h + 1)
                if nh < 32:
                    load_pair(nh, np_, 1 - ps)
                kvl = kvl_t[ps]
                bld.wait("te", "qt", ql)
                bld.wait("te", f"kvp{ps}", kvl)
                bld.wait("te", "act", pbt_g)
                bld.wait("te", "ve", l_guard)
                nv = 0
                for qs in range(2):
                    for kt in range(ng):
                        bank = (ps2a, ps2b)[kt % 2]
                        bld.wait("te", "ve", p1v[kt % 2])
                        s1 = bld.comp("te", lambda e, qs=qs, kt=kt, bank=bank,
                                      kpair=kpair: e.matmul(
                            bank[:, 0:512], qtile[:, qs * 128:(qs + 1) * 128],
                            kpair[:, kt * 512:(kt + 1) * 512], start=True, stop=True))
                        bld.wait("ve", "te", s1)
                        if kt >= ng - 2:
                            j2 = kt - (ng - 2)
                            bld.compq("ve", lambda e, p=p, qs=qs, j2=j2, bank=bank:
                                      e.tensor_tensor(
                                tmpf[2][:, :], bank[:, 0:512],
                                maskT_ap(p, qs)[:, j2 * 512:(j2 + 1) * 512], ALU.add))
                            p1v[kt % 2] = bld.comp("ve", lambda e, kt=kt: e.reduce_max(
                                g8[:, kt:kt + 1], tmpf[2][:, :], axis=AX.X))
                        else:
                            p1v[kt % 2] = bld.comp("ve", lambda e, kt=kt, bank=bank:
                                                   e.reduce_max(
                                g8[:, kt:kt + 1], bank[:, 0:512], axis=AX.X))
                    bld.compq("ve", lambda e, ng=ng: e.reduce_max(
                        g8[:, 40:41], g8[:, 0:ng], axis=AX.X))
                    nv = bld.comp("ve", lambda e, qs=qs: e.tensor_scalar(
                        wtm[:, qs:qs + 1], g8[:, 40:41], -1.0, None, ALU.mult))
                bld.wait("te", "ve", nv)
                tr_t = 0
                for qs in range(2):
                    tr_t = bld.comp("te", lambda e, qs=qs: e.transpose(
                        pbt[0:1, qs * 128:(qs + 1) * 128], wtm[:, qs:qs + 1],
                        ident[:, :]))
                bld.wait("act", "te", tr_t)
                bld.compq("act", lambda e: e.copy(negb[0:1, 0:256], pbt[0:1, 0:256]))
                ng_t = bld.comp("act", lambda e: e.copy(
                    negb[0:1, 256:512], pbt[0:1, 0:256]))
                pbt_g = ng_t
                olb = (pb4, pb6)[p % 2]
                fs, fv = ol_free[p % 2]
                bld.wait("te", fs, fv)
                bld.wait("te", "ve", l_guard)
                bld.wait("te", "act", ng_t)
                first = True
                p_t = [0, 0]
                av_t = 0
                for g in range(ng):
                    slot = g % 2
                    sbank = (ps2a, ps2b)[slot]
                    bld.wait("te", "act", p_t[slot])
                    s_t = 0
                    for ki in range(4):
                        bld.compq("te", lambda e, g=g, ki=ki, sbank=sbank, kpair=kpair:
                                  e.matmul(
                            sbank[:, ki * 256:(ki + 1) * 256],
                            kpair[:, (g * 4 + ki) * 128:(g * 4 + ki + 1) * 128],
                            qtile[:, :], start=True, stop=False))
                        fn = lambda e, ki=ki, sbank=sbank: e.matmul(
                            sbank[:, ki * 256:(ki + 1) * 256],
                            ones_row[0:1, 0:128], negb[0:1, 0:256],
                            start=False, stop=True)
                        if ki == 3:
                            s_t = bld.comp("te", fn)
                        else:
                            bld.compq("te", fn)
                    if g >= 2 * p:
                        j = g - 2 * p
                        bld.wait("ve", "te", s_t)
                        mk_t = bld.comp("ve", lambda e, p=p, j=j, sbank=sbank:
                                        e.tensor_tensor(
                            sbank[:, :], sbank[:, :],
                            maskS[:, (p * 2 + j) * 1024:(p * 2 + j + 1) * 1024],
                            ALU.add))
                        bld.wait("act", "ve", mk_t)
                    else:
                        bld.wait("act", "te", s_t)
                    p_t[slot] = bld.comp("act", lambda e, slot=slot, sbank=sbank:
                                         e.activation(
                        out=pbuf[slot][:, :], in_=sbank[:, :], func=AF.Exp,
                        scale=ATTB_SCALE))
                    bld.wait("te", "act", p_t[slot])
                    for ki in range(4):
                        bld.compq("te", lambda e, g=g, ki=ki, slot=slot, olb=olb,
                                  vpair=vpair, first=(first and ki == 0),
                                  last=(g == ng - 1 and ki == 3): e.matmul(
                            olb[:, 0:256],
                            vpair[:, (g * 4 + ki) * 128:(g * 4 + ki + 1) * 128],
                            pbuf[slot][:, ki * 256:(ki + 1) * 256],
                            start=first, stop=last))
                        if ki == 0:
                            first = False
                    for g2 in range(2):
                        last = (g == ng - 1 and g2 == 1)
                        lfirst = (g == 0 and g2 == 0)
                        fn2 = lambda e, g2=g2, slot=slot, last=last, lfirst=lfirst: \
                            e.matmul(
                            pb5[0:1, 0:512], ones_col[:, :],
                            pbuf[slot][:, g2 * 512:(g2 + 1) * 512],
                            start=lfirst, stop=last)
                        if g2 == 1:
                            av_t = bld.comp("te", fn2)
                        else:
                            bld.compq("te", fn2)
                bld.wait("ve", "te", av_t)
                rl = rowbuf[0:1, 1536:1792]
                bld.compq("ve", lambda e: e.tensor_copy(
                    rowbuf[0:1, 1792:2048], pb5[0:1, 256:512]))
                bld.compq("ve", lambda e: e.tensor_tensor(
                    rowbuf[0:1, 1792:2048], pb5[0:1, 0:256], rowbuf[0:1, 1792:2048],
                    ALU.add))
                bld.compq("ve", lambda e, rl=rl: e.reciprocal(
                    rl, rowbuf[0:1, 1792:2048]))
                vt = bld.comp("ve", lambda e, rl=rl: e.tensor_copy(
                    rowbf[0:1, 0:256], rl))
                bld.wait("te", "ve", vt)
                bc_t = bld.comp("te", lambda e: e.matmul(
                    ps2a[:, 0:256], ones_row[:, :], rowbf[0:1, 0:256],
                    start=True, stop=True))
                bld.wait("ve", "te", bc_t)
                bld.compq("ve", lambda e: e.tensor_copy(
                    tmpf[2][:, 0:256], ps2a[:, 0:256]))
                ct = bld.comp("ve", lambda e, h=h, p=p, olb=olb: e.tensor_tensor(
                    xt[:, h * TPC + p * 256: h * TPC + p * 256 + 256],
                    olb[:, 0:256], tmpf[2][:, 0:256], ALU.mult))
                ol_free[p % 2] = ("ve", ct)
                l_guard = ct
                q_guard = bld.sem["te"].n
                kvp_guard[ps] = q_guard
        bld.join()

        emit_proj_fm(KC, KC, ow_h, xt_rhs, residual_epi(src, dst, obias_s))

    subs = [sublayer_attn_a, sublayer_ffn, sublayer_attn_b, sublayer_ffn]
    src = x_fm
    for i in range(N_SUB):
        dst = y_out if i == N_SUB - 1 else Xd
        subs[i](src, dst)
        src = Xd

    bld.join()
    for k in ("rst0", "rst1"):
        bld.wait("act", k, bld.skey(k).n)

    with nc.Block() as block:
        block.tensor(lambda e: [f(e) for f in bld.ops["te"]])
        block.vector(lambda e: [f(e) for f in bld.ops["ve"]])
        block.scalar(lambda e: [f(e) for f in bld.ops["act"]])
        block.sync(lambda e: [f(e) for f in bld.ops["sy"]])
        block.gpsimd(lambda e: [f(e) for f in bld.ops["gp"]])
    return nc


def kernel(**inputs):
    in_maps = _prep_inputs(inputs)
    nc = build_program()
    res = run_bass_kernel_spmd(nc, in_maps, core_ids=list(range(NC)))
    return _reassemble(res.results)


# version 08
# speedup vs baseline: 1.1040x; 1.0136x over previous
import numpy as np
import concourse.bass as bass
import concourse.mybir as mybir
from concourse.bass_utils import run_bass_kernel_spmd

F32 = mybir.dt.float32
BF16 = mybir.dt.bfloat16
AF = mybir.ActivationFunctionType
ALU = mybir.AluOpType
AX = mybir.AxisListType

B = 2
S = 4096
D = 4096
TPC = 1024
NC = 8
KC = 32
MASK_NEG = -1.0e9
ATTA_SCALE = 1.0 / float(np.sqrt(512.0))
ATTB_SCALE = 1.0 / float(np.sqrt(128.0))
CQ_FACTOR = 0.55
LN_EPS = 1e-5

N_SUB = 4



def _tile_oc_kc(w):
    Dk, O = w.shape
    return np.ascontiguousarray(
        w.reshape(Dk // 128, 128, O // 128, 128).transpose(2, 0, 1, 3))


def _tile_kc_ob(w, ob):
    Dk, O = w.shape
    return np.ascontiguousarray(
        w.reshape(Dk // 128, 128, O // ob, ob).transpose(2, 0, 1, 3))


def _prep_inputs(inp):
    x = np.asarray(inp["L0_attn_a_x"], np.float32)

    wqk = np.asarray(inp["L0_attn_a_Wqk_w"], np.float32)
    bqk = np.asarray(inp["L0_attn_a_Wqk_b"], np.float32)
    perm = np.empty(1024, np.int64)
    for h in range(8):
        for e in range(64):
            perm[h * 64 + e] = h * 128 + 2 * e
            perm[512 + h * 64 + e] = h * 128 + 2 * e + 1
    wqk_tiled = _tile_oc_kc(np.ascontiguousarray(wqk.T[:, perm]))
    bqk_dev = np.ascontiguousarray(bqk[perm].reshape(8, 128).T)

    wv = np.asarray(inp["L0_attn_a_Wv_w"], np.float32)
    bv = np.asarray(inp["L0_attn_a_Wv_b"], np.float32)
    wv_tiled = _tile_kc_ob(np.ascontiguousarray(wv.T), 512)
    bv_dev = bv.reshape(1, 512)

    wo = np.asarray(inp["L0_attn_a_Wo_w"], np.float32)
    bo = np.asarray(inp["L0_attn_a_Wo_b"], np.float32)
    wo_tiled = _tile_oc_kc(np.ascontiguousarray(wo.T))
    bo_dev = np.ascontiguousarray(bo.reshape(32, 128).T)

    qkv = np.asarray(inp["L0_attn_b_qkv_w"], np.float32)
    qkv_b = np.asarray(inp["L0_attn_b_qkv_b"], np.float32)
    qw = qkv[:, 0:4096].reshape(4096, 32, 128)
    kw = qkv[:, 4096:8192].reshape(4096, 32, 128)
    vw = qkv[:, 8192:12288]
    qb = qkv_b[0:4096].reshape(32, 128)
    kb = qkv_b[4096:8192].reshape(32, 128)
    vb = qkv_b[8192:12288]
    qk_cols = np.empty((4096, 64, 128), np.float32)
    qk_bias = np.empty((64, 128), np.float32)
    for h in range(32):
        qk_cols[:, 2 * h, 0:64] = qw[:, h, 0:64]
        qk_cols[:, 2 * h, 64:128] = kw[:, h, 0:64]
        qk_cols[:, 2 * h + 1, 0:64] = qw[:, h, 64:128]
        qk_cols[:, 2 * h + 1, 64:128] = kw[:, h, 64:128]
        qk_bias[2 * h, 0:64] = qb[h, 0:64]
        qk_bias[2 * h, 64:128] = kb[h, 0:64]
        qk_bias[2 * h + 1, 0:64] = qb[h, 64:128]
        qk_bias[2 * h + 1, 64:128] = kb[h, 64:128]
    wqkv_qk = _tile_oc_kc(qk_cols.reshape(4096, 64 * 128))
    bqkv_qk = np.ascontiguousarray(qk_bias.T)
    wqkv_v = _tile_kc_ob(np.ascontiguousarray(vw), 512)
    bqkv_v = np.ascontiguousarray(vb.reshape(8, 512))

    ow = np.asarray(inp["L0_attn_b_out_w"], np.float32)
    ob_ = np.asarray(inp["L0_attn_b_out_b"], np.float32)
    ow_tiled = _tile_oc_kc(ow)
    ob_dev = np.ascontiguousarray(ob_.reshape(32, 128).T)

    gw = np.asarray(inp["L0_ffn_gate_w"], np.float32)
    gw_tiled = _tile_kc_ob(np.ascontiguousarray(gw.T), 8)
    la = np.asarray(inp["L0_ffn_lora_A"], np.float32)
    lb = np.asarray(inp["L0_ffn_lora_B"], np.float32)
    a_cat = np.ascontiguousarray(np.concatenate([la[e].T for e in range(8)], axis=1))
    b_cat = np.ascontiguousarray(np.concatenate([lb[e].T for e in range(8)], axis=0))
    a_tiled = _tile_oc_kc(a_cat)
    b_tiled = _tile_oc_kc(b_cat)

    ident = np.eye(128, dtype=np.float32)
    indic = np.zeros((4, 8, 128), np.float32)
    for rc in range(4):
        for m in range(128):
            indic[rc, (rc * 128 + m) // 64, m] = 1.0

    inv_freq = 10000.0 ** (-(np.arange(64, dtype=np.float64) / 64.0))

    shared = {
        "wqk": wqk_tiled, "bqk": bqk_dev, "wv": wv_tiled, "bv": bv_dev,
        "wo": wo_tiled, "bo": bo_dev,
        "wqkv_qk": wqkv_qk, "bqkv_qk": bqkv_qk, "wqkv_v": wqkv_v, "bqkv_v": bqkv_v,
        "ow": ow_tiled, "obias": ob_dev,
        "gw": gw_tiled, "wa": a_tiled, "wbw": b_tiled,
        "ident": ident, "indic": indic,
    }

    in_maps = []
    for core in range(NC):
        b = core // 4
        c = core % 4
        blocks = [c + 4 * j for j in range(8)]
        pos = np.concatenate([np.arange(g * 128, (g + 1) * 128) for g in blocks])
        x_fm = np.ascontiguousarray(x[b, pos, :].T)

        ang = pos[:, None].astype(np.float64) * inv_freq[None, :]
        cos_t = np.cos(ang).astype(np.float32).T
        sin_t = np.sin(ang).astype(np.float32).T
        ropeC = np.ascontiguousarray(np.concatenate([cos_t, cos_t], axis=0))
        ropeS = np.ascontiguousarray(np.concatenate([sin_t, sin_t], axis=0))

        mask = np.zeros((4, 2, 128, 1024), np.float32)
        for p in range(4):
            for j in range(2):
                g = 2 * p + j
                for ki in range(4):
                    kappa = 4 * g + ki
                    gk = kappa * 128 + np.arange(128)
                    for sub in range(2):
                        slot = 2 * p + sub
                        gq = (c + 4 * slot) * 128 + np.arange(128)
                        m = np.where(gk[:, None] <= gq[None, :], 0.0, MASK_NEG)
                        mask[p, j, :, ki * 256 + sub * 128: ki * 256 + sub * 128 + 128] = m

        maskT = np.zeros((4, 2, 128, 1024), np.float32)
        for p in range(4):
            for sub in range(2):
                slot = 2 * p + sub
                gq = (c + 4 * slot) * 128 + np.arange(128)
                gk = 8 * p * 128 + np.arange(1024)
                maskT[p, sub] = np.where(gk[None, :] <= gq[:, None], 0.0, MASK_NEG)
        d = {"x_fm": x_fm, "ropeC": ropeC, "ropeS": ropeS,
             "maskb": mask.reshape(8, 128, 1024),
             "maskbT": maskT.reshape(8, 128, 1024)}
        d.update(shared)
        in_maps.append(d)
    return in_maps


def _reassemble(results):
    out = np.empty((B, S, D), np.float32)
    for core in range(NC):
        b = core // 4
        c = core % 4
        yt = results[core]["y"].T.reshape(8, 128, 4096)
        for j in range(8):
            g = c + 4 * j
            out[b, g * 128:(g + 1) * 128, :] = yt[j]
    return out



class Sem:
    def __init__(self, nc, name):
        self.h = nc.alloc_semaphore(name)
        self.n = 0

    def inc(self, d):
        self.n += d
        return self.n


class Bld:
    ENGS = ("te", "ve", "act", "sy", "gp")

    def __init__(self, nc):
        self.nc = nc
        self.ops = {k: [] for k in self.ENGS}
        self.sem = {k: Sem(nc, "s_" + k) for k in
                    ("te", "ve", "act", "sy", "gp", "st", "gpc")}

    def emit(self, eng, fn):
        self.ops[eng].append(fn)

    def wait(self, eng, sem_name, val):
        if val <= 0:
            return
        s = self.skey(sem_name)
        self.emit(eng, lambda e, s=s, v=val: e.wait_ge(s.h, v))

    def skey(self, key):
        if key not in self.sem:
            self.sem[key] = Sem(self.nc, "s_" + key)
        return self.sem[key]

    def dma(self, eng, out, in_, key):
        if eng == "act":
            self._selfedge("act")
        s = self.skey(key)
        self.emit(eng, lambda e, o=out, i=in_, s=s:
                  e.dma_start(out=o, in_=i).then_inc(s.h, 16))
        return s.inc(16)

    def _selfedge(self, eng):
        s = self.sem[eng]
        if s.n > 0:
            v = s.n
            self.emit(eng, lambda e, s=s, v=v: e.wait_ge(s.h, v))

    def comp(self, eng, fn, sem_name=None):
        if eng in ("ve", "act"):
            self._selfedge(eng)
        s = self.sem[sem_name or ("gpc" if eng == "gp" else eng)]
        self.emit(eng, lambda e, f=fn, s=s: f(e).then_inc(s.h, 1))
        return s.inc(1)

    def compq(self, eng, fn):
        if eng in ("ve", "act"):
            self.comp(eng, fn)
            return
        self.emit(eng, lambda e, f=fn: f(e))

    def join(self):
        targets = {k: v.n for k, v in self.sem.items()}
        for eng in self.ENGS:
            for sn, v in targets.items():
                if sn == eng:
                    continue
                self.wait(eng, sn, v)


def build_program():
    nc = bass.Bass("TRN2", target_bir_lowering=False, num_devices=NC)
    bld = Bld(nc)

    DT = nc.dram_tensor
    x_fm = DT("x_fm", [D, TPC], F32, kind="ExternalInput").ap()
    y_out = DT("y", [D, TPC], F32, kind="ExternalOutput").ap()
    wqk_h = DT("wqk", [8, KC, 128, 128], F32, kind="ExternalInput").ap()
    bqk_h = DT("bqk", [128, 8], F32, kind="ExternalInput").ap()
    wv_h = DT("wv", [1, KC, 128, 512], F32, kind="ExternalInput").ap()
    bv_h = DT("bv", [1, 512], F32, kind="ExternalInput").ap()
    wo_h = DT("wo", [KC, 4, 128, 128], F32, kind="ExternalInput").ap()
    bo_h = DT("bo", [128, KC], F32, kind="ExternalInput").ap()
    wqkqk_h = DT("wqkv_qk", [64, KC, 128, 128], F32, kind="ExternalInput").ap()
    bqkqk_h = DT("bqkv_qk", [128, 64], F32, kind="ExternalInput").ap()
    wqkvv_h = DT("wqkv_v", [8, KC, 128, 512], F32, kind="ExternalInput").ap()
    bqkvv_h = DT("bqkv_v", [8, 512], F32, kind="ExternalInput").ap()
    ow_h = DT("ow", [KC, KC, 128, 128], F32, kind="ExternalInput").ap()
    obias_h = DT("obias", [128, KC], F32, kind="ExternalInput").ap()
    gw_h = DT("gw", [1, KC, 128, 8], F32, kind="ExternalInput").ap()
    wa_h = DT("wa", [4, KC, 128, 128], F32, kind="ExternalInput").ap()
    wbw_h = DT("wbw", [KC, 4, 128, 128], F32, kind="ExternalInput").ap()
    ident_h = DT("ident", [128, 128], F32, kind="ExternalInput").ap()
    indic_h = DT("indic", [4, 8, 128], F32, kind="ExternalInput").ap()
    ropeC_h = DT("ropeC", [128, TPC], F32, kind="ExternalInput").ap()
    ropeS_h = DT("ropeS", [128, TPC], F32, kind="ExternalInput").ap()
    maskb_h = DT("maskb", [8, 128, 1024], F32, kind="ExternalInput").ap()
    maskbT_h = DT("maskbT", [8, 128, 1024], F32, kind="ExternalInput").ap()

    Xd = DT("Xd", [D, TPC], F32, kind="Internal").ap()
    qbuf = DT("qbuf", [D, TPC], BF16, kind="Internal").ap()
    kvbK_in = DT("kvbK_in", [D * TPC], BF16, kind="Internal").ap()
    kvbV_in = DT("kvbV_in", [8 * TPC * 512], BF16, kind="Internal").ap()
    kvbK_all = DT("kvbK_all", [4, D * TPC], BF16, kind="Internal").ap()
    kvbV_all = DT("kvbV_all", [32, TPC * 512], BF16, kind="Internal").ap()
    kva_in = DT("kva_in", [2 * 512 * TPC], BF16, kind="Internal").ap()
    kva_all = DT("kva_all", [4, 2 * 512 * TPC], BF16, kind="Internal").ap()

    kvbK = kvbK_in.rearrange("(d t) -> d t", d=D)
    kvbVo = kvbV_in.rearrange("(o t d) -> o t d", o=8, t=TPC)
    kvaK = kva_in[0:512 * TPC].rearrange("(d t) -> d t", d=512)
    kvaV = kva_in[512 * TPC:2 * 512 * TPC].rearrange("(t d) -> t d", t=TPC)

    A = lambda name, shape, dt: nc.alloc_sbuf_tensor(name, shape, dt).ap()
    xt = A("xt", [128, KC * TPC], BF16)
    off1, _ = nc.bump_sbuf(16384)
    qa = nc.alloc_sbuf_tensor_at("qa", [65, 8 * TPC], BF16, offset=off1).ap()
    maskS = nc.alloc_sbuf_tensor_at("maskS", [128, 8 * 1024], BF16, offset=off1).ap()
    off2, _ = nc.bump_sbuf(8192)
    ctxa = nc.alloc_sbuf_tensor_at("ctxa", [128, 4 * TPC], BF16, offset=off2).ap()
    ropeC = nc.alloc_sbuf_tensor_at("ropeCs", [128, TPC], F32, offset=off2).ap()
    ropeS = nc.alloc_sbuf_tensor_at("ropeSs", [128, TPC], F32, offset=off2 + 4096).ap()
    wsl = [A(f"wsl{i}", [128, 16 * 512], BF16) for i in range(2)]

    def qk_slab(i):
        return wsl[i // 2][:, (i % 2) * (KC * 128):(i % 2 + 1) * (KC * 128)]

    wb_oc = [A(f"wboc{i}", [128, KC * 128], BF16) for i in range(2)]
    hhat = A("hhat", [128, 4 * TPC], BF16)
    katile = A("katile", [65, KC * 128], BF16)
    vatile = A("vatile", [128, KC * 65], BF16)
    pbuf = [A(f"pbuf{i}", [128, 1024], BF16) for i in range(2)]
    kbufs = [A(f"kbuf{i}", [128, 512], BF16) for i in range(2)]
    vbufs = [A(f"vbuf{i}", [128, 512], BF16) for i in range(2)]
    qtile = A("qtile", [128, 256], BF16)
    stg = [A(f"stg{i}", [128, 512], BF16) for i in range(4)]
    tmpf = [A(f"tmpf{i}", [128, 512], F32) for i in range(3)]
    xb2 = [A(f"xb{i}", [128, 512], BF16) for i in range(2)]
    sq2 = [A(f"sq{i}", [128, 512], BF16) for i in range(2)]
    rowbuf = A("rowbuf", [1, 2048], F32)
    rowbf = A("rowbf", [1, 1024], BF16)
    g8 = A("g8", [128, 48], F32)
    wtm = A("wtm", [128, 8], BF16)
    wfm = A("wfm", [8, TPC], BF16)
    ident = A("idents", [128, 128], BF16)
    indic = A("indics", [8, 4 * 128], BF16)
    ones_col = A("ones_col", [128, 1], BF16)
    ones_row = A("ones_row", [1, 128], BF16)
    bqk_s = A("bqk_s", [128, 8], F32)
    bo_s = A("bo_s", [128, KC], F32)
    bqkqk_s = A("bqkqk_s", [128, 64], F32)
    obias_s = A("obias_s", [128, KC], F32)
    bv_s = A("bv_s", [1, 512], BF16)
    bvv_s = A("bvv_s", [1, 512], BF16)
    gw_s = A("gw_s", [128, KC * 8], BF16)

    P = lambda name, shape, dt=F32: nc.alloc_psum_tensor(name, shape, dt).ap()
    ps2a = P("ps2a", [128, 1024])
    ps2b = P("ps2b", [128, 1024])
    pb4 = P("pb4", [128, 512])
    pb5 = P("pb5", [128, 512])
    pb6 = P("pb6", [128, 512])
    pbt = P("pbt", [128, 1024], BF16)

    bld.dma("sy", bqk_s[:, :], bqk_h[:, :], "cst")
    bld.dma("sy", bo_s[:, :], bo_h[:, :], "cst")
    bld.dma("sy", bqkqk_s[:, :], bqkqk_h[:, :], "cst")
    bld.dma("sy", obias_s[:, :], obias_h[:, :], "cst")
    bld.dma("gp", ident[:, :], ident_h[:, :], "cstg")
    bld.dma("gp", indic.rearrange("e (r c) -> e r c", r=4),
            indic_h.rearrange("r e c -> e r c"), "cstg")
    bld.dma("gp", bv_s[:, :], bv_h[:, :], "cstg")
    bld.dma("gp", gw_s.rearrange("d (k e) -> d k e", k=KC),
            gw_h[0].rearrange("k d e -> d k e"), "cstg")
    bld.comp("gp", lambda e: e.memset(ones_col[:, :], 1.0))
    bld.comp("gp", lambda e: e.memset(ones_row[:, :], 1.0))
    bld.join()

    def slab_dma(eng, buf, n_kc, width, w_oc, key):
        return bld.dma(eng,
                       buf[:, 0:n_kc * width].rearrange("d (k o) -> d k o", k=n_kc),
                       w_oc.rearrange("k d o -> d k o"), key)

    def phase_ln(src):
        tmp_use = [0, 0]
        sqg = [0, 0]
        mmg = [0, 0]
        for tt in range(2):
            t0 = tt * 512
            sum_done = 0
            for kc in range(KC):
                sl = kc % 2
                bld.wait("sy", "ve", tmp_use[sl])
                lt = bld.dma("sy", tmpf[sl][:, :],
                             src[kc * 128:(kc + 1) * 128, t0:t0 + 512],
                             f"ltmp{sl}")
                bld.wait("ve", f"ltmp{sl}", lt)
                bld.wait("ve", "act", sqg[sl])
                bld.wait("ve", "te", mmg[sl])
                ct = bld.comp("ve", lambda e, sl=sl: e.tensor_copy(
                    xb2[sl][:, :], tmpf[sl][:, :]))
                tmp_use[sl] = ct
                bld.wait("act", "ve", ct)
                bld.wait("act", "te", mmg[sl])
                sq_t = bld.comp("act", lambda e, sl=sl: e.activation(
                    out=sq2[sl][:, :], in_=xb2[sl][:, :], func=AF.Square))
                sqg[sl] = sq_t
                bld.wait("te", "ve", ct)
                bld.compq("te", lambda e, sl=sl, kc=kc: e.matmul(
                    pb4[0:1, :], ones_col[:, :], xb2[sl][:, :],
                    start=(kc == 0), stop=(kc == KC - 1)))
                bld.wait("te", "act", sq_t)
                sum_done = bld.comp("te", lambda e, sl=sl, kc=kc: e.matmul(
                    pb5[0:1, :], ones_col[:, :], sq2[sl][:, :],
                    start=(kc == 0), stop=(kc == KC - 1)))
                mmg[sl] = sum_done
            m_row = rowbuf[0:1, 0:512]
            e2_row = rowbuf[0:1, 512:1024]
            var_row = rowbuf[0:1, 1024:1536]
            r_row = rowbuf[0:1, 1536:2048]
            bld.wait("ve", "te", sum_done)
            bld.compq("ve", lambda e: e.tensor_scalar(
                m_row, pb4[0:1, :], 1.0 / D, None, ALU.mult))
            bld.compq("ve", lambda e: e.tensor_scalar(
                e2_row, pb5[0:1, :], 1.0 / D, None, ALU.mult))
            bld.compq("ve", lambda e: e.tensor_tensor(var_row, m_row, m_row, ALU.mult))
            bld.compq("ve", lambda e: e.tensor_tensor(
                var_row, e2_row, var_row, ALU.subtract))
            bld.compq("ve", lambda e: e.tensor_scalar(
                var_row, var_row, LN_EPS, None, ALU.add))
            vt = bld.comp("ve", lambda e: e.reciprocal(r_row, var_row))
            bld.wait("act", "ve", vt)
            at = bld.comp("act", lambda e: e.activation(
                out=r_row, in_=r_row, func=AF.Sqrt))
            bld.wait("ve", "act", at)
            m_bf = rowbf[0:1, 0:512]
            r_bf = rowbf[0:1, 512:1024]
            bld.compq("ve", lambda e: e.tensor_copy(m_bf, m_row))
            rowt = bld.comp("ve", lambda e: e.tensor_copy(r_bf, r_row))
            bld.wait("te", "ve", rowt)
            bld.compq("te", lambda e: e.matmul(
                pb6[:, :], ones_row[:, :], m_bf, start=True, stop=True))
            bc_t = bld.comp("te", lambda e: e.matmul(
                ps2a[:, 0:512], ones_row[:, :], r_bf, start=True, stop=True))
            bld.wait("ve", "te", bc_t)
            for kc in range(KC):
                sl = kc % 2
                bld.wait("sy", "ve", tmp_use[sl])
                lt = bld.dma("sy", tmpf[sl][:, :],
                             src[kc * 128:(kc + 1) * 128, t0:t0 + 512],
                             f"ltmp{sl}")
                bld.wait("ve", f"ltmp{sl}", lt)
                bld.compq("ve", lambda e, sl=sl: e.tensor_tensor(
                    tmpf[sl][:, :], tmpf[sl][:, :], pb6[:, :], ALU.subtract))
                tmp_use[sl] = bld.comp("ve", lambda e, sl=sl, kc=kc, t0=t0:
                                       e.tensor_tensor(
                    xt[:, kc * TPC + t0: kc * TPC + t0 + 512],
                    tmpf[sl][:, :], ps2a[:, 0:512], ALU.mult))
        bld.join()

    def emit_proj_fm(n_oc, n_kc, w_dram, rhs_fn, epi_fn):
        bank_free = [("ve", 0), ("ve", 0)]
        buf_guard = [0, 0]
        for oc in range(n_oc):
            slot = oc % 2
            bld.wait("gp", "te", buf_guard[slot])
            wl = slab_dma("gp", wb_oc[slot], n_kc, 128, w_dram[oc], f"wb{slot}")
            bld.wait("te", f"wb{slot}", wl)
            for tt in range(2):
                bank = (pb4, pb5)[tt]
                fs, fv = bank_free[tt]
                bld.wait("te", fs, fv)
                mm_t = 0
                for kc in range(n_kc):
                    fn = lambda e, oc=oc, tt=tt, kc=kc, bank=bank, slot=slot: e.matmul(
                        bank[:, :], wb_oc[slot][:, kc * 128:(kc + 1) * 128],
                        rhs_fn(kc, tt), start=(kc == 0), stop=(kc == n_kc - 1))
                    if kc == n_kc - 1:
                        mm_t = bld.comp("te", fn)
                    else:
                        bld.compq("te", fn)
                bank_free[tt] = epi_fn(oc, tt, bank, mm_t)
            buf_guard[slot] = bld.sem["te"].n
        bld.join()

    def xt_rhs(kc, tt):
        return xt[:, kc * TPC + tt * 512: kc * TPC + tt * 512 + 512]

    def residual_epi(src, dst, bias_s):
        slot_guard = [0, 0]

        def epi(oc, tt, bank, mm_t, src=src, dst=dst, bias_s=bias_s):
            i = (oc * 2 + tt) % 2
            bld.wait("sy", f"rst{i}", slot_guard[i])
            lx = bld.dma("sy", tmpf[i][:, :],
                         src[oc * 128:(oc + 1) * 128, tt * 512: tt * 512 + 512],
                         f"ltmp{i}")
            bld.wait("ve", f"ltmp{i}", lx)
            bld.wait("ve", "te", mm_t)
            if bias_s is not None:
                vt = bld.comp("ve", lambda e, oc=oc, bank=bank, i=i:
                              e.scalar_tensor_tensor(
                    tmpf[i][:, :], bank[:, :], bias_s[:, oc:oc + 1],
                    tmpf[i][:, :], ALU.add, ALU.add))
            else:
                vt = bld.comp("ve", lambda e, bank=bank, i=i: e.tensor_tensor(
                    tmpf[i][:, :], bank[:, :], tmpf[i][:, :], ALU.add))
            bld.wait("act", "ve", vt)
            stt = bld.dma("act", dst[oc * 128:(oc + 1) * 128, tt * 512: tt * 512 + 512],
                          tmpf[i][:, :], f"rst{i}")
            slot_guard[i] = stt
            return ("ve", vt)
        return epi

    def sublayer_attn_a(src, dst):
        phase_ln(src)

        sg_guard = {}

        def epi(oc, tt, bank, mm_t):
            sl = (oc * 2 + tt) % 4
            bld.wait("act", "te", mm_t)
            bld.wait("act", f"qst{sl}", sg_guard.get(sl, 0))
            at = bld.comp("act", lambda e, oc=oc, bank=bank, sl=sl: e.add(
                stg[sl][:, :], bank[:, :], bqk_s[:, oc:oc + 1]))
            bld.wait("sy", "act", at)
            if oc < 4:
                for half in range(2):
                    h = oc * 2 + half
                    bld.dma("sy", qa[0:64, h * TPC + tt * 512: h * TPC + tt * 512 + 512],
                            stg[sl][half * 64:(half + 1) * 64, :], f"qst{sl}")
            else:
                ko = oc - 4
                for half in range(2):
                    h = ko * 2 + half
                    bld.dma("sy", kvaK[h * 64:(h + 1) * 64, tt * 512: tt * 512 + 512],
                            stg[sl][half * 64:(half + 1) * 64, :], f"qst{sl}")
            done = bld.skey(f"qst{sl}").n
            sg_guard[sl] = done
            bld.wait("ve", "act", at)
            vt = bld.comp("ve", lambda e: e.nop())
            return ("ve", vt)
        emit_proj_fm(8, KC, wqk_h, xt_rhs, epi)

        for half in range(2):
            slab_dma("gp", wsl[half], 16, 512,
                     wv_h[0, half * 16:(half + 1) * 16], f"wsl{half}")
        bld.wait("te", "wsl0", bld.skey("wsl0").n)
        bld.wait("te", "wsl1", bld.skey("wsl1").n)
        vfree = [("ve", 0), ("ve", 0)]
        st_g = [0, 0]
        for tc in range(8):
            bank = (pb4, pb5)[tc % 2]
            fs, fv = vfree[tc % 2]
            bld.wait("te", fs, fv)
            for kc in range(KC):
                bld.compq("te", lambda e, tc=tc, kc=kc, bank=bank: e.matmul(
                    bank[:, :],
                    xt[:, kc * TPC + tc * 128: kc * TPC + tc * 128 + 128],
                    wsl[kc // 16][:, (kc % 16) * 512:(kc % 16) * 512 + 512],
                    start=(kc == 0), stop=False))
            mm_t = bld.comp("te", lambda e, bank=bank: e.matmul(
                bank[:, :], ones_row[:, :], bv_s[:, :], start=False, stop=True))
            bld.wait("ve", "te", mm_t)
            bld.wait("ve", f"vst{tc % 2}", st_g[tc % 2])
            vt = bld.comp("ve", lambda e, tc=tc, bank=bank: e.tensor_copy(
                stg[tc % 2][:, :], bank[:, :]))
            vfree[tc % 2] = ("ve", vt)
            bld.wait("act", "ve", vt)
            st_g[tc % 2] = bld.dma(
                "act", kvaV[tc * 128:(tc + 1) * 128, :], stg[tc % 2][:, :],
                f"vst{tc % 2}")
        bld.join()

        bld.comp("gp", lambda e: e.memset(katile[64:65, :], 1.0))
        bld.comp("gp", lambda e: e.memset(
            vatile.rearrange("p (k c) -> p k c", k=KC)[:, :, 64:65], 1.0))

        bld.comp("gp", lambda e: e.collective_compute(
            "AllGather", ALU.bypass, ins=[kva_in[:]], outs=[kva_all[:, :]],
            replica_groups=[[0, 1, 2, 3], [4, 5, 6, 7]]))
        bld.join()

        kvaKg = kva_all[:, 0:512 * TPC].rearrange("r (d t) -> r d t", d=512)
        kvaVg = kva_all[:, 512 * TPC:2 * 512 * TPC].rearrange("r (t d) -> r t d", t=TPC)

        kv_guard = 0
        ol_free = 0
        pbt_g = [0]
        for h in range(8):
            bld.wait("sy", "te", kv_guard)
            for r in range(4):
                bld.dma("sy",
                        katile[0:64, :].rearrange("p (kk t) -> p kk t", kk=KC)[:, r::4, :],
                        kvaKg[r, h * 64:(h + 1) * 64, :].rearrange(
                            "d (s t) -> d s t", s=8), "kva")
                bld.dma("sy",
                        vatile.rearrange("p (kk c) -> p kk c", kk=KC)[:, r::4, 0:64],
                        kvaVg[r, :, h * 64:(h + 1) * 64].rearrange(
                            "(s p) d -> p s d", s=8), "kva")
            kvload = bld.skey("kva").n
            bld.wait("te", "kva", kvload)
            bld.wait("te", "ve", ol_free)
            bld.wait("te", "act", pbt_g[0])
            p1v = [0, 0]
            trp = 0
            for qb in range(8):
                for kt in range(8):
                    bank = (pb4, pb5)[kt % 2]
                    bld.wait("te", "ve", p1v[kt % 2])
                    s1 = bld.comp("te", lambda e, h=h, qb=qb, kt=kt, bank=bank: e.matmul(
                        bank[:, :], qa[0:64, h * TPC + qb * 128: h * TPC + qb * 128 + 128],
                        katile[0:64, kt * 512:(kt + 1) * 512], start=True, stop=True))
                    bld.wait("ve", "te", s1)
                    p1v[kt % 2] = bld.comp("ve", lambda e, kt=kt, bank=bank: e.reduce_max(
                        g8[:, kt:kt + 1], bank[:, :], axis=AX.X))
                bld.compq("ve", lambda e: e.reduce_max(
                    g8[:, 40:41], g8[:, 0:8], axis=AX.X))
                bld.wait("ve", "te", trp)
                nv = bld.comp("ve", lambda e: e.tensor_scalar(
                    wtm[:, 0:1], g8[:, 40:41], -1.0, None, ALU.mult))
                bld.wait("te", "ve", nv)
                trp = bld.comp("te", lambda e, qb=qb: e.transpose(
                    pbt[0:1, qb * 128:(qb + 1) * 128], wtm[:, 0:1], ident[:, :]))
            bld.wait("act", "te", trp)
            at = bld.comp("act", lambda e: e.copy(rowbf[0:1, 0:1024], pbt[0:1, 0:1024]))
            pbt_g[0] = at
            bld.wait("sy", "act", at)
            cq_l = bld.dma("sy", qa[64:65, h * TPC:(h + 1) * TPC],
                           rowbf[0:1, 0:1024], "cq")

            for qt in range(2):
                bld.wait("te", "cq", cq_l)
                bld.wait("te", "kva", kvload)
                bld.wait("te", "ve", ol_free)
                first = True
                p_t = [0, 0]
                av_t = 0
                for kk in range(KC):
                    sb = (ps2a, ps2b)[kk % 2][:, 0:512]
                    bld.wait("te", "act", p_t[kk % 2])
                    s_t = bld.comp("te", lambda e, kk=kk, h=h, qt=qt, sb=sb: e.matmul(
                        sb, katile[:, kk * 128:(kk + 1) * 128],
                        qa[:, h * TPC + qt * 512: h * TPC + qt * 512 + 512],
                        start=True, stop=True))
                    bld.wait("act", "te", s_t)
                    p_t[kk % 2] = bld.comp("act", lambda e, kk=kk, sb=sb: e.activation(
                        out=pbuf[kk % 2][:, 0:512], in_=sb, func=AF.Exp))
                    bld.wait("te", "act", p_t[kk % 2])
                    fnv = lambda e, kk=kk, first=first: e.matmul(
                        pb6[0:65, :], vatile[:, kk * 65: kk * 65 + 65],
                        pbuf[kk % 2][:, 0:512], start=first, stop=(kk == KC - 1))
                    if kk == KC - 1:
                        av_t = bld.comp("te", fnv)
                    else:
                        bld.compq("te", fnv)
                    first = False
                bld.wait("ve", "te", av_t)
                bld.wait("ve", "cq", cq_l)
                rl = rowbuf[0:1, 1536:2048]
                bld.compq("ve", lambda e, rl=rl: e.reciprocal(rl, pb6[64:65, :]))
                vt = bld.comp("ve", lambda e, rl=rl: e.tensor_scalar(
                    rowbf[0:1, 0:512], rl, ATTA_SCALE, None, ALU.mult))
                bld.wait("te", "ve", vt)
                bc_t = bld.comp("te", lambda e: e.matmul(
                    pb4[0:64, :], ones_row[0:1, 0:64], rowbf[0:1, 0:512],
                    start=True, stop=True))
                bld.wait("ve", "te", bc_t)
                bld.compq("ve", lambda e: e.tensor_copy(tmpf[2][0:64, :], pb4[0:64, :]))
                bld.wait("ve", "ctxst", bld.skey("ctxst").n)
                ct = bld.comp("ve", lambda e: e.tensor_tensor(
                    stg[3][0:64, :], pb6[0:64, :], tmpf[2][0:64, :], ALU.mult))
                ol_free = ct
                bld.wait("sy", "ve", ct)
                bld.dma("sy", ctxa[(h % 2) * 64:(h % 2) * 64 + 64,
                                   (h // 2) * TPC + qt * 512:
                                   (h // 2) * TPC + qt * 512 + 512],
                        stg[3][0:64, :], "ctxst")
            kv_guard = bld.sem["te"].n
        bld.join()

        def ctx_rhs(kc, tt):
            return ctxa[:, kc * TPC + tt * 512: kc * TPC + tt * 512 + 512]
        emit_proj_fm(KC, 4, wo_h, ctx_rhs, residual_epi(src, dst, bo_s))

    def sublayer_ffn(src, dst):
        phase_ln(src)
        tr_t = 0
        gate_free = 0
        for tc in range(8):
            bld.wait("te", "ve", gate_free)
            mm_t = 0
            for kc in range(KC):
                fn = lambda e, tc=tc, kc=kc: e.matmul(
                    pb4[:, 0:8], xt[:, kc * TPC + tc * 128: kc * TPC + tc * 128 + 128],
                    gw_s[:, kc * 8:(kc + 1) * 8], start=(kc == 0), stop=(kc == KC - 1))
                if kc == KC - 1:
                    mm_t = bld.comp("te", fn)
                else:
                    bld.compq("te", fn)
            bld.wait("ve", "te", mm_t)
            g = g8[:, 0:8]
            m1 = g8[:, 8:9]
            mk1 = g8[:, 9:17]
            gg2 = g8[:, 17:25]
            m2 = g8[:, 25:26]
            mk2 = g8[:, 26:34]
            dd = g8[:, 34:35]
            s1 = g8[:, 35:36]
            sn = g8[:, 36:37]
            wa_ = g8[:, 37:45]
            gate_free = bld.comp("ve", lambda e, g=g: e.tensor_copy(g, pb4[:, 0:8]))
            bld.compq("ve", lambda e, g=g, m1=m1: e.reduce_max(m1, g, axis=AX.X))
            bld.compq("ve", lambda e, g=g, m1=m1, mk1=mk1: e.tensor_scalar(
                mk1, g, m1, None, ALU.is_ge))
            bld.compq("ve", lambda e, g=g, mk1=mk1, gg2=gg2: e.scalar_tensor_tensor(
                gg2, mk1, MASK_NEG, g, ALU.mult, ALU.add))
            bld.compq("ve", lambda e, gg2=gg2, m2=m2: e.reduce_max(m2, gg2, axis=AX.X))
            bld.compq("ve", lambda e, gg2=gg2, m2=m2, mk2=mk2: e.tensor_scalar(
                mk2, gg2, m2, None, ALU.is_ge))
            vt = bld.comp("ve", lambda e, m1=m1, m2=m2, dd=dd: e.tensor_tensor(
                dd, m1, m2, ALU.subtract))
            bld.wait("act", "ve", vt)
            at = bld.comp("act", lambda e, dd=dd, s1=s1: e.activation(
                out=s1, in_=dd, func=AF.Sigmoid))
            bld.wait("ve", "act", at)
            bld.compq("ve", lambda e, s1=s1, sn=sn: e.tensor_scalar(
                sn, s1, -1.0, 1.0, ALU.mult, ALU.add))
            bld.compq("ve", lambda e, mk1=mk1, s1=s1, wa_=wa_: e.tensor_scalar(
                wa_, mk1, s1, None, ALU.mult))
            bld.wait("ve", "te", tr_t)
            vt = bld.comp("ve", lambda e, mk2=mk2, sn=sn, wa_=wa_: e.scalar_tensor_tensor(
                wtm[:, :], mk2, sn, wa_, ALU.mult, ALU.add))
            bld.wait("te", "ve", vt)
            tr_t = bld.comp("te", lambda e, tc=tc: e.transpose(
                pbt[0:8, tc * 128:(tc + 1) * 128], wtm[:, :], ident[:, :]))
        bld.wait("ve", "te", tr_t)
        wf_t = bld.comp("ve", lambda e: e.tensor_copy(wfm[:, :], pbt[0:8, 0:1024]))

        hh_free = [("ve", 0), ("ve", 0)]
        buf_guard = [0, 0]
        wtg = 0
        for rc in range(4):
            slot = rc % 2
            bld.wait("gp", "te", buf_guard[slot])
            wl = slab_dma("gp", wb_oc[slot], KC, 128, wa_h[rc], f"wb{slot}")
            bld.wait("te", f"wb{slot}", wl)
            for tt in range(2):
                bank = (pb4, pb5)[tt]
                fs, fv = hh_free[tt]
                bld.wait("te", fs, fv)
                mm_t = 0
                for kc in range(KC):
                    fn = lambda e, kc=kc, tt=tt, bank=bank, slot=slot: e.matmul(
                        bank[:, :], wb_oc[slot][:, kc * 128:(kc + 1) * 128],
                        xt_rhs(kc, tt), start=(kc == 0), stop=(kc == KC - 1))
                    if kc == KC - 1:
                        mm_t = bld.comp("te", fn)
                    else:
                        bld.compq("te", fn)
                bld.wait("te", "ve", wf_t)
                bld.wait("te", "ve", wtg)
                wb_t = bld.comp("te", lambda e, rc=rc, tt=tt: e.matmul(
                    pb6[:, :], indic[:, rc * 128:(rc + 1) * 128],
                    wfm[:, tt * 512: tt * 512 + 512], start=True, stop=True))
                bld.wait("ve", "te", wb_t)
                wtg = bld.comp("ve", lambda e: e.tensor_copy(stg[2][:, :], pb6[:, :]))
                hh = bld.comp("ve", lambda e, rc=rc, tt=tt, bank=bank: e.tensor_tensor(
                    hhat[:, rc * TPC + tt * 512: rc * TPC + tt * 512 + 512],
                    bank[:, :], stg[2][:, :], ALU.mult))
                hh_free[tt] = ("ve", hh)
            buf_guard[slot] = bld.sem["te"].n
        bld.join()

        def h_rhs(kc, tt):
            return hhat[:, kc * TPC + tt * 512: kc * TPC + tt * 512 + 512]
        emit_proj_fm(KC, 4, wbw_h, h_rhs, residual_epi(src, dst, None))

    def sublayer_attn_b(src, dst):
        phase_ln(src)
        bld.dma("sy", ropeC[:, :], ropeC_h[:, :], "cst")
        rl_t = bld.dma("sy", ropeS[:, :], ropeS_h[:, :], "cst")
        bld.wait("ve", "cst", rl_t)

        slab_guard = [0, 0, 0, 0]
        st_guard = {0: 0, 2: 0}
        rope_free = {0: ("ve", 0), 1: ("ve", 0)}
        for h in range(32):
            sA = (h % 2) * 2
            sB = sA + 1
            bld.wait("gp", "te", slab_guard[sA])
            wlA = slab_dma("gp", qk_slab(sA), KC, 128, wqkqk_h[2 * h], f"slab{sA}")
            bld.wait("gp", "te", slab_guard[sB])
            wlB = slab_dma("gp", qk_slab(sB), KC, 128, wqkqk_h[2 * h + 1], f"slab{sB}")
            bld.wait("te", f"slab{sA}", wlA)
            bld.wait("te", f"slab{sB}", wlB)
            for tt in range(2):
                pA = (pb4, pb6)[tt]
                pB = (pb5[:, 0:512], ps2a[:, 0:512])[tt]
                fs, fv = rope_free[tt]
                bld.wait("te", fs, fv)
                for kc in range(KC):
                    bld.compq("te", lambda e, kc=kc, sA=sA, tt=tt, pA=pA: e.matmul(
                        pA[:, 0:512], qk_slab(sA)[:, kc * 128:(kc + 1) * 128],
                        xt_rhs(kc, tt), start=(kc == 0), stop=(kc == KC - 1)))
                mm_t = 0
                for kc in range(KC):
                    fn = lambda e, kc=kc, sB=sB, tt=tt, pB=pB: e.matmul(
                        pB, qk_slab(sB)[:, kc * 128:(kc + 1) * 128],
                        xt_rhs(kc, tt), start=(kc == 0), stop=(kc == KC - 1))
                    if kc == KC - 1:
                        mm_t = bld.comp("te", fn)
                    else:
                        bld.compq("te", fn)
                t0 = tt * 512
                bA = bqkqk_s[:, 2 * h: 2 * h + 1]
                bB = bqkqk_s[:, 2 * h + 1: 2 * h + 2]
                sg0 = tt * 2
                bld.wait("ve", "te", mm_t)
                bld.wait("ve", f"ropest{sg0}", st_guard[sg0])
                bld.compq("ve", lambda e, pA=pA, bA=bA, t0=t0: e.scalar_tensor_tensor(
                    tmpf[0][:, :], pA[:, 0:512], bA, ropeC[:, t0:t0 + 512],
                    ALU.add, ALU.mult))
                bld.compq("ve", lambda e, pB=pB, bB=bB, t0=t0: e.scalar_tensor_tensor(
                    tmpf[1][:, :], pB, bB, ropeS[:, t0:t0 + 512],
                    ALU.add, ALU.mult))
                bld.compq("ve", lambda e, sg0=sg0: e.tensor_tensor(
                    stg[sg0][:, :], tmpf[0][:, :], tmpf[1][:, :], ALU.subtract))
                bld.compq("ve", lambda e, pB=pB, bB=bB, t0=t0: e.scalar_tensor_tensor(
                    tmpf[0][:, :], pB, bB, ropeC[:, t0:t0 + 512],
                    ALU.add, ALU.mult))
                bld.compq("ve", lambda e, pA=pA, bA=bA, t0=t0: e.scalar_tensor_tensor(
                    tmpf[1][:, :], pA[:, 0:512], bA, ropeS[:, t0:t0 + 512],
                    ALU.add, ALU.mult))
                vt = bld.comp("ve", lambda e, sg0=sg0: e.tensor_tensor(
                    stg[sg0 + 1][:, :], tmpf[0][:, :], tmpf[1][:, :], ALU.add))
                rope_free[tt] = ("ve", vt)
                bld.wait("act", "ve", vt)
                bld.dma("act", qbuf[h * 128: h * 128 + 64, t0:t0 + 512],
                        stg[sg0][0:64, :], f"ropest{sg0}")
                bld.dma("act", qbuf[h * 128 + 64: h * 128 + 128, t0:t0 + 512],
                        stg[sg0 + 1][0:64, :], f"ropest{sg0}")
                bld.dma("act", kvbK[h * 128: h * 128 + 64, t0:t0 + 512],
                        stg[sg0][64:128, :], f"ropest{sg0}")
                stq = bld.dma("act", kvbK[h * 128 + 64: h * 128 + 128, t0:t0 + 512],
                              stg[sg0 + 1][64:128, :], f"ropest{sg0}")
                st_guard[sg0] = stq
            slab_guard[sA] = bld.sem["te"].n
            slab_guard[sB] = slab_guard[sA]
        bld.join()

        bld.comp("gp", lambda e: e.collective_compute(
            "AllGather", ALU.bypass, ins=[kvbK_in[:]], outs=[kvbK_all[:, :]],
            replica_groups=[[0, 1, 2, 3], [4, 5, 6, 7]]))

        vv_guard = 0
        vfree = [("ve", 0), ("ve", 0)]
        st_g = [0, 0]
        for ob in range(8):
            bld.wait("gp", "te", vv_guard)
            bld.dma("gp", bvv_s[:, :], bqkvv_h[ob:ob + 1, :], "bvv")
            for half in range(2):
                slab_dma("gp", wsl[half], 16, 512,
                         wqkvv_h[ob, half * 16:(half + 1) * 16], f"wsl{half}")
            bld.wait("te", "bvv", bld.skey("bvv").n)
            bld.wait("te", "wsl0", bld.skey("wsl0").n)
            bld.wait("te", "wsl1", bld.skey("wsl1").n)
            for tc in range(8):
                bank = (pb4, pb5)[tc % 2]
                fs, fv = vfree[tc % 2]
                bld.wait("te", fs, fv)
                for kc in range(KC):
                    bld.compq("te", lambda e, tc=tc, kc=kc, bank=bank: e.matmul(
                        bank[:, :],
                        xt[:, kc * TPC + tc * 128: kc * TPC + tc * 128 + 128],
                        wsl[kc // 16][:, (kc % 16) * 512:(kc % 16) * 512 + 512],
                        start=(kc == 0), stop=False))
                mm_t = bld.comp("te", lambda e, bank=bank: e.matmul(
                    bank[:, :], ones_row[:, :], bvv_s[:, :], start=False, stop=True))
                bld.wait("ve", "te", mm_t)
                bld.wait("ve", f"vst{tc % 2}", st_g[tc % 2])
                vt = bld.comp("ve", lambda e, tc=tc, bank=bank: e.tensor_copy(
                    stg[tc % 2][:, :], bank[:, :]))
                vfree[tc % 2] = ("ve", vt)
                bld.wait("act", "ve", vt)
                st_g[tc % 2] = bld.dma(
                    "act", kvbVo[ob, tc * 128:(tc + 1) * 128, :],
                    stg[tc % 2][:, :], f"vst{tc % 2}")
            vv_guard = bld.sem["te"].n
            bld.wait("gp", "vst0", st_g[0])
            bld.wait("gp", "vst1", st_g[1])
            bld.comp("gp", lambda e, ob=ob: e.collective_compute(
                "AllGather", ALU.bypass,
                ins=[kvbV_in[ob * TPC * 512:(ob + 1) * TPC * 512]],
                outs=[kvbV_all[ob * 4:(ob + 1) * 4, :]],
                replica_groups=[[0, 1, 2, 3], [4, 5, 6, 7]]))
        bld.join()

        bld.dma("gp", maskS.rearrange("k (g c) -> k g c", g=8),
                maskb_h.rearrange("g k c -> k g c"), "cstg")
        bld.join()
        bld.dma("gp", wsl[1].rearrange("k (g c) -> k g c", g=8),
                maskbT_h.rearrange("g k c -> k g c"), "mskT")
        bld.join()

        kvbKg = kvbK_all.rearrange("r (d t) -> r d t", d=D)

        kpairs = [wsl[0][:, 0:KC * 128], ctxa[:, 0:KC * 128]]
        vpairs = [wsl[0][:, KC * 128: 2 * KC * 128], hhat[:, 0:KC * 128]]

        def maskT_ap(p, qs):
            return wsl[1][:, (p * 2 + qs) * 1024:(p * 2 + qs + 1) * 1024]

        kvp_guard = [0, 0]
        kvl_t = [0, 0]
        ol_free = [("ve", 0), ("ve", 0)]
        l_guard = 0
        q_guard = 0
        p1v = [0, 0]
        pbt_g = 0
        negb = rowbf[0:1, 512:1024]

        def load_pair(h, p, ps):
            ng = 2 * p + 2
            bld.wait("sy", "te", kvp_guard[ps])
            for g in range(ng):
                bld.dma("sy",
                        kpairs[ps][:, g * 512:(g + 1) * 512].rearrange(
                            "p (r t) -> p r t", r=4),
                        kvbKg[:, h * 128:(h + 1) * 128,
                              g * 128:(g + 1) * 128].rearrange("r d t -> d r t"),
                        f"kvp{ps}")
                bld.dma("sy",
                        vpairs[ps][:, g * 512:(g + 1) * 512].rearrange(
                            "p (r c) -> p r c", r=4),
                        kvbV_all[(h // 4) * 4:(h // 4) * 4 + 4, :].rearrange(
                            "r (t d) -> r t d", t=TPC)[
                            :, g * 128:(g + 1) * 128,
                            (h % 4) * 128:(h % 4) * 128 + 128].rearrange(
                            "r t c -> t r c"),
                        f"kvp{ps}")
            kvl_t[ps] = bld.skey(f"kvp{ps}").n

        load_pair(0, 0, 0)
        for h in range(32):
            for p in range(4):
                ng = 2 * p + 2
                ps = (h * 4 + p) % 2
                kpair = kpairs[ps]
                vpair = vpairs[ps]
                bld.wait("sy", "te", q_guard)
                ql = bld.dma("sy", qtile[:, :],
                             qbuf[h * 128:(h + 1) * 128, p * 256:(p + 1) * 256], "qt")
                np_, nh = (p + 1, h) if p < 3 else (0, h + 1)
                if nh < 32:
                    load_pair(nh, np_, 1 - ps)
                kvl = kvl_t[ps]
                bld.wait("te", "qt", ql)
                bld.wait("te", f"kvp{ps}", kvl)
                bld.wait("te", "act", pbt_g)
                bld.wait("te", "ve", l_guard)
                nv = 0
                for qs in range(2):
                    for kt in range(ng):
                        bank = (ps2a, ps2b)[kt % 2]
                        bld.wait("te", "ve", p1v[kt % 2])
                        s1 = bld.comp("te", lambda e, qs=qs, kt=kt, bank=bank,
                                      kpair=kpair: e.matmul(
                            bank[:, 0:512], qtile[:, qs * 128:(qs + 1) * 128],
                            kpair[:, kt * 512:(kt + 1) * 512], start=True, stop=True))
                        bld.wait("ve", "te", s1)
                        if kt >= ng - 2:
                            j2 = kt - (ng - 2)
                            bld.compq("ve", lambda e, p=p, qs=qs, j2=j2, bank=bank:
                                      e.tensor_tensor(
                                tmpf[2][:, :], bank[:, 0:512],
                                maskT_ap(p, qs)[:, j2 * 512:(j2 + 1) * 512], ALU.add))
                            p1v[kt % 2] = bld.comp("ve", lambda e, kt=kt: e.reduce_max(
                                g8[:, kt:kt + 1], tmpf[2][:, :], axis=AX.X))
                        else:
                            p1v[kt % 2] = bld.comp("ve", lambda e, kt=kt, bank=bank:
                                                   e.reduce_max(
                                g8[:, kt:kt + 1], bank[:, 0:512], axis=AX.X))
                    bld.compq("ve", lambda e, ng=ng: e.reduce_max(
                        g8[:, 40:41], g8[:, 0:ng], axis=AX.X))
                    nv = bld.comp("ve", lambda e, qs=qs: e.tensor_scalar(
                        wtm[:, qs:qs + 1], g8[:, 40:41], -1.0, None, ALU.mult))
                bld.wait("te", "ve", nv)
                tr_t = 0
                for qs in range(2):
                    tr_t = bld.comp("te", lambda e, qs=qs: e.transpose(
                        pbt[0:1, qs * 128:(qs + 1) * 128], wtm[:, qs:qs + 1],
                        ident[:, :]))
                bld.wait("act", "te", tr_t)
                bld.compq("act", lambda e: e.copy(negb[0:1, 0:256], pbt[0:1, 0:256]))
                ng_t = bld.comp("act", lambda e: e.copy(
                    negb[0:1, 256:512], pbt[0:1, 0:256]))
                pbt_g = ng_t
                olb = (pb4, pb6)[p % 2]
                fs, fv = ol_free[p % 2]
                bld.wait("te", fs, fv)
                bld.wait("te", "ve", l_guard)
                bld.wait("te", "act", ng_t)
                first = True
                p_t = [0, 0]
                av_t = 0
                for g in range(ng):
                    slot = g % 2
                    sbank = (ps2a, ps2b)[slot]
                    bld.wait("te", "act", p_t[slot])
                    s_t = 0
                    for ki in range(4):
                        bld.compq("te", lambda e, g=g, ki=ki, sbank=sbank, kpair=kpair:
                                  e.matmul(
                            sbank[:, ki * 256:(ki + 1) * 256],
                            kpair[:, (g * 4 + ki) * 128:(g * 4 + ki + 1) * 128],
                            qtile[:, :], start=True, stop=False))
                        fn = lambda e, ki=ki, sbank=sbank: e.matmul(
                            sbank[:, ki * 256:(ki + 1) * 256],
                            ones_row[0:1, 0:128], negb[0:1, 0:256],
                            start=False, stop=True)
                        if ki == 3:
                            s_t = bld.comp("te", fn)
                        else:
                            bld.compq("te", fn)
                    if g >= 2 * p:
                        j = g - 2 * p
                        bld.wait("ve", "te", s_t)
                        mk_t = bld.comp("ve", lambda e, p=p, j=j, sbank=sbank:
                                        e.tensor_tensor(
                            sbank[:, :], sbank[:, :],
                            maskS[:, (p * 2 + j) * 1024:(p * 2 + j + 1) * 1024],
                            ALU.add))
                        bld.wait("act", "ve", mk_t)
                    else:
                        bld.wait("act", "te", s_t)
                    p_t[slot] = bld.comp("act", lambda e, slot=slot, sbank=sbank:
                                         e.activation(
                        out=pbuf[slot][:, :], in_=sbank[:, :], func=AF.Exp,
                        scale=ATTB_SCALE))
                    bld.wait("te", "act", p_t[slot])
                    for ki in range(4):
                        bld.compq("te", lambda e, g=g, ki=ki, slot=slot, olb=olb,
                                  vpair=vpair, first=(first and ki == 0),
                                  last=(g == ng - 1 and ki == 3): e.matmul(
                            olb[:, 0:256],
                            vpair[:, (g * 4 + ki) * 128:(g * 4 + ki + 1) * 128],
                            pbuf[slot][:, ki * 256:(ki + 1) * 256],
                            start=first, stop=last))
                        if ki == 0:
                            first = False
                    for g2 in range(2):
                        last = (g == ng - 1 and g2 == 1)
                        lfirst = (g == 0 and g2 == 0)
                        fn2 = lambda e, g2=g2, slot=slot, last=last, lfirst=lfirst: \
                            e.matmul(
                            pb5[0:1, 0:512], ones_col[:, :],
                            pbuf[slot][:, g2 * 512:(g2 + 1) * 512],
                            start=lfirst, stop=last)
                        if g2 == 1:
                            av_t = bld.comp("te", fn2)
                        else:
                            bld.compq("te", fn2)
                bld.wait("ve", "te", av_t)
                rl = rowbuf[0:1, 1536:1792]
                bld.compq("ve", lambda e: e.tensor_copy(
                    rowbuf[0:1, 1792:2048], pb5[0:1, 256:512]))
                bld.compq("ve", lambda e: e.tensor_tensor(
                    rowbuf[0:1, 1792:2048], pb5[0:1, 0:256], rowbuf[0:1, 1792:2048],
                    ALU.add))
                bld.compq("ve", lambda e, rl=rl: e.reciprocal(
                    rl, rowbuf[0:1, 1792:2048]))
                vt = bld.comp("ve", lambda e, rl=rl: e.tensor_copy(
                    rowbf[0:1, 0:256], rl))
                bld.wait("te", "ve", vt)
                bc_t = bld.comp("te", lambda e: e.matmul(
                    ps2a[:, 0:256], ones_row[:, :], rowbf[0:1, 0:256],
                    start=True, stop=True))
                bld.wait("ve", "te", bc_t)
                bld.compq("ve", lambda e: e.tensor_copy(
                    tmpf[2][:, 0:256], ps2a[:, 0:256]))
                ct = bld.comp("ve", lambda e, h=h, p=p, olb=olb: e.tensor_tensor(
                    xt[:, h * TPC + p * 256: h * TPC + p * 256 + 256],
                    olb[:, 0:256], tmpf[2][:, 0:256], ALU.mult))
                ol_free[p % 2] = ("ve", ct)
                l_guard = ct
                q_guard = bld.sem["te"].n
                kvp_guard[ps] = q_guard
        bld.join()

        emit_proj_fm(KC, KC, ow_h, xt_rhs, residual_epi(src, dst, obias_s))

    subs = [sublayer_attn_a, sublayer_ffn, sublayer_attn_b, sublayer_ffn]
    src = x_fm
    for i in range(N_SUB):
        dst = y_out if i == N_SUB - 1 else Xd
        subs[i](src, dst)
        src = Xd

    bld.join()
    for k in ("rst0", "rst1"):
        bld.wait("act", k, bld.skey(k).n)

    with nc.Block() as block:
        block.tensor(lambda e: [f(e) for f in bld.ops["te"]])
        block.vector(lambda e: [f(e) for f in bld.ops["ve"]])
        block.scalar(lambda e: [f(e) for f in bld.ops["act"]])
        block.sync(lambda e: [f(e) for f in bld.ops["sy"]])
        block.gpsimd(lambda e: [f(e) for f in bld.ops["gp"]])
    return nc


def kernel(**inputs):
    in_maps = _prep_inputs(inputs)
    nc = build_program()
    res = run_bass_kernel_spmd(nc, in_maps, core_ids=list(range(NC)))
    return _reassemble(res.results)


# version 09 (final)
# speedup vs baseline: 1.1116x; 1.0069x over previous
import numpy as np
import concourse.bass as bass
import concourse.mybir as mybir
from concourse.bass_utils import run_bass_kernel_spmd

F32 = mybir.dt.float32
BF16 = mybir.dt.bfloat16
AF = mybir.ActivationFunctionType
ALU = mybir.AluOpType
AX = mybir.AxisListType

B = 2
S = 4096
D = 4096
TPC = 1024
NC = 8
KC = 32
MASK_NEG = -1.0e9
ATTA_SCALE = 1.0 / float(np.sqrt(512.0))
ATTB_SCALE = 1.0 / float(np.sqrt(128.0))
CQ_FACTOR = 0.55
LN_EPS = 1e-5

N_SUB = 4



def _tile_oc_kc(w):
    Dk, O = w.shape
    return np.ascontiguousarray(
        w.reshape(Dk // 128, 128, O // 128, 128).transpose(2, 0, 1, 3))


def _tile_kc_ob(w, ob):
    Dk, O = w.shape
    return np.ascontiguousarray(
        w.reshape(Dk // 128, 128, O // ob, ob).transpose(2, 0, 1, 3))


def _prep_inputs(inp):
    x = np.asarray(inp["L0_attn_a_x"], np.float32)

    wqk = np.asarray(inp["L0_attn_a_Wqk_w"], np.float32)
    bqk = np.asarray(inp["L0_attn_a_Wqk_b"], np.float32)
    perm = np.empty(1024, np.int64)
    for h in range(8):
        for e in range(64):
            perm[h * 64 + e] = h * 128 + 2 * e
            perm[512 + h * 64 + e] = h * 128 + 2 * e + 1
    wqk_tiled = _tile_oc_kc(np.ascontiguousarray(wqk.T[:, perm]))
    bqk_dev = np.ascontiguousarray(bqk[perm].reshape(8, 128).T)

    wv = np.asarray(inp["L0_attn_a_Wv_w"], np.float32)
    bv = np.asarray(inp["L0_attn_a_Wv_b"], np.float32)
    wv_tiled = _tile_kc_ob(np.ascontiguousarray(wv.T), 512)
    bv_dev = bv.reshape(1, 512)

    wo = np.asarray(inp["L0_attn_a_Wo_w"], np.float32)
    bo = np.asarray(inp["L0_attn_a_Wo_b"], np.float32)
    wo_tiled = _tile_oc_kc(np.ascontiguousarray(wo.T))
    bo_dev = np.ascontiguousarray(bo.reshape(32, 128).T)

    qkv = np.asarray(inp["L0_attn_b_qkv_w"], np.float32)
    qkv_b = np.asarray(inp["L0_attn_b_qkv_b"], np.float32)
    qw = qkv[:, 0:4096].reshape(4096, 32, 128)
    kw = qkv[:, 4096:8192].reshape(4096, 32, 128)
    vw = qkv[:, 8192:12288]
    qb = qkv_b[0:4096].reshape(32, 128)
    kb = qkv_b[4096:8192].reshape(32, 128)
    vb = qkv_b[8192:12288]
    qk_cols = np.empty((4096, 64, 128), np.float32)
    qk_bias = np.empty((64, 128), np.float32)
    for h in range(32):
        qk_cols[:, 2 * h, 0:64] = qw[:, h, 0:64]
        qk_cols[:, 2 * h, 64:128] = kw[:, h, 0:64]
        qk_cols[:, 2 * h + 1, 0:64] = qw[:, h, 64:128]
        qk_cols[:, 2 * h + 1, 64:128] = kw[:, h, 64:128]
        qk_bias[2 * h, 0:64] = qb[h, 0:64]
        qk_bias[2 * h, 64:128] = kb[h, 0:64]
        qk_bias[2 * h + 1, 0:64] = qb[h, 64:128]
        qk_bias[2 * h + 1, 64:128] = kb[h, 64:128]
    wqkv_qk = _tile_oc_kc(qk_cols.reshape(4096, 64 * 128))
    bqkv_qk = np.ascontiguousarray(qk_bias.T)
    wqkv_v = _tile_kc_ob(np.ascontiguousarray(vw), 512)
    bqkv_v = np.ascontiguousarray(vb.reshape(8, 512))

    ow = np.asarray(inp["L0_attn_b_out_w"], np.float32)
    ob_ = np.asarray(inp["L0_attn_b_out_b"], np.float32)
    ow_tiled = _tile_oc_kc(ow)
    ob_dev = np.ascontiguousarray(ob_.reshape(32, 128).T)

    gw = np.asarray(inp["L0_ffn_gate_w"], np.float32)
    gw_tiled = _tile_kc_ob(np.ascontiguousarray(gw.T), 8)
    la = np.asarray(inp["L0_ffn_lora_A"], np.float32)
    lb = np.asarray(inp["L0_ffn_lora_B"], np.float32)
    a_cat = np.ascontiguousarray(np.concatenate([la[e].T for e in range(8)], axis=1))
    b_cat = np.ascontiguousarray(np.concatenate([lb[e].T for e in range(8)], axis=0))
    a_tiled = _tile_oc_kc(a_cat)
    b_tiled = _tile_oc_kc(b_cat)

    ident = np.eye(128, dtype=np.float32)
    indic = np.zeros((4, 8, 128), np.float32)
    for rc in range(4):
        for m in range(128):
            indic[rc, (rc * 128 + m) // 64, m] = 1.0

    inv_freq = 10000.0 ** (-(np.arange(64, dtype=np.float64) / 64.0))

    shared = {
        "wqk": wqk_tiled, "bqk": bqk_dev, "wv": wv_tiled, "bv": bv_dev,
        "wo": wo_tiled, "bo": bo_dev,
        "wqkv_qk": wqkv_qk, "bqkv_qk": bqkv_qk, "wqkv_v": wqkv_v, "bqkv_v": bqkv_v,
        "ow": ow_tiled, "obias": ob_dev,
        "gw": gw_tiled, "wa": a_tiled, "wbw": b_tiled,
        "ident": ident, "indic": indic,
    }

    in_maps = []
    for core in range(NC):
        b = core // 4
        c = core % 4
        blocks = [c + 4 * j for j in range(8)]
        pos = np.concatenate([np.arange(g * 128, (g + 1) * 128) for g in blocks])
        x_fm = np.ascontiguousarray(x[b, pos, :].T)

        ang = pos[:, None].astype(np.float64) * inv_freq[None, :]
        cos_t = np.cos(ang).astype(np.float32).T
        sin_t = np.sin(ang).astype(np.float32).T
        ropeC = np.ascontiguousarray(np.concatenate([cos_t, cos_t], axis=0))
        ropeS = np.ascontiguousarray(np.concatenate([sin_t, sin_t], axis=0))

        mask = np.zeros((4, 2, 128, 1024), np.float32)
        for p in range(4):
            for j in range(2):
                g = 2 * p + j
                for ki in range(4):
                    kappa = 4 * g + ki
                    gk = kappa * 128 + np.arange(128)
                    for sub in range(2):
                        slot = 2 * p + sub
                        gq = (c + 4 * slot) * 128 + np.arange(128)
                        m = np.where(gk[:, None] <= gq[None, :], 0.0, MASK_NEG)
                        mask[p, j, :, ki * 256 + sub * 128: ki * 256 + sub * 128 + 128] = m

        maskT = np.zeros((4, 2, 128, 1024), np.float32)
        for p in range(4):
            for sub in range(2):
                slot = 2 * p + sub
                gq = (c + 4 * slot) * 128 + np.arange(128)
                gk = 8 * p * 128 + np.arange(1024)
                maskT[p, sub] = np.where(gk[None, :] <= gq[:, None], 0.0, MASK_NEG)
        d = {"x_fm": x_fm, "ropeC": ropeC, "ropeS": ropeS,
             "maskb": mask.reshape(8, 128, 1024),
             "maskbT": maskT.reshape(8, 128, 1024)}
        d.update(shared)
        in_maps.append(d)
    return in_maps


def _reassemble(results):
    out = np.empty((B, S, D), np.float32)
    for core in range(NC):
        b = core // 4
        c = core % 4
        yt = results[core]["y"].T.reshape(8, 128, 4096)
        for j in range(8):
            g = c + 4 * j
            out[b, g * 128:(g + 1) * 128, :] = yt[j]
    return out



class Sem:
    def __init__(self, nc, name):
        self.h = nc.alloc_semaphore(name)
        self.n = 0

    def inc(self, d):
        self.n += d
        return self.n


class Bld:
    ENGS = ("te", "ve", "act", "sy", "gp")

    def __init__(self, nc):
        self.nc = nc
        self.ops = {k: [] for k in self.ENGS}
        self.sem = {k: Sem(nc, "s_" + k) for k in
                    ("te", "ve", "act", "sy", "gp", "st", "gpc")}

    def emit(self, eng, fn):
        self.ops[eng].append(fn)

    def wait(self, eng, sem_name, val):
        if val <= 0:
            return
        s = self.skey(sem_name)
        self.emit(eng, lambda e, s=s, v=val: e.wait_ge(s.h, v))

    def skey(self, key):
        if key not in self.sem:
            self.sem[key] = Sem(self.nc, "s_" + key)
        return self.sem[key]

    def dma(self, eng, out, in_, key):
        if eng == "act":
            self._selfedge("act")
        s = self.skey(key)
        self.emit(eng, lambda e, o=out, i=in_, s=s:
                  e.dma_start(out=o, in_=i).then_inc(s.h, 16))
        return s.inc(16)

    def _selfedge(self, eng):
        s = self.sem[eng]
        if s.n > 0:
            v = s.n
            self.emit(eng, lambda e, s=s, v=v: e.wait_ge(s.h, v))

    def comp(self, eng, fn, sem_name=None):
        if eng in ("ve", "act"):
            self._selfedge(eng)
        s = self.sem[sem_name or ("gpc" if eng == "gp" else eng)]
        self.emit(eng, lambda e, f=fn, s=s: f(e).then_inc(s.h, 1))
        return s.inc(1)

    def compq(self, eng, fn):
        if eng in ("ve", "act"):
            self.comp(eng, fn)
            return
        self.emit(eng, lambda e, f=fn: f(e))

    def join(self):
        targets = {k: v.n for k, v in self.sem.items()}
        for eng in self.ENGS:
            for sn, v in targets.items():
                if sn == eng:
                    continue
                self.wait(eng, sn, v)


def build_program():
    nc = bass.Bass("TRN2", target_bir_lowering=False, num_devices=NC)
    bld = Bld(nc)

    DT = nc.dram_tensor
    x_fm = DT("x_fm", [D, TPC], F32, kind="ExternalInput").ap()
    y_out = DT("y", [D, TPC], F32, kind="ExternalOutput").ap()
    wqk_h = DT("wqk", [8, KC, 128, 128], F32, kind="ExternalInput").ap()
    bqk_h = DT("bqk", [128, 8], F32, kind="ExternalInput").ap()
    wv_h = DT("wv", [1, KC, 128, 512], F32, kind="ExternalInput").ap()
    bv_h = DT("bv", [1, 512], F32, kind="ExternalInput").ap()
    wo_h = DT("wo", [KC, 4, 128, 128], F32, kind="ExternalInput").ap()
    bo_h = DT("bo", [128, KC], F32, kind="ExternalInput").ap()
    wqkqk_h = DT("wqkv_qk", [64, KC, 128, 128], F32, kind="ExternalInput").ap()
    bqkqk_h = DT("bqkv_qk", [128, 64], F32, kind="ExternalInput").ap()
    wqkvv_h = DT("wqkv_v", [8, KC, 128, 512], F32, kind="ExternalInput").ap()
    bqkvv_h = DT("bqkv_v", [8, 512], F32, kind="ExternalInput").ap()
    ow_h = DT("ow", [KC, KC, 128, 128], F32, kind="ExternalInput").ap()
    obias_h = DT("obias", [128, KC], F32, kind="ExternalInput").ap()
    gw_h = DT("gw", [1, KC, 128, 8], F32, kind="ExternalInput").ap()
    wa_h = DT("wa", [4, KC, 128, 128], F32, kind="ExternalInput").ap()
    wbw_h = DT("wbw", [KC, 4, 128, 128], F32, kind="ExternalInput").ap()
    ident_h = DT("ident", [128, 128], F32, kind="ExternalInput").ap()
    indic_h = DT("indic", [4, 8, 128], F32, kind="ExternalInput").ap()
    ropeC_h = DT("ropeC", [128, TPC], F32, kind="ExternalInput").ap()
    ropeS_h = DT("ropeS", [128, TPC], F32, kind="ExternalInput").ap()
    maskb_h = DT("maskb", [8, 128, 1024], F32, kind="ExternalInput").ap()
    maskbT_h = DT("maskbT", [8, 128, 1024], F32, kind="ExternalInput").ap()

    Xd = DT("Xd", [D, TPC], F32, kind="Internal").ap()
    qbuf = DT("qbuf", [D, TPC], BF16, kind="Internal").ap()
    kvbK_in = DT("kvbK_in", [D * TPC], BF16, kind="Internal").ap()
    kvbV_in = DT("kvbV_in", [8 * TPC * 512], BF16, kind="Internal").ap()
    kvbK_all = DT("kvbK_all", [4, D * TPC], BF16, kind="Internal").ap()
    kvbV_all = DT("kvbV_all", [32, TPC * 512], BF16, kind="Internal").ap()
    kva_in = DT("kva_in", [2 * 512 * TPC], BF16, kind="Internal").ap()
    kva_all = DT("kva_all", [4, 2 * 512 * TPC], BF16, kind="Internal").ap()

    kvbK = kvbK_in.rearrange("(d t) -> d t", d=D)
    kvbVo = kvbV_in.rearrange("(o t d) -> o t d", o=8, t=TPC)
    kvaK = kva_in[0:512 * TPC].rearrange("(d t) -> d t", d=512)
    kvaV = kva_in[512 * TPC:2 * 512 * TPC].rearrange("(t d) -> t d", t=TPC)

    A = lambda name, shape, dt: nc.alloc_sbuf_tensor(name, shape, dt).ap()
    xt = A("xt", [128, KC * TPC], BF16)
    off1, _ = nc.bump_sbuf(16384)
    qa = nc.alloc_sbuf_tensor_at("qa", [65, 8 * TPC], BF16, offset=off1).ap()
    maskS = nc.alloc_sbuf_tensor_at("maskS", [128, 8 * 1024], BF16, offset=off1).ap()
    off2, _ = nc.bump_sbuf(8192)
    ctxa = nc.alloc_sbuf_tensor_at("ctxa", [128, 4 * TPC], BF16, offset=off2).ap()
    ropeC = nc.alloc_sbuf_tensor_at("ropeCs", [128, TPC], F32, offset=off2).ap()
    ropeS = nc.alloc_sbuf_tensor_at("ropeSs", [128, TPC], F32, offset=off2 + 4096).ap()
    wsl = [A(f"wsl{i}", [128, 16 * 512], BF16) for i in range(2)]

    def qk_slab(i):
        return wsl[i // 2][:, (i % 2) * (KC * 128):(i % 2 + 1) * (KC * 128)]

    wb_oc = [A(f"wboc{i}", [128, KC * 128], BF16) for i in range(2)]
    hhat = A("hhat", [128, 4 * TPC], BF16)
    katile = A("katile", [65, KC * 128], BF16)
    vatile = A("vatile", [128, KC * 65], BF16)
    pbuf = [A(f"pbuf{i}", [128, 1024], BF16) for i in range(2)]
    kbufs = [A(f"kbuf{i}", [128, 512], BF16) for i in range(2)]
    vbufs = [A(f"vbuf{i}", [128, 512], BF16) for i in range(2)]
    qtile = A("qtile", [128, 256], BF16)
    stg = [A(f"stg{i}", [128, 512], BF16) for i in range(4)]
    tmpf = [A(f"tmpf{i}", [128, 512], F32) for i in range(3)]
    xb2 = [A(f"xb{i}", [128, 512], BF16) for i in range(2)]
    sq2 = [A(f"sq{i}", [128, 512], BF16) for i in range(2)]
    rowbuf = A("rowbuf", [1, 2048], F32)
    rowbf = A("rowbf", [1, 1024], BF16)
    g8 = A("g8", [128, 48], F32)
    wtm = A("wtm", [128, 8], BF16)
    wfm = A("wfm", [8, TPC], BF16)
    ident = A("idents", [128, 128], BF16)
    indic = A("indics", [8, 4 * 128], BF16)
    ones_col = A("ones_col", [128, 1], BF16)
    ones_row = A("ones_row", [1, 128], BF16)
    bqk_s = A("bqk_s", [128, 8], F32)
    bo_s = A("bo_s", [128, KC], F32)
    bqkqk_s = A("bqkqk_s", [128, 64], F32)
    obias_s = A("obias_s", [128, KC], F32)
    bv_s = A("bv_s", [1, 512], BF16)
    bvv_s = A("bvv_s", [1, 512], BF16)
    gw_s = A("gw_s", [128, KC * 8], BF16)

    P = lambda name, shape, dt=F32: nc.alloc_psum_tensor(name, shape, dt).ap()
    ps2a = P("ps2a", [128, 1024])
    ps2b = P("ps2b", [128, 1024])
    pb4 = P("pb4", [128, 512])
    pb5 = P("pb5", [128, 512])
    pb6 = P("pb6", [128, 512])
    pbt = P("pbt", [128, 1024], BF16)

    bld.dma("sy", bqk_s[:, :], bqk_h[:, :], "cst")
    bld.dma("sy", bo_s[:, :], bo_h[:, :], "cst")
    bld.dma("sy", bqkqk_s[:, :], bqkqk_h[:, :], "cst")
    bld.dma("sy", obias_s[:, :], obias_h[:, :], "cst")
    bld.dma("gp", ident[:, :], ident_h[:, :], "cstg")
    bld.dma("gp", indic.rearrange("e (r c) -> e r c", r=4),
            indic_h.rearrange("r e c -> e r c"), "cstg")
    bld.dma("gp", bv_s[:, :], bv_h[:, :], "cstg")
    bld.dma("gp", gw_s.rearrange("d (k e) -> d k e", k=KC),
            gw_h[0].rearrange("k d e -> d k e"), "cstg")
    bld.comp("gp", lambda e: e.memset(ones_col[:, :], 1.0))
    bld.comp("gp", lambda e: e.memset(ones_row[:, :], 1.0))
    bld.join()

    def slab_dma(eng, buf, n_kc, width, w_oc, key):
        return bld.dma(eng,
                       buf[:, 0:n_kc * width].rearrange("d (k o) -> d k o", k=n_kc),
                       w_oc.rearrange("k d o -> d k o"), key)

    def phase_ln(src):
        tmp_use = [0, 0]
        sqg = [0, 0]
        mmg = [0, 0]
        for tt in range(2):
            t0 = tt * 512
            sum_done = 0
            for kc in range(KC):
                sl = kc % 2
                bld.wait("sy", "ve", tmp_use[sl])
                lt = bld.dma("sy", tmpf[sl][:, :],
                             src[kc * 128:(kc + 1) * 128, t0:t0 + 512],
                             f"ltmp{sl}")
                bld.wait("ve", f"ltmp{sl}", lt)
                bld.wait("ve", "act", sqg[sl])
                bld.wait("ve", "te", mmg[sl])
                ct = bld.comp("ve", lambda e, sl=sl: e.tensor_copy(
                    xb2[sl][:, :], tmpf[sl][:, :]))
                tmp_use[sl] = ct
                bld.wait("act", "ve", ct)
                bld.wait("act", "te", mmg[sl])
                sq_t = bld.comp("act", lambda e, sl=sl: e.activation(
                    out=sq2[sl][:, :], in_=xb2[sl][:, :], func=AF.Square))
                sqg[sl] = sq_t
                bld.wait("te", "ve", ct)
                bld.compq("te", lambda e, sl=sl, kc=kc: e.matmul(
                    pb4[0:1, :], ones_col[:, :], xb2[sl][:, :],
                    start=(kc == 0), stop=(kc == KC - 1)))
                bld.wait("te", "act", sq_t)
                sum_done = bld.comp("te", lambda e, sl=sl, kc=kc: e.matmul(
                    pb5[0:1, :], ones_col[:, :], sq2[sl][:, :],
                    start=(kc == 0), stop=(kc == KC - 1)))
                mmg[sl] = sum_done
            m_row = rowbuf[0:1, 0:512]
            e2_row = rowbuf[0:1, 512:1024]
            var_row = rowbuf[0:1, 1024:1536]
            r_row = rowbuf[0:1, 1536:2048]
            bld.wait("ve", "te", sum_done)
            bld.compq("ve", lambda e: e.tensor_scalar(
                m_row, pb4[0:1, :], 1.0 / D, None, ALU.mult))
            bld.compq("ve", lambda e: e.tensor_scalar(
                e2_row, pb5[0:1, :], 1.0 / D, None, ALU.mult))
            bld.compq("ve", lambda e: e.tensor_tensor(var_row, m_row, m_row, ALU.mult))
            bld.compq("ve", lambda e: e.tensor_tensor(
                var_row, e2_row, var_row, ALU.subtract))
            bld.compq("ve", lambda e: e.tensor_scalar(
                var_row, var_row, LN_EPS, None, ALU.add))
            vt = bld.comp("ve", lambda e: e.reciprocal(r_row, var_row))
            bld.wait("act", "ve", vt)
            at = bld.comp("act", lambda e: e.activation(
                out=r_row, in_=r_row, func=AF.Sqrt))
            bld.wait("ve", "act", at)
            m_bf = rowbf[0:1, 0:512]
            r_bf = rowbf[0:1, 512:1024]
            bld.compq("ve", lambda e: e.tensor_copy(m_bf, m_row))
            rowt = bld.comp("ve", lambda e: e.tensor_copy(r_bf, r_row))
            bld.wait("te", "ve", rowt)
            bld.compq("te", lambda e: e.matmul(
                pb6[:, :], ones_row[:, :], m_bf, start=True, stop=True))
            bc_t = bld.comp("te", lambda e: e.matmul(
                ps2a[:, 0:512], ones_row[:, :], r_bf, start=True, stop=True))
            bld.wait("ve", "te", bc_t)
            for kc in range(KC):
                sl = kc % 2
                bld.wait("sy", "ve", tmp_use[sl])
                lt = bld.dma("sy", tmpf[sl][:, :],
                             src[kc * 128:(kc + 1) * 128, t0:t0 + 512],
                             f"ltmp{sl}")
                bld.wait("ve", f"ltmp{sl}", lt)
                bld.compq("ve", lambda e, sl=sl: e.tensor_tensor(
                    tmpf[sl][:, :], tmpf[sl][:, :], pb6[:, :], ALU.subtract))
                tmp_use[sl] = bld.comp("ve", lambda e, sl=sl, kc=kc, t0=t0:
                                       e.tensor_tensor(
                    xt[:, kc * TPC + t0: kc * TPC + t0 + 512],
                    tmpf[sl][:, :], ps2a[:, 0:512], ALU.mult))
        bld.join()

    def emit_proj_fm(n_oc, n_kc, w_dram, rhs_fn, epi_fn):
        bank_free = [("ve", 0), ("ve", 0)]
        buf_guard = [0, 0]
        for oc in range(n_oc):
            slot = oc % 2
            bld.wait("gp", "te", buf_guard[slot])
            wl = slab_dma("gp", wb_oc[slot], n_kc, 128, w_dram[oc], f"wb{slot}")
            bld.wait("te", f"wb{slot}", wl)
            for tt in range(2):
                bank = (pb4, pb5)[tt]
                fs, fv = bank_free[tt]
                bld.wait("te", fs, fv)
                mm_t = 0
                for kc in range(n_kc):
                    fn = lambda e, oc=oc, tt=tt, kc=kc, bank=bank, slot=slot: e.matmul(
                        bank[:, :], wb_oc[slot][:, kc * 128:(kc + 1) * 128],
                        rhs_fn(kc, tt), start=(kc == 0), stop=(kc == n_kc - 1))
                    if kc == n_kc - 1:
                        mm_t = bld.comp("te", fn)
                    else:
                        bld.compq("te", fn)
                bank_free[tt] = epi_fn(oc, tt, bank, mm_t)
            buf_guard[slot] = bld.sem["te"].n
        bld.join()

    def xt_rhs(kc, tt):
        return xt[:, kc * TPC + tt * 512: kc * TPC + tt * 512 + 512]

    def residual_epi(src, dst, bias_s):
        slot_guard = [0, 0]

        def epi(oc, tt, bank, mm_t, src=src, dst=dst, bias_s=bias_s):
            i = (oc * 2 + tt) % 2
            bld.wait("sy", f"rst{i}", slot_guard[i])
            lx = bld.dma("sy", tmpf[i][:, :],
                         src[oc * 128:(oc + 1) * 128, tt * 512: tt * 512 + 512],
                         f"ltmp{i}")
            bld.wait("ve", f"ltmp{i}", lx)
            bld.wait("ve", "te", mm_t)
            if bias_s is not None:
                vt = bld.comp("ve", lambda e, oc=oc, bank=bank, i=i:
                              e.scalar_tensor_tensor(
                    tmpf[i][:, :], bank[:, :], bias_s[:, oc:oc + 1],
                    tmpf[i][:, :], ALU.add, ALU.add))
            else:
                vt = bld.comp("ve", lambda e, bank=bank, i=i: e.tensor_tensor(
                    tmpf[i][:, :], bank[:, :], tmpf[i][:, :], ALU.add))
            bld.wait("act", "ve", vt)
            stt = bld.dma("act", dst[oc * 128:(oc + 1) * 128, tt * 512: tt * 512 + 512],
                          tmpf[i][:, :], f"rst{i}")
            slot_guard[i] = stt
            return ("ve", vt)
        return epi

    def sublayer_attn_a(src, dst):
        phase_ln(src)

        sg_guard = {}

        def epi(oc, tt, bank, mm_t):
            sl = (oc * 2 + tt) % 4
            bld.wait("act", "te", mm_t)
            bld.wait("act", f"qst{sl}", sg_guard.get(sl, 0))
            at = bld.comp("act", lambda e, oc=oc, bank=bank, sl=sl: e.add(
                stg[sl][:, :], bank[:, :], bqk_s[:, oc:oc + 1]))
            bld.wait("sy", "act", at)
            if oc < 4:
                for half in range(2):
                    h = oc * 2 + half
                    bld.dma("sy", qa[0:64, h * TPC + tt * 512: h * TPC + tt * 512 + 512],
                            stg[sl][half * 64:(half + 1) * 64, :], f"qst{sl}")
            else:
                ko = oc - 4
                for half in range(2):
                    h = ko * 2 + half
                    bld.dma("sy", kvaK[h * 64:(h + 1) * 64, tt * 512: tt * 512 + 512],
                            stg[sl][half * 64:(half + 1) * 64, :], f"qst{sl}")
            done = bld.skey(f"qst{sl}").n
            sg_guard[sl] = done
            bld.wait("ve", "act", at)
            vt = bld.comp("ve", lambda e: e.nop())
            return ("ve", vt)
        emit_proj_fm(8, KC, wqk_h, xt_rhs, epi)

        for half in range(2):
            slab_dma("gp", wsl[half], 16, 512,
                     wv_h[0, half * 16:(half + 1) * 16], f"wsl{half}")
        bld.wait("te", "wsl0", bld.skey("wsl0").n)
        bld.wait("te", "wsl1", bld.skey("wsl1").n)
        vfree = [("ve", 0), ("ve", 0)]
        st_g = [0, 0]
        for tc in range(8):
            bank = (pb4, pb5)[tc % 2]
            fs, fv = vfree[tc % 2]
            bld.wait("te", fs, fv)
            for kc in range(KC):
                bld.compq("te", lambda e, tc=tc, kc=kc, bank=bank: e.matmul(
                    bank[:, :],
                    xt[:, kc * TPC + tc * 128: kc * TPC + tc * 128 + 128],
                    wsl[kc // 16][:, (kc % 16) * 512:(kc % 16) * 512 + 512],
                    start=(kc == 0), stop=False))
            mm_t = bld.comp("te", lambda e, bank=bank: e.matmul(
                bank[:, :], ones_row[:, :], bv_s[:, :], start=False, stop=True))
            bld.wait("ve", "te", mm_t)
            bld.wait("ve", f"vst{tc % 2}", st_g[tc % 2])
            vt = bld.comp("ve", lambda e, tc=tc, bank=bank: e.tensor_copy(
                stg[tc % 2][:, :], bank[:, :]))
            vfree[tc % 2] = ("ve", vt)
            bld.wait("act", "ve", vt)
            st_g[tc % 2] = bld.dma(
                "act", kvaV[tc * 128:(tc + 1) * 128, :], stg[tc % 2][:, :],
                f"vst{tc % 2}")
        bld.join()

        bld.comp("gp", lambda e: e.memset(katile[64:65, :], 1.0))
        bld.comp("gp", lambda e: e.memset(
            vatile.rearrange("p (k c) -> p k c", k=KC)[:, :, 64:65], 1.0))

        bld.comp("gp", lambda e: e.collective_compute(
            "AllGather", ALU.bypass, ins=[kva_in[:]], outs=[kva_all[:, :]],
            replica_groups=[[0, 1, 2, 3], [4, 5, 6, 7]]))
        bld.join()

        kvaKg = kva_all[:, 0:512 * TPC].rearrange("r (d t) -> r d t", d=512)
        kvaVg = kva_all[:, 512 * TPC:2 * 512 * TPC].rearrange("r (t d) -> r t d", t=TPC)

        kv_guard = 0
        ol_free = 0
        pbt_g = [0]
        for h in range(8):
            bld.wait("sy", "te", kv_guard)
            for r in range(4):
                bld.dma("sy",
                        katile[0:64, :].rearrange("p (kk t) -> p kk t", kk=KC)[:, r::4, :],
                        kvaKg[r, h * 64:(h + 1) * 64, :].rearrange(
                            "d (s t) -> d s t", s=8), "kva")
                bld.dma("sy",
                        vatile.rearrange("p (kk c) -> p kk c", kk=KC)[:, r::4, 0:64],
                        kvaVg[r, :, h * 64:(h + 1) * 64].rearrange(
                            "(s p) d -> p s d", s=8), "kva")
            kvload = bld.skey("kva").n
            bld.wait("te", "kva", kvload)
            bld.wait("te", "ve", ol_free)
            bld.wait("te", "act", pbt_g[0])
            p1v = [0, 0]
            trp = 0
            for qb in range(8):
                for kt in range(8):
                    bank = (pb4, pb5)[kt % 2]
                    bld.wait("te", "ve", p1v[kt % 2])
                    s1 = bld.comp("te", lambda e, h=h, qb=qb, kt=kt, bank=bank: e.matmul(
                        bank[:, :], qa[0:64, h * TPC + qb * 128: h * TPC + qb * 128 + 128],
                        katile[0:64, kt * 512:(kt + 1) * 512], start=True, stop=True))
                    bld.wait("ve", "te", s1)
                    p1v[kt % 2] = bld.comp("ve", lambda e, kt=kt, bank=bank: e.reduce_max(
                        g8[:, kt:kt + 1], bank[:, :], axis=AX.X))
                bld.compq("ve", lambda e: e.reduce_max(
                    g8[:, 40:41], g8[:, 0:8], axis=AX.X))
                bld.wait("ve", "te", trp)
                nv = bld.comp("ve", lambda e: e.tensor_scalar(
                    wtm[:, 0:1], g8[:, 40:41], -1.0, None, ALU.mult))
                bld.wait("te", "ve", nv)
                trp = bld.comp("te", lambda e, qb=qb: e.transpose(
                    pbt[0:1, qb * 128:(qb + 1) * 128], wtm[:, 0:1], ident[:, :]))
            bld.wait("act", "te", trp)
            at = bld.comp("act", lambda e: e.copy(rowbf[0:1, 0:1024], pbt[0:1, 0:1024]))
            pbt_g[0] = at
            bld.wait("sy", "act", at)
            cq_l = bld.dma("sy", qa[64:65, h * TPC:(h + 1) * TPC],
                           rowbf[0:1, 0:1024], "cq")

            for qt in range(2):
                bld.wait("te", "cq", cq_l)
                bld.wait("te", "kva", kvload)
                bld.wait("te", "ve", ol_free)
                first = True
                p_t = [0, 0]
                av_t = 0
                for kk in range(KC):
                    sb = (ps2a, ps2b)[kk % 2][:, 0:512]
                    bld.wait("te", "act", p_t[kk % 2])
                    s_t = bld.comp("te", lambda e, kk=kk, h=h, qt=qt, sb=sb: e.matmul(
                        sb, katile[:, kk * 128:(kk + 1) * 128],
                        qa[:, h * TPC + qt * 512: h * TPC + qt * 512 + 512],
                        start=True, stop=True))
                    bld.wait("act", "te", s_t)
                    p_t[kk % 2] = bld.comp("act", lambda e, kk=kk, sb=sb: e.activation(
                        out=pbuf[kk % 2][:, 0:512], in_=sb, func=AF.Exp))
                    bld.wait("te", "act", p_t[kk % 2])
                    fnv = lambda e, kk=kk, first=first: e.matmul(
                        pb6[0:65, :], vatile[:, kk * 65: kk * 65 + 65],
                        pbuf[kk % 2][:, 0:512], start=first, stop=(kk == KC - 1))
                    if kk == KC - 1:
                        av_t = bld.comp("te", fnv)
                    else:
                        bld.compq("te", fnv)
                    first = False
                bld.wait("ve", "te", av_t)
                bld.wait("ve", "cq", cq_l)
                rl = rowbuf[0:1, 1536:2048]
                bld.compq("ve", lambda e, rl=rl: e.reciprocal(rl, pb6[64:65, :]))
                vt = bld.comp("ve", lambda e, rl=rl: e.tensor_scalar(
                    rowbf[0:1, 0:512], rl, ATTA_SCALE, None, ALU.mult))
                bld.wait("te", "ve", vt)
                bc_t = bld.comp("te", lambda e: e.matmul(
                    pb4[0:64, :], ones_row[0:1, 0:64], rowbf[0:1, 0:512],
                    start=True, stop=True))
                bld.wait("ve", "te", bc_t)
                bld.compq("ve", lambda e: e.tensor_copy(tmpf[2][0:64, :], pb4[0:64, :]))
                bld.wait("ve", "ctxst", bld.skey("ctxst").n)
                ct = bld.comp("ve", lambda e: e.tensor_tensor(
                    stg[3][0:64, :], pb6[0:64, :], tmpf[2][0:64, :], ALU.mult))
                ol_free = ct
                bld.wait("sy", "ve", ct)
                bld.dma("sy", ctxa[(h % 2) * 64:(h % 2) * 64 + 64,
                                   (h // 2) * TPC + qt * 512:
                                   (h // 2) * TPC + qt * 512 + 512],
                        stg[3][0:64, :], "ctxst")
            kv_guard = bld.sem["te"].n
        bld.join()

        def ctx_rhs(kc, tt):
            return ctxa[:, kc * TPC + tt * 512: kc * TPC + tt * 512 + 512]
        emit_proj_fm(KC, 4, wo_h, ctx_rhs, residual_epi(src, dst, bo_s))

    def sublayer_ffn(src, dst):
        phase_ln(src)
        tr_t = 0
        gate_free = 0
        for tc in range(8):
            bld.wait("te", "ve", gate_free)
            mm_t = 0
            for kc in range(KC):
                fn = lambda e, tc=tc, kc=kc: e.matmul(
                    pb4[:, 0:8], xt[:, kc * TPC + tc * 128: kc * TPC + tc * 128 + 128],
                    gw_s[:, kc * 8:(kc + 1) * 8], start=(kc == 0), stop=(kc == KC - 1))
                if kc == KC - 1:
                    mm_t = bld.comp("te", fn)
                else:
                    bld.compq("te", fn)
            bld.wait("ve", "te", mm_t)
            g = g8[:, 0:8]
            m1 = g8[:, 8:9]
            mk1 = g8[:, 9:17]
            gg2 = g8[:, 17:25]
            m2 = g8[:, 25:26]
            mk2 = g8[:, 26:34]
            dd = g8[:, 34:35]
            s1 = g8[:, 35:36]
            sn = g8[:, 36:37]
            wa_ = g8[:, 37:45]
            gate_free = bld.comp("ve", lambda e, g=g: e.tensor_copy(g, pb4[:, 0:8]))
            bld.compq("ve", lambda e, g=g, m1=m1: e.reduce_max(m1, g, axis=AX.X))
            bld.compq("ve", lambda e, g=g, m1=m1, mk1=mk1: e.tensor_scalar(
                mk1, g, m1, None, ALU.is_ge))
            bld.compq("ve", lambda e, g=g, mk1=mk1, gg2=gg2: e.scalar_tensor_tensor(
                gg2, mk1, MASK_NEG, g, ALU.mult, ALU.add))
            bld.compq("ve", lambda e, gg2=gg2, m2=m2: e.reduce_max(m2, gg2, axis=AX.X))
            bld.compq("ve", lambda e, gg2=gg2, m2=m2, mk2=mk2: e.tensor_scalar(
                mk2, gg2, m2, None, ALU.is_ge))
            vt = bld.comp("ve", lambda e, m1=m1, m2=m2, dd=dd: e.tensor_tensor(
                dd, m1, m2, ALU.subtract))
            bld.wait("act", "ve", vt)
            at = bld.comp("act", lambda e, dd=dd, s1=s1: e.activation(
                out=s1, in_=dd, func=AF.Sigmoid))
            bld.wait("ve", "act", at)
            bld.compq("ve", lambda e, s1=s1, sn=sn: e.tensor_scalar(
                sn, s1, -1.0, 1.0, ALU.mult, ALU.add))
            bld.compq("ve", lambda e, mk1=mk1, s1=s1, wa_=wa_: e.tensor_scalar(
                wa_, mk1, s1, None, ALU.mult))
            bld.wait("ve", "te", tr_t)
            vt = bld.comp("ve", lambda e, mk2=mk2, sn=sn, wa_=wa_: e.scalar_tensor_tensor(
                wtm[:, :], mk2, sn, wa_, ALU.mult, ALU.add))
            bld.wait("te", "ve", vt)
            tr_t = bld.comp("te", lambda e, tc=tc: e.transpose(
                pbt[0:8, tc * 128:(tc + 1) * 128], wtm[:, :], ident[:, :]))
        bld.wait("ve", "te", tr_t)
        wf_t = bld.comp("ve", lambda e: e.tensor_copy(wfm[:, :], pbt[0:8, 0:1024]))

        hh_free = [("ve", 0), ("ve", 0)]
        buf_guard = [0, 0]
        wtg = 0
        for rc in range(4):
            slot = rc % 2
            bld.wait("gp", "te", buf_guard[slot])
            wl = slab_dma("gp", wb_oc[slot], KC, 128, wa_h[rc], f"wb{slot}")
            bld.wait("te", f"wb{slot}", wl)
            for tt in range(2):
                bank = (pb4, pb5)[tt]
                fs, fv = hh_free[tt]
                bld.wait("te", fs, fv)
                mm_t = 0
                for kc in range(KC):
                    fn = lambda e, kc=kc, tt=tt, bank=bank, slot=slot: e.matmul(
                        bank[:, :], wb_oc[slot][:, kc * 128:(kc + 1) * 128],
                        xt_rhs(kc, tt), start=(kc == 0), stop=(kc == KC - 1))
                    if kc == KC - 1:
                        mm_t = bld.comp("te", fn)
                    else:
                        bld.compq("te", fn)
                bld.wait("te", "ve", wf_t)
                bld.wait("te", "ve", wtg)
                wb_t = bld.comp("te", lambda e, rc=rc, tt=tt: e.matmul(
                    pb6[:, :], indic[:, rc * 128:(rc + 1) * 128],
                    wfm[:, tt * 512: tt * 512 + 512], start=True, stop=True))
                bld.wait("ve", "te", wb_t)
                wtg = bld.comp("ve", lambda e: e.tensor_copy(stg[2][:, :], pb6[:, :]))
                hh = bld.comp("ve", lambda e, rc=rc, tt=tt, bank=bank: e.tensor_tensor(
                    hhat[:, rc * TPC + tt * 512: rc * TPC + tt * 512 + 512],
                    bank[:, :], stg[2][:, :], ALU.mult))
                hh_free[tt] = ("ve", hh)
            buf_guard[slot] = bld.sem["te"].n
        bld.join()

        def h_rhs(kc, tt):
            return hhat[:, kc * TPC + tt * 512: kc * TPC + tt * 512 + 512]
        emit_proj_fm(KC, 4, wbw_h, h_rhs, residual_epi(src, dst, None))

    def sublayer_attn_b(src, dst):
        phase_ln(src)
        bld.dma("sy", ropeC[:, :], ropeC_h[:, :], "cst")
        rl_t = bld.dma("sy", ropeS[:, :], ropeS_h[:, :], "cst")
        bld.wait("ve", "cst", rl_t)

        slab_guard = [0, 0, 0, 0]
        st_guard = {0: 0, 2: 0}
        rope_free = {0: ("ve", 0), 1: ("ve", 0)}
        for h in range(32):
            sA = (h % 2) * 2
            sB = sA + 1
            bld.wait("gp", "te", slab_guard[sA])
            wlA = slab_dma("gp", qk_slab(sA), KC, 128, wqkqk_h[2 * h], f"slab{sA}")
            bld.wait("gp", "te", slab_guard[sB])
            wlB = slab_dma("gp", qk_slab(sB), KC, 128, wqkqk_h[2 * h + 1], f"slab{sB}")
            bld.wait("te", f"slab{sA}", wlA)
            bld.wait("te", f"slab{sB}", wlB)
            for tt in range(2):
                pA = (pb4, pb6)[tt]
                pB = (pb5[:, 0:512], ps2a[:, 0:512])[tt]
                fs, fv = rope_free[tt]
                bld.wait("te", fs, fv)
                for kc in range(KC):
                    bld.compq("te", lambda e, kc=kc, sA=sA, tt=tt, pA=pA: e.matmul(
                        pA[:, 0:512], qk_slab(sA)[:, kc * 128:(kc + 1) * 128],
                        xt_rhs(kc, tt), start=(kc == 0), stop=(kc == KC - 1)))
                mm_t = 0
                for kc in range(KC):
                    fn = lambda e, kc=kc, sB=sB, tt=tt, pB=pB: e.matmul(
                        pB, qk_slab(sB)[:, kc * 128:(kc + 1) * 128],
                        xt_rhs(kc, tt), start=(kc == 0), stop=(kc == KC - 1))
                    if kc == KC - 1:
                        mm_t = bld.comp("te", fn)
                    else:
                        bld.compq("te", fn)
                t0 = tt * 512
                bA = bqkqk_s[:, 2 * h: 2 * h + 1]
                bB = bqkqk_s[:, 2 * h + 1: 2 * h + 2]
                sg0 = tt * 2
                bld.wait("ve", "te", mm_t)
                bld.wait("ve", f"ropest{sg0}", st_guard[sg0])
                bld.compq("ve", lambda e, pA=pA, bA=bA, t0=t0: e.scalar_tensor_tensor(
                    tmpf[0][:, :], pA[:, 0:512], bA, ropeC[:, t0:t0 + 512],
                    ALU.add, ALU.mult))
                bld.compq("ve", lambda e, pB=pB, bB=bB, t0=t0: e.scalar_tensor_tensor(
                    tmpf[1][:, :], pB, bB, ropeS[:, t0:t0 + 512],
                    ALU.add, ALU.mult))
                bld.compq("ve", lambda e, sg0=sg0: e.tensor_tensor(
                    stg[sg0][:, :], tmpf[0][:, :], tmpf[1][:, :], ALU.subtract))
                bld.compq("ve", lambda e, pB=pB, bB=bB, t0=t0: e.scalar_tensor_tensor(
                    tmpf[0][:, :], pB, bB, ropeC[:, t0:t0 + 512],
                    ALU.add, ALU.mult))
                bld.compq("ve", lambda e, pA=pA, bA=bA, t0=t0: e.scalar_tensor_tensor(
                    tmpf[1][:, :], pA[:, 0:512], bA, ropeS[:, t0:t0 + 512],
                    ALU.add, ALU.mult))
                vt = bld.comp("ve", lambda e, sg0=sg0: e.tensor_tensor(
                    stg[sg0 + 1][:, :], tmpf[0][:, :], tmpf[1][:, :], ALU.add))
                rope_free[tt] = ("ve", vt)
                bld.wait("act", "ve", vt)
                bld.dma("act", qbuf[h * 128: h * 128 + 64, t0:t0 + 512],
                        stg[sg0][0:64, :], f"ropest{sg0}")
                bld.dma("act", qbuf[h * 128 + 64: h * 128 + 128, t0:t0 + 512],
                        stg[sg0 + 1][0:64, :], f"ropest{sg0}")
                bld.dma("act", kvbK[h * 128: h * 128 + 64, t0:t0 + 512],
                        stg[sg0][64:128, :], f"ropest{sg0}")
                stq = bld.dma("act", kvbK[h * 128 + 64: h * 128 + 128, t0:t0 + 512],
                              stg[sg0 + 1][64:128, :], f"ropest{sg0}")
                st_guard[sg0] = stq
            slab_guard[sA] = bld.sem["te"].n
            slab_guard[sB] = slab_guard[sA]
        bld.join()

        vv_guard = 0
        vfree = [("ve", 0), ("ve", 0)]
        st_g = [0, 0]
        for ob in range(8):
            bld.wait("gp", "te", vv_guard)
            bld.dma("gp", bvv_s[:, :], bqkvv_h[ob:ob + 1, :], "bvv")
            for half in range(2):
                slab_dma("gp", wsl[half], 16, 512,
                         wqkvv_h[ob, half * 16:(half + 1) * 16], f"wsl{half}")
            if ob == 0:
                bld.comp("gp", lambda e: e.collective_compute(
                    "AllGather", ALU.bypass, ins=[kvbK_in[:]],
                    outs=[kvbK_all[:, :]],
                    replica_groups=[[0, 1, 2, 3], [4, 5, 6, 7]]))
            bld.wait("te", "bvv", bld.skey("bvv").n)
            bld.wait("te", "wsl0", bld.skey("wsl0").n)
            bld.wait("te", "wsl1", bld.skey("wsl1").n)
            for tc in range(8):
                bank = (pb4, pb5)[tc % 2]
                fs, fv = vfree[tc % 2]
                bld.wait("te", fs, fv)
                for kc in range(KC):
                    bld.compq("te", lambda e, tc=tc, kc=kc, bank=bank: e.matmul(
                        bank[:, :],
                        xt[:, kc * TPC + tc * 128: kc * TPC + tc * 128 + 128],
                        wsl[kc // 16][:, (kc % 16) * 512:(kc % 16) * 512 + 512],
                        start=(kc == 0), stop=False))
                mm_t = bld.comp("te", lambda e, bank=bank: e.matmul(
                    bank[:, :], ones_row[:, :], bvv_s[:, :], start=False, stop=True))
                bld.wait("ve", "te", mm_t)
                bld.wait("ve", f"vst{tc % 2}", st_g[tc % 2])
                vt = bld.comp("ve", lambda e, tc=tc, bank=bank: e.tensor_copy(
                    stg[tc % 2][:, :], bank[:, :]))
                vfree[tc % 2] = ("ve", vt)
                bld.wait("act", "ve", vt)
                st_g[tc % 2] = bld.dma(
                    "act", kvbVo[ob, tc * 128:(tc + 1) * 128, :],
                    stg[tc % 2][:, :], f"vst{tc % 2}")
            vv_guard = bld.sem["te"].n
            bld.wait("gp", "vst0", st_g[0])
            bld.wait("gp", "vst1", st_g[1])
            bld.comp("gp", lambda e, ob=ob: e.collective_compute(
                "AllGather", ALU.bypass,
                ins=[kvbV_in[ob * TPC * 512:(ob + 1) * TPC * 512]],
                outs=[kvbV_all[ob * 4:(ob + 1) * 4, :]],
                replica_groups=[[0, 1, 2, 3], [4, 5, 6, 7]]))
        bld.join()

        bld.dma("gp", maskS.rearrange("k (g c) -> k g c", g=8),
                maskb_h.rearrange("g k c -> k g c"), "cstg")
        bld.join()
        bld.dma("gp", wsl[1].rearrange("k (g c) -> k g c", g=8),
                maskbT_h.rearrange("g k c -> k g c"), "mskT")
        bld.join()

        kvbKg = kvbK_all.rearrange("r (d t) -> r d t", d=D)

        kpairs = [wsl[0][:, 0:KC * 128], ctxa[:, 0:KC * 128]]
        vpairs = [wsl[0][:, KC * 128: 2 * KC * 128], hhat[:, 0:KC * 128]]

        def maskT_ap(p, qs):
            return wsl[1][:, (p * 2 + qs) * 1024:(p * 2 + qs + 1) * 1024]

        kvp_guard = [0, 0]
        kvl_t = [0, 0]
        ol_free = [("ve", 0), ("ve", 0)]
        l_guard = 0
        q_guard = 0
        p1v = [0, 0]
        pbt_g = 0
        negb = rowbf[0:1, 512:1024]

        def load_pair(h, p, ps):
            ng = 2 * p + 2
            bld.wait("sy", "te", kvp_guard[ps])
            for g in range(ng):
                bld.dma("sy",
                        kpairs[ps][:, g * 512:(g + 1) * 512].rearrange(
                            "p (r t) -> p r t", r=4),
                        kvbKg[:, h * 128:(h + 1) * 128,
                              g * 128:(g + 1) * 128].rearrange("r d t -> d r t"),
                        f"kvp{ps}")
                bld.dma("sy",
                        vpairs[ps][:, g * 512:(g + 1) * 512].rearrange(
                            "p (r c) -> p r c", r=4),
                        kvbV_all[(h // 4) * 4:(h // 4) * 4 + 4, :].rearrange(
                            "r (t d) -> r t d", t=TPC)[
                            :, g * 128:(g + 1) * 128,
                            (h % 4) * 128:(h % 4) * 128 + 128].rearrange(
                            "r t c -> t r c"),
                        f"kvp{ps}")
            kvl_t[ps] = bld.skey(f"kvp{ps}").n

        load_pair(0, 0, 0)
        for h in range(32):
            for p in range(4):
                ng = 2 * p + 2
                ps = (h * 4 + p) % 2
                kpair = kpairs[ps]
                vpair = vpairs[ps]
                bld.wait("sy", "te", q_guard)
                ql = bld.dma("sy", qtile[:, :],
                             qbuf[h * 128:(h + 1) * 128, p * 256:(p + 1) * 256], "qt")
                np_, nh = (p + 1, h) if p < 3 else (0, h + 1)
                if nh < 32:
                    load_pair(nh, np_, 1 - ps)
                kvl = kvl_t[ps]
                bld.wait("te", "qt", ql)
                bld.wait("te", f"kvp{ps}", kvl)
                bld.wait("te", "act", pbt_g)
                bld.wait("te", "ve", l_guard)
                nv = 0
                for qs in range(2):
                    for kt in range(ng):
                        bank = (ps2a, ps2b)[kt % 2]
                        bld.wait("te", "ve", p1v[kt % 2])
                        s1 = bld.comp("te", lambda e, qs=qs, kt=kt, bank=bank,
                                      kpair=kpair: e.matmul(
                            bank[:, 0:512], qtile[:, qs * 128:(qs + 1) * 128],
                            kpair[:, kt * 512:(kt + 1) * 512], start=True, stop=True))
                        bld.wait("ve", "te", s1)
                        if kt >= ng - 2:
                            j2 = kt - (ng - 2)
                            bld.compq("ve", lambda e, p=p, qs=qs, j2=j2, bank=bank:
                                      e.tensor_tensor(
                                tmpf[2][:, :], bank[:, 0:512],
                                maskT_ap(p, qs)[:, j2 * 512:(j2 + 1) * 512], ALU.add))
                            p1v[kt % 2] = bld.comp("ve", lambda e, kt=kt: e.reduce_max(
                                g8[:, kt:kt + 1], tmpf[2][:, :], axis=AX.X))
                        else:
                            p1v[kt % 2] = bld.comp("ve", lambda e, kt=kt, bank=bank:
                                                   e.reduce_max(
                                g8[:, kt:kt + 1], bank[:, 0:512], axis=AX.X))
                    bld.compq("ve", lambda e, ng=ng: e.reduce_max(
                        g8[:, 40:41], g8[:, 0:ng], axis=AX.X))
                    nv = bld.comp("ve", lambda e, qs=qs: e.tensor_scalar(
                        wtm[:, qs:qs + 1], g8[:, 40:41], -1.0, None, ALU.mult))
                bld.wait("te", "ve", nv)
                tr_t = 0
                for qs in range(2):
                    tr_t = bld.comp("te", lambda e, qs=qs: e.transpose(
                        pbt[0:1, qs * 128:(qs + 1) * 128], wtm[:, qs:qs + 1],
                        ident[:, :]))
                bld.wait("act", "te", tr_t)
                bld.compq("act", lambda e: e.copy(negb[0:1, 0:256], pbt[0:1, 0:256]))
                ng_t = bld.comp("act", lambda e: e.copy(
                    negb[0:1, 256:512], pbt[0:1, 0:256]))
                pbt_g = ng_t
                olb = (pb4, pb6)[p % 2]
                fs, fv = ol_free[p % 2]
                bld.wait("te", fs, fv)
                bld.wait("te", "ve", l_guard)
                bld.wait("te", "act", ng_t)
                first = True
                p_t = [0, 0]
                av_t = 0
                for g in range(ng):
                    slot = g % 2
                    sbank = (ps2a, ps2b)[slot]
                    bld.wait("te", "act", p_t[slot])
                    s_t = 0
                    for ki in range(4):
                        bld.compq("te", lambda e, g=g, ki=ki, sbank=sbank, kpair=kpair:
                                  e.matmul(
                            sbank[:, ki * 256:(ki + 1) * 256],
                            kpair[:, (g * 4 + ki) * 128:(g * 4 + ki + 1) * 128],
                            qtile[:, :], start=True, stop=False))
                        fn = lambda e, ki=ki, sbank=sbank: e.matmul(
                            sbank[:, ki * 256:(ki + 1) * 256],
                            ones_row[0:1, 0:128], negb[0:1, 0:256],
                            start=False, stop=True)
                        if ki == 3:
                            s_t = bld.comp("te", fn)
                        else:
                            bld.compq("te", fn)
                    if g >= 2 * p:
                        j = g - 2 * p
                        bld.wait("ve", "te", s_t)
                        mk_t = bld.comp("ve", lambda e, p=p, j=j, sbank=sbank:
                                        e.tensor_tensor(
                            sbank[:, :], sbank[:, :],
                            maskS[:, (p * 2 + j) * 1024:(p * 2 + j + 1) * 1024],
                            ALU.add))
                        bld.wait("act", "ve", mk_t)
                    else:
                        bld.wait("act", "te", s_t)
                    p_t[slot] = bld.comp("act", lambda e, slot=slot, sbank=sbank:
                                         e.activation(
                        out=pbuf[slot][:, :], in_=sbank[:, :], func=AF.Exp,
                        scale=ATTB_SCALE))
                    bld.wait("te", "act", p_t[slot])
                    for ki in range(4):
                        bld.compq("te", lambda e, g=g, ki=ki, slot=slot, olb=olb,
                                  vpair=vpair, first=(first and ki == 0),
                                  last=(g == ng - 1 and ki == 3): e.matmul(
                            olb[:, 0:256],
                            vpair[:, (g * 4 + ki) * 128:(g * 4 + ki + 1) * 128],
                            pbuf[slot][:, ki * 256:(ki + 1) * 256],
                            start=first, stop=last))
                        if ki == 0:
                            first = False
                    for g2 in range(2):
                        last = (g == ng - 1 and g2 == 1)
                        lfirst = (g == 0 and g2 == 0)
                        fn2 = lambda e, g2=g2, slot=slot, last=last, lfirst=lfirst: \
                            e.matmul(
                            pb5[0:1, 0:512], ones_col[:, :],
                            pbuf[slot][:, g2 * 512:(g2 + 1) * 512],
                            start=lfirst, stop=last)
                        if g2 == 1:
                            av_t = bld.comp("te", fn2)
                        else:
                            bld.compq("te", fn2)
                bld.wait("ve", "te", av_t)
                rl = rowbuf[0:1, 1536:1792]
                bld.compq("ve", lambda e: e.tensor_copy(
                    rowbuf[0:1, 1792:2048], pb5[0:1, 256:512]))
                bld.compq("ve", lambda e: e.tensor_tensor(
                    rowbuf[0:1, 1792:2048], pb5[0:1, 0:256], rowbuf[0:1, 1792:2048],
                    ALU.add))
                bld.compq("ve", lambda e, rl=rl: e.reciprocal(
                    rl, rowbuf[0:1, 1792:2048]))
                vt = bld.comp("ve", lambda e, rl=rl: e.tensor_copy(
                    rowbf[0:1, 0:256], rl))
                bld.wait("te", "ve", vt)
                bc_t = bld.comp("te", lambda e: e.matmul(
                    ps2a[:, 0:256], ones_row[:, :], rowbf[0:1, 0:256],
                    start=True, stop=True))
                bld.wait("ve", "te", bc_t)
                bld.compq("ve", lambda e: e.tensor_copy(
                    tmpf[2][:, 0:256], ps2a[:, 0:256]))
                ct = bld.comp("ve", lambda e, h=h, p=p, olb=olb: e.tensor_tensor(
                    xt[:, h * TPC + p * 256: h * TPC + p * 256 + 256],
                    olb[:, 0:256], tmpf[2][:, 0:256], ALU.mult))
                ol_free[p % 2] = ("ve", ct)
                l_guard = ct
                q_guard = bld.sem["te"].n
                kvp_guard[ps] = q_guard
        bld.join()

        emit_proj_fm(KC, KC, ow_h, xt_rhs, residual_epi(src, dst, obias_s))

    subs = [sublayer_attn_a, sublayer_ffn, sublayer_attn_b, sublayer_ffn]
    src = x_fm
    for i in range(N_SUB):
        dst = y_out if i == N_SUB - 1 else Xd
        subs[i](src, dst)
        src = Xd

    bld.join()
    for k in ("rst0", "rst1"):
        bld.wait("act", k, bld.skey(k).n)

    with nc.Block() as block:
        block.tensor(lambda e: [f(e) for f in bld.ops["te"]])
        block.vector(lambda e: [f(e) for f in bld.ops["ve"]])
        block.scalar(lambda e: [f(e) for f in bld.ops["act"]])
        block.sync(lambda e: [f(e) for f in bld.ops["sy"]])
        block.gpsimd(lambda e: [f(e) for f in bld.ops["gp"]])
    return nc


def kernel(**inputs):
    in_maps = _prep_inputs(inputs)
    nc = build_program()
    res = run_bass_kernel_spmd(nc, in_maps, core_ids=list(range(NC)))
    return _reassemble(res.results)
